# version 16
# baseline (speedup 1.0000x reference)
from contextlib import ExitStack
import numpy as np
import ml_dtypes

import concourse.bass as bass
import concourse.mybir as mybir
from concourse.bass_utils import run_bass_kernel_spmd
from concourse.alu_op_type import AluOpType as OP
from bass_rust import ReduceOp
from concourse import library_config

F32 = mybir.dt.float32
BF16 = mybir.dt.bfloat16
AF = mybir.ActivationFunctionType
AX = mybir.AxisListType
bf16 = ml_dtypes.bfloat16

B, L, D, N, RT = 4, 4096, 2048, 16, 128
TOK = L // 2
DH = D // 2
NDT = DH // 128
HL = L // 2
MAGIC = float(3 * 2 ** 22)



def build_scan_prog():
    nc = bass.Bass("TRN2", target_bir_lowering=False, debug=False)
    xln = nc.dram_tensor("xln", [DH, L], BF16, kind="ExternalInput").ap()
    dtin = nc.dram_tensor("dt", [RT, L], BF16, kind="ExternalInput").ap()
    Bm = nc.dram_tensor("Bm", [N, L], F32, kind="ExternalInput").ap()
    Cm = nc.dram_tensor("Cm", [N, L], F32, kind="ExternalInput").ap()
    dtwT = nc.dram_tensor("dtwT", [RT, DH], F32, kind="ExternalInput").ap()
    dtbias = nc.dram_tensor("dtbias", [DH, 1], F32, kind="ExternalInput").ap()
    Alog = nc.dram_tensor("Alog", [DH, N], F32, kind="ExternalInput").ap()
    selin = nc.dram_tensor("sel", [N, N * 128], BF16, kind="ExternalInput").ap()
    yout = nc.dram_tensor("yout", [DH, L], F32, kind="ExternalOutput").ap()
    brep_d = nc.dram_tensor("brep_d", [N * 128, L], BF16, kind="Internal").ap()
    crep_d = nc.dram_tensor("crep_d", [N * 128, L], BF16, kind="Internal").ap()

    with ExitStack() as es:
        sbt = lambda name, shape, dt: es.enter_context(nc.sbuf_tensor(name, shape, dt))
        t_dtwb = sbt("t_dtwb", [RT, DH], BF16)
        t_dtb = sbt("t_dtb", [128, NDT], F32)
        t_alog = sbt("t_alog", [128, NDT * N], F32)
        t_a = sbt("t_a", [128, NDT * N], F32)
        t_an = sbt("t_an", [128, NDT * N], F32)
        t_bc = sbt("t_bc", [N, L], F32)
        t_bcb = sbt("t_bcb", [N, L], BF16)
        t_sel = sbt("t_sel", [N, N * 128], BF16)
        t_dt = sbt("t_dt", [RT, L], BF16)
        t_xln = sbt("t_xln", [128, 2 * L], BF16)
        t_delta = sbt("t_delta", [128, L], BF16)
        t_dx = sbt("t_dx", [128, L], BF16)
        t_sp1 = sbt("t_sp1", [128, HL], F32)
        t_sp2 = sbt("t_sp2", [128, HL], F32)
        t_sp3 = sbt("t_sp3", [128, HL], F32)
        t_sp4 = sbt("t_sp4", [128, HL], F32)
        t_da = sbt("t_da", [128, L], F32)
        t_u = sbt("t_u", [128, L], BF16)
        t_h = sbt("t_h", [128, L], BF16)
        t_yp = sbt("t_yp", [128, L], BF16)
        t_mg = sbt("t_mg", [128, L], BF16)
        t_p1 = sbt("t_p1", [128, L], BF16)
        t_p2 = sbt("t_p2", [128, L], BF16)
        t_p3 = sbt("t_p3", [128, L], BF16)
        t_p4 = sbt("t_p4", [128, L], BF16)
        t_brep = sbt("t_brep", [128, 2 * L], BF16)
        t_crep = sbt("t_crep", [128, L], BF16)
        ps_a = es.enter_context(nc.psum_tensor("ps_a", [128, 2048], F32))
        ps_b = es.enter_context(nc.psum_tensor("ps_b", [128, 2048], F32))

        s_in = es.enter_context(nc.semaphore("s_in"))
        s_bc = es.enter_context(nc.semaphore("s_bc"))
        s_xl0 = es.enter_context(nc.semaphore("s_xl0"))
        s_xl1 = es.enter_context(nc.semaphore("s_xl1"))
        s_br0 = es.enter_context(nc.semaphore("s_br0"))
        s_br1 = es.enter_context(nc.semaphore("s_br1"))
        s_brc = es.enter_context(nc.semaphore("s_brc"))
        s_bs0 = es.enter_context(nc.semaphore("s_bs0"))
        s_bs1 = es.enter_context(nc.semaphore("s_bs1"))
        s_t = es.enter_context(nc.semaphore("s_t"))
        s_s = es.enter_context(nc.semaphore("s_s"))
        s_v = es.enter_context(nc.semaphore("s_v"))
        s_y = es.enter_context(nc.semaphore("s_y"))

        ev_v = {}
        v = 0
        v += 1
        v += 1
        v += 1
        ev_v["castB"] = v
        ev_v["setup"] = v
        for un in range(N * 2):
            v += 1
            ev_v[("evac", un)] = v
            if un == N * 2 - 1:
                v += 1
                ev_v["castC"] = v
        for un in range(N * 2, N * 4):
            v += 1
            ev_v[("evac", un)] = v
        for j in range(NDT):
            for half in range(2):
                v += 1
                ev_v[("delta", j, half)] = v
            v += 1
            ev_v[("dx", j)] = v
            for n in range(N):
                v += 1
                ev_v[("u", j, n)] = v
                v += 1
                ev_v[("tts", j, n)] = v
                v += 1
                ev_v[("ym", j, n)] = v
                nm = bin(n + 1).count("1")
                extra = 0
                if n % 2 == 1:
                    k = n + 1
                    extra = 0
                    kk = k
                    while kk % 2 == 0 and kk > 1:
                        extra += 1
                        kk //= 2
                v += extra
                ev_v[("unit", j, n)] = v
            ev_v[("ysum", j)] = v
        V_TOTAL = v

        ev_s = {}
        s = 0
        s += 2
        s -= 1
        ev_s["setup"] = s
        for j in range(NDT):
            for half in range(2):
                s += 2
                ev_s[("psfree", j, half)] = s
                s += 2
                ev_s[("sp", j, half)] = s
            for n in range(N):
                s += 1
                ev_s[("da", j, n)] = s
        S_TOTAL = s

        ev_t = {}
        t = 0
        for un in range(N * 2 * 2):
            t += 1
            ev_t[("bld", un)] = t
        for j in range(NDT):
            for half in range(2):
                t += 1
                ev_t[("dtproj", j, half)] = t
        T_TOTAL = t

        NPARAM = 1 + NDT * 2 + 1 + 1
        ev_bs = {}
        bs = 0
        for un in range(N * 2 * 2):
            bs += 16
            ev_bs[("bst", un)] = bs
        ev_br = {}
        br = 0
        for j in range(NDT):
            for n in range(N):
                br += 32
                ev_br[("brl", j, n)] = br
        ev_xl = {}
        for j in range(NDT):
            ev_xl[j] = 16 * (j + 1)

        block = es.enter_context(nc.Block())

        @block.sync
        def _(eng):
            eng.dma_start(out=t_sp1[:, 0:DH], in_=dtwT).then_inc(s_in, 16)
            for j in range(NDT):
                eng.dma_start(out=t_dtb[:, j:j + 1],
                              in_=dtbias[j * 128:(j + 1) * 128, :]).then_inc(s_in, 16)
                eng.dma_start(out=t_alog[:, j * N:(j + 1) * N],
                              in_=Alog[j * 128:(j + 1) * 128, :]).then_inc(s_in, 16)
            eng.dma_start(out=t_dt[:], in_=dtin).then_inc(s_in, 16)
            eng.dma_start(out=t_sel[:], in_=selin).then_inc(s_in, 16)
            eng.dma_start(out=t_bc[:], in_=Bm).then_inc(s_bc, 16)
            eng.wait_ge(s_v, ev_v["castB"])
            eng.dma_start(out=t_bc[:], in_=Cm).then_inc(s_bc, 16)
            for j in range(NDT):
                if j >= 2:
                    eng.wait_ge(s_v, ev_v[("dx", j - 2)])
                eng.dma_start(out=t_xln[:, (j % 2) * L:(j % 2 + 1) * L],
                              in_=xln[j * 128:(j + 1) * 128, :]).then_inc(
                    s_xl0 if j % 2 == 0 else s_xl1, 16)

        @block.gpsimd
        def _(eng):
            for which in range(2):
                for n in range(N):
                    for half in range(2):
                        un = which * (N * 2) + n * 2 + half
                        eng.wait_ge(s_v, ev_v[("evac", un)])
                        srcap = t_u[:, half * HL:(half + 1) * HL]
                        dst = (brep_d if which == 0 else crep_d)[
                            n * 128:(n + 1) * 128, half * HL:(half + 1) * HL]
                        eng.dma_start(out=dst, in_=srcap).then_inc(
                            s_bs0 if un % 2 == 0 else s_bs1, 16)
            for j in range(NDT):
                for n in range(N):
                    sl = n % 2
                    idx = j * N + n
                    if j == 0 and n == 0:
                        eng.wait_ge(s_bs0, 16 * (N * 2))
                        eng.wait_ge(s_bs1, 16 * (N * 2))
                    if (j, n) > (0, 1):
                        pj, pn = (j, n - 2) if n >= 2 else (j - 1, N - 2 + n)
                        eng.wait_ge(s_v, ev_v[("u", pj, pn)])
                    eng.dma_start(out=t_brep[:, sl * L:(sl + 1) * L],
                                  in_=brep_d[n * 128:(n + 1) * 128, :]).then_inc(
                        s_br0 if sl == 0 else s_br1, 16)
                    if (j, n) != (0, 0):
                        pj, pn = (j, n - 1) if n >= 1 else (j - 1, N - 1)
                        eng.wait_ge(s_v, ev_v[("ym", pj, pn)])
                    eng.dma_start(out=t_crep[:],
                                  in_=crep_d[n * 128:(n + 1) * 128, :]).then_inc(s_brc, 16)
                eng.wait_ge(s_v, ev_v[("ysum", j)])
                eng.dma_start(out=yout[j * 128:(j + 1) * 128, :],
                              in_=t_da[:]).then_inc(s_y, 16)
            eng.wait_ge(s_y, 16 * NDT)

        @block.tensor
        def _(eng):
            eng.wait_ge(s_in, 16 * NPARAM)
            eng.wait_ge(s_v, ev_v["setup"])
            for which in range(2):
                if which == 1:
                    eng.wait_ge(s_v, ev_v["castC"])
                for n in range(N):
                    for half in range(2):
                        un = which * (N * 2) + n * 2 + half
                        pst = ps_a if un % 2 == 0 else ps_b
                        if un >= 2:
                            eng.wait_ge(s_v, ev_v[("evac", un - 2)])
                        last = None
                        for blk in range(4):
                            last = eng.matmul(
                                out=pst[:, blk * 512:(blk + 1) * 512],
                                lhsT=t_sel[:, n * 128:(n + 1) * 128],
                                rhs=t_bcb[:, half * HL + blk * 512:
                                          half * HL + (blk + 1) * 512],
                                start=True, stop=True)
                        last.then_inc(s_t, 1)
            for j in range(NDT):
                for half in range(2):
                    un = j * 2 + half
                    pst = ps_a if un % 2 == 0 else ps_b
                    if un < 2:
                        eng.wait_ge(s_v, ev_v[("evac", N * 4 - 2 + un)])
                    else:
                        eng.wait_ge(s_s, ev_s[("psfree", (un - 2) // 2, (un - 2) % 2)])
                    last = None
                    for blk in range(4):
                        last = eng.matmul(
                            out=pst[:, blk * 512:(blk + 1) * 512],
                            lhsT=t_dtwb[:, j * 128:(j + 1) * 128],
                            rhs=t_dt[:, half * HL + blk * 512:half * HL + (blk + 1) * 512],
                            start=True, stop=True)
                    last.then_inc(s_t, 1)

        @block.scalar
        def _(eng):
            eng.wait_ge(s_in, 16 * NPARAM)
            eng.activation(out=t_a[:], in_=t_alog[:], func=AF.Exp).then_inc(s_s, 1)
            for j in range(NDT):
                for half in range(2):
                    un = j * 2 + half
                    pst = ps_a if un % 2 == 0 else ps_b
                    eng.wait_ge(s_t, ev_t[("dtproj", j, half)])
                    if un >= 1:
                        eng.wait_ge(s_v, ev_v[("delta", (un - 1) // 2, (un - 1) % 2)])
                    bias_ap = t_dtb[:, j:j + 1]
                    eng.activation(out=t_sp1[:], in_=pst[:], func=AF.Abs,
                                   bias=bias_ap).then_inc(s_s, 1)
                    eng.activation(out=t_sp2[:], in_=pst[:], func=AF.Relu,
                                   bias=bias_ap).then_inc(s_s, 1)
                    eng.drain()
                    eng.activation(out=t_sp3[:], in_=t_sp1[:], func=AF.Exp,
                                   scale=-1.0).then_inc(s_s, 1)
                    eng.drain()
                    eng.activation(out=t_sp4[:], in_=t_sp3[:], func=AF.Ln,
                                   bias=1.0).then_inc(s_s, 1)
                for n in range(N):
                    if (j, n) != (0, 0):
                        pj, pn = (j, n - 1) if n >= 1 else (j - 1, N - 1)
                        eng.wait_ge(s_v, ev_v[("tts", pj, pn)])
                    if n == 0 and j > 0:
                        eng.wait_ge(s_v, ev_v[("ysum", j - 1)])
                        eng.wait_ge(s_y, 16 * j)
                    eng.wait_ge(s_v, ev_v[("dx", j)])
                    eng.activation(out=t_da[:], in_=t_delta[:], func=AF.Exp,
                                   scale=t_an[:, j * N + n:j * N + n + 1]).then_inc(s_s, 1)

        @block.vector
        def _(eng):
            eng.wait_ge(s_in, 16 * NPARAM)
            eng.tensor_copy(out=t_dtwb[:], in_=t_sp1[:, 0:DH]).then_inc(s_v, 1)
            eng.wait_ge(s_s, 1)
            eng.tensor_scalar(out=t_an[:], in0=t_a[:], scalar1=-1.0, scalar2=None,
                              op0=OP.mult).then_inc(s_v, 1)
            eng.wait_ge(s_bc, 16)
            eng.tensor_copy(out=t_bcb[:], in_=t_bc[:]).then_inc(s_v, 1)
            for which in range(2):
                for n in range(N):
                    for half in range(2):
                        un = which * (N * 2) + n * 2 + half
                        pst = ps_a if un % 2 == 0 else ps_b
                        eng.wait_ge(s_t, ev_t[("bld", un)])
                        if un >= 2:
                            eng.wait_ge(s_bs0 if un % 2 == 0 else s_bs1,
                                        16 * (un // 2))
                        eng.tensor_copy(out=t_u[:, half * HL:(half + 1) * HL],
                                        in_=pst[:]).then_inc(s_v, 1)
                        if un == N * 2 - 1:
                            eng.wait_ge(s_bc, 32)
                            eng.tensor_copy(out=t_bcb[:], in_=t_bc[:]).then_inc(s_v, 1)
            for j in range(NDT):
                for half in range(2):
                    eng.wait_ge(s_s, ev_s[("sp", j, half)])
                    if j > 0 and half == 0:
                        eng.wait_ge(s_s, ev_s[("da", j - 1, N - 1)])
                    eng.tensor_tensor(out=t_delta[:, half * HL:(half + 1) * HL],
                                      in0=t_sp2[:], in1=t_sp4[:], op=OP.add).then_inc(s_v, 1)
                eng.wait_ge(s_xl0 if j % 2 == 0 else s_xl1, 16 * (j // 2 + 1))
                eng.drain()
                eng.tensor_tensor(out=t_dx[:], in0=t_delta[:],
                                  in1=t_xln[:, (j % 2) * L:(j % 2 + 1) * L],
                                  op=OP.mult).then_inc(s_v, 1)
                eng.drain()
                for n in range(N):
                    sl = n % 2
                    idx = j * N + n
                    if j == 0 and n == 0:
                        eng.wait_ge(s_bs0, 16 * (N * 2))
                        eng.wait_ge(s_bs1, 16 * (N * 2))
                    eng.wait_ge(s_br0 if sl == 0 else s_br1, 16 * (idx // 2 + 1))
                    eng.drain()
                    eng.tensor_tensor(out=t_u[:], in0=t_dx[:],
                                      in1=t_brep[:, sl * L:(sl + 1) * L],
                                      op=OP.mult).then_inc(s_v, 1)
                    eng.wait_ge(s_s, ev_s[("da", j, n)])
                    eng.drain()
                    eng.tensor_tensor_scan(out=t_h[:], data0=t_da[:], data1=t_u[:],
                                           initial=0.0, op0=OP.mult,
                                           op1=OP.add).then_inc(s_v, 1)
                    ym_dst = t_p1 if n % 2 == 0 else t_yp
                    eng.wait_ge(s_brc, 16 * (idx + 1))
                    eng.drain()
                    eng.tensor_tensor(out=ym_dst[:], in0=t_h[:],
                                      in1=t_crep[:],
                                      op=OP.mult).then_inc(s_v, 1)
                    if n % 2 == 1:
                        k = n + 1
                        chain = 0
                        kk = k
                        while kk % 2 == 0 and kk > 1:
                            chain += 1
                            kk //= 2
                        bufs = [t_mg, t_u]
                        cur = None
                        lvslots = [t_p2, t_p3, t_p4]
                        for c in range(chain):
                            src0 = t_p1 if c == 0 else cur
                            src1 = t_yp if c == 0 else lvslots[c - 1]
                            if c == chain - 1:
                                if k == N:
                                    if j > 0:
                                        eng.wait_ge(s_y, 16 * j)
                                    dst = t_da
                                else:
                                    dst = lvslots[c]
                            else:
                                dst = bufs[c % 2]
                            eng.drain()
                            eng.tensor_tensor(out=dst[:], in0=src0[:], in1=src1[:],
                                              op=OP.add).then_inc(s_v, 1)
                            cur = dst

    return nc


def _scan_in_maps(inputs):
    raise NotImplementedError


def kernel(**inputs):
    raise NotImplementedError



class _DryInst:
    def __init__(self, cnt):
        self.cnt = cnt

    def then_inc(self, sem, k):
        self.cnt[id(sem)] = self.cnt.get(id(sem), 0) + k
        return self


class _DryEng:
    def __init__(self, cnt):
        self._cnt = cnt

    def __getattr__(self, name):
        if name in ("wait_ge", "drain"):
            return lambda *a, **k: None
        return lambda *a, **k: _DryInst(self._cnt)


class EvTab:

    def __init__(self):
        self.cnt = {}
        self.ev = {}
        self.dry = True

    def rec(self, sem, name):
        val = self.cnt.get(id(sem), 0)
        if self.dry:
            self.ev[name] = val
        return self.ev[name]

    def get(self, name):
        return self.ev[name]


def build_token_prog(phase):
    NT = TOK // 128
    NK = D // 128
    EPS_LN = 1e-5
    EPS_RMS = 1e-6
    EPS_Q = 1e-5
    nc = bass.Bass("TRN2", target_bir_lowering=False, debug=False)
    Win = nc.dram_tensor("Win", [D, D], F32, kind="ExternalInput").ap()
    gam = nc.dram_tensor("gam", [1, D], F32, kind="ExternalInput").ap()
    ident = nc.dram_tensor("ident", [128, 128], BF16, kind="ExternalInput").ap()
    xin = nc.dram_tensor("xin", [TOK, D], F32, kind="ExternalInput").ap()
    if phase == 1:
        r1o = nc.dram_tensor("r1o", [TOK, D], F32, kind="ExternalOutput").ap()
        xlnT = nc.dram_tensor("xlnT", [D, TOK], BF16, kind="ExternalOutput").ap()
        dtv = nc.dram_tensor("dtv", [RT, TOK], BF16, kind="ExternalOutput").ap()
        bcv = nc.dram_tensor("bcv", [2 * N, TOK], F32, kind="ExternalOutput").ap()
        xpwTin = nc.dram_tensor("xpwT", [D, RT + 2 * N], F32, kind="ExternalInput").ap()
    else:
        identf = nc.dram_tensor("identf", [128, 128], F32, kind="ExternalInput").ap()
        yTin = nc.dram_tensor("yTin", [D, TOK], F32, kind="ExternalInput").ap()
        dpar = nc.dram_tensor("dpar", [1, D], F32, kind="ExternalInput").ap()
        outo = nc.dram_tensor("outo", [TOK, D], F32, kind="ExternalOutput").ap()

    with ExitStack() as es:
        sbt = lambda name, shape, dt: es.enter_context(nc.sbuf_tensor(name, shape, dt))
        t_w = sbt("t_w", [128, 2 * D], F32)
        t_wq = sbt("t_wq", [128, NK * D], BF16)
        t_x = sbt("t_x", [128, 2 * D], F32)
        t_xln = sbt("t_xln", [128, D], F32)
        t_scr = sbt("t_scr", [128, D], F32)
        t_q = sbt("t_q", [128, D], F32)
        t_xq = sbt("t_xq", [128, D], BF16)
        t_xqT = sbt("t_xqT", [128, NK * 128], BF16)
        t_gam = sbt("t_gam", [128, D], F32)
        t_r1 = sbt("t_r1", [128, D], F32)
        t_st = sbt("t_st", [128, 24], F32)
        t_cst = sbt("t_cst", [128, 4], F32)
        t_ones = sbt("t_ones", [128, 128], F32)
        t_id = sbt("t_id", [128, 128], BF16)
        if phase == 1:
            t_xlb = sbt("t_xlb", [128, D], BF16)
            t_xt = sbt("t_xt", [128, NK * 128], BF16)
            t_xpw = sbt("t_xpw", [128, NK * (RT + 2 * N)], BF16)
            t_xpwf = sbt("t_xpwf", [128, RT + 2 * N], F32)
            t_pd = sbt("t_pd", [128, 128], BF16)
            t_pb = sbt("t_pb", [32, 128], F32)
        else:
            t_idf = sbt("t_idf", [128, 128], F32)
            t_y = sbt("t_y", [128, 2 * D], F32)
            t_yt = sbt("t_yt", [128, D], F32)
            t_r2 = sbt("t_r2", [128, D], F32)
            t_dr = sbt("t_dr", [128, D], F32)
        ps_m = es.enter_context(nc.psum_tensor("ps_m", [128, D], F32))
        ps_t1 = es.enter_context(nc.psum_tensor("ps_t1", [128, 128], BF16))
        ps_t2 = es.enter_context(nc.psum_tensor("ps_t2", [128, 128], BF16))
        if phase == 1:
            ps_p1 = es.enter_context(nc.psum_tensor("ps_p1", [128, 128], F32))
            ps_p2 = es.enter_context(nc.psum_tensor("ps_p2", [32, 128], F32))
        else:
            ps_f1 = es.enter_context(nc.psum_tensor("ps_f1", [128, 128], F32))
            ps_f2 = es.enter_context(nc.psum_tensor("ps_f2", [128, 128], F32))

        sems = {}
        for nm in ["s_wa0", "s_wa1", "s_wb0", "s_wb1", "s_x0", "s_x1", "s_g",
                   "s_xp", "s_t", "s_s", "s_v", "s_g2", "s_or", "s_odt", "s_ob",
                   "s_y0", "s_y1"]:
            sems[nm] = es.enter_context(nc.semaphore(nm))
        (s_wa0, s_wa1, s_wb0, s_wb1, s_x0, s_x1, s_g, s_xp, s_t, s_s, s_v,
         s_g2, s_or, s_odt, s_ob, s_y0, s_y1) = (
            sems[k] for k in ["s_wa0", "s_wa1", "s_wb0", "s_wb1", "s_x0", "s_x1",
                              "s_g", "s_xp", "s_t", "s_s", "s_v", "s_g2", "s_or",
                              "s_odt", "s_ob", "s_y0", "s_y1"])

        tab = EvTab()

        def ix(sem):
            return tab.cnt.get(id(sem), 0)

        def em_sync(eng, E):
            eng.dma_start(out=t_gam[0:1, :], in_=gam).then_inc(s_g, 16)
            eng.dma_start(out=t_id[:], in_=ident).then_inc(s_g, 16)
            if phase == 3:
                eng.dma_start(out=t_dr[0:1, :], in_=dpar).then_inc(s_g, 16)
                eng.dma_start(out=t_idf[:], in_=identf).then_inc(s_g, 16)
            if phase == 1:
                for k in range(NK):
                    if k >= 1:
                        eng.wait_ge(s_v, E(("xpc", k - 1)))
                    eng.dma_start(out=t_xpwf[:],
                                  in_=xpwTin[k * 128:(k + 1) * 128, :]).then_inc(s_xp, 16)
            for k in range(NK):
                if k >= 2:
                    eng.wait_ge(s_v, E(("wstat", k - 2)))
                eng.dma_start(out=t_w[:, (k % 2) * D:(k % 2 + 1) * D],
                              in_=Win[k * 128:(k + 1) * 128, :]).then_inc(
                    s_wa0 if k % 2 == 0 else s_wa1, 16)
            for k in range(NK):
                eng.wait_ge(s_v, E(("wstat", NK - 2 + k)) if k < 2 else E(("wq", k - 2)))
                eng.dma_start(out=t_w[:, (k % 2) * D:(k % 2 + 1) * D],
                              in_=Win[k * 128:(k + 1) * 128, :]).then_inc(
                    s_wb0 if k % 2 == 0 else s_wb1, 16)
            for i in range(NT):
                if i >= 2:
                    eng.wait_ge(s_v, E(("ep", i - 2)))
                eng.dma_start(out=t_x[:, (i % 2) * D:(i % 2 + 1) * D],
                              in_=xin[i * 128:(i + 1) * 128, :]).then_inc(
                    s_x0 if i % 2 == 0 else s_x1, 16)

        def em_gp_full(eng, E):
            if phase == 3:
                for i in range(NT + 1):
                    if i < NT:
                        if i >= 2:
                            eng.wait_ge(s_v, E(("r2", i - 2)))
                        for b in range(NK):
                            eng.dma_start(
                                out=t_y[:, (i % 2) * D + b * 128:(i % 2) * D + (b + 1) * 128],
                                in_=yTin[b * 128:(b + 1) * 128, i * 128:(i + 1) * 128]
                            ).then_inc(s_y0 if i % 2 == 0 else s_y1, 16)
                    if i >= 1:
                        eng.wait_ge(s_v, E(("ep", i - 1)))
                        eng.dma_start(out=outo[(i - 1) * 128:i * 128, :],
                                      in_=t_r1[:]).then_inc(s_or, 16)
            else:
                for i in range(NT):
                    eng.wait_ge(s_v, E(("ep", i)))
                    eng.dma_start(out=r1o[i * 128:(i + 1) * 128, :],
                                  in_=t_r1[:]).then_inc(s_or, 16)
                    for b in range(NK):
                        eng.wait_ge(s_v, E(("xtev", i, b)))
                        eng.dma_start(
                            out=xlnT[b * 128:(b + 1) * 128, i * 128:(i + 1) * 128],
                            in_=t_xt[:, b * 128:(b + 1) * 128]).then_inc(s_ob, 16)
                    eng.wait_ge(s_v, E(("xp", i)))
                    eng.dma_start(out=dtv[:, i * 128:(i + 1) * 128],
                                  in_=t_pd[:]).then_inc(s_odt, 16)
                    eng.dma_start(out=bcv[:, i * 128:(i + 1) * 128],
                                  in_=t_pb[:]).then_inc(s_odt, 16)
            eng.wait_ge(s_or, 16 * NT)
            if phase == 1:
                eng.wait_ge(s_ob, 16 * NT * NK)
                eng.wait_ge(s_odt, 32 * NT)

        def em_tensor(eng, E):
            ps_rep = ps_p1 if phase == 1 else ps_f1
            eng.wait_ge(s_g, 16 * (4 if phase == 3 else 2))
            eng.wait_ge(s_v, E("ones"))
            nrep = NK * (2 if phase == 3 else 1)
            for blk in range(nrep):
                src_row = t_gam if blk < NK else t_dr
                cb = (blk % NK) * 128
                if blk > 0:
                    eng.wait_ge(s_v, E(("rev", blk - 1)))
                eng.matmul(out=ps_rep[:], lhsT=t_ones[0:1, :],
                           rhs=src_row[0:1, cb:cb + 128],
                           start=True, stop=True).then_inc(s_t, 1)
                tab.rec(s_t, ("rep", blk))
            eng.wait_ge(s_v, E(("wstat", NK - 1)))
            eng.wait_ge(s_v, E(("rev", nrep - 1)))
            eng.matmul(out=ps_rep[:, 0:1], lhsT=t_ones[:],
                       rhs=t_st[:, 18:19], start=True, stop=True).then_inc(s_t, 1)
            tab.rec(s_t, ("wsum",))
            for k in range(NK):
                eng.wait_ge(s_v, E(("wq3", k)))
                for b in range(NK):
                    pst = ps_t1 if (k * NK + b) % 2 == 0 else ps_t2
                    if k * NK + b >= 2:
                        eng.wait_ge(s_v, E(("wev", (k * NK + b - 2) // NK,
                                            (k * NK + b - 2) % NK)))
                    eng.transpose(out=pst[:], in_=t_xq[:, b * 128:(b + 1) * 128],
                                  identity=t_id[:]).then_inc(s_t, 1)
                    tab.rec(s_t, ("wtr", k, b))
            for i in range(NT):
                if phase == 3:
                    eng.wait_ge(s_y0 if i % 2 == 0 else s_y1, 16 * NK * (i // 2 + 1))
                    for b in range(NK):
                        un = i * NK + b
                        pst = ps_f1 if un % 2 == 0 else ps_f2
                        eng.wait_ge(s_v, E(("yev_prev", i, b)))
                        eng.transpose(out=pst[:],
                                      in_=t_y[:, (i % 2) * D + b * 128:(i % 2) * D + (b + 1) * 128],
                                      identity=t_idf[:]).then_inc(s_t, 1)
                        tab.rec(s_t, ("ytr", i, b))
                eng.wait_ge(s_v, E(("xq", i)))
                for b in range(NK):
                    un = i * NK + b
                    pst = ps_t1 if un % 2 == 0 else ps_t2
                    eng.wait_ge(s_v, E(("xqev_prev", i, b)))
                    eng.transpose(out=pst[:], in_=t_xq[:, b * 128:(b + 1) * 128],
                                  identity=t_id[:]).then_inc(s_t, 1)
                    tab.rec(s_t, ("xqtr", i, b))
                eng.wait_ge(s_v, E(("xqev", i, NK - 1)))
                if i > 0:
                    eng.wait_ge(s_v, E(("ep", i - 1)))
                for ob in range(4):
                    last = None
                    for k in range(NK):
                        last = eng.matmul(out=ps_m[:, ob * 512:(ob + 1) * 512],
                                          lhsT=t_xqT[:, k * 128:(k + 1) * 128],
                                          rhs=t_wq[:, k * D + ob * 512:k * D + (ob + 1) * 512],
                                          start=(k == 0), stop=(k == NK - 1))
                    last.then_inc(s_t, 1)
                tab.rec(s_t, ("mm", i))
                if phase == 1:
                    eng.wait_ge(s_v, E(("xlb", i)))
                    for b in range(NK):
                        un = i * NK + b
                        pst = ps_t1 if un % 2 == 0 else ps_t2
                        eng.wait_ge(s_v, E(("xtev_prev", i, b)))
                        eng.transpose(out=pst[:], in_=t_xlb[:, b * 128:(b + 1) * 128],
                                      identity=t_id[:]).then_inc(s_t, 1)
                        tab.rec(s_t, ("xltr", i, b))
                    if i > 0:
                        eng.wait_ge(s_v, E(("xp", i - 1)))
                    else:
                        eng.wait_ge(s_v, E(("wsev",)))
                    lastd = lastb = None
                    for k in range(NK):
                        lastd = eng.matmul(out=ps_p1[:],
                                           lhsT=t_xpw[:, k * 160:k * 160 + 128],
                                           rhs=t_xt[:, k * 128:(k + 1) * 128],
                                           start=(k == 0), stop=(k == NK - 1))
                        lastb = eng.matmul(out=ps_p2[:],
                                           lhsT=t_xpw[:, k * 160 + 128:k * 160 + 160],
                                           rhs=t_xt[:, k * 128:(k + 1) * 128],
                                           start=(k == 0), stop=(k == NK - 1))
                    lastd.then_inc(s_t, 1)
                    lastb.then_inc(s_t, 1)
                    tab.rec(s_t, ("xpt", i))

        def em_scalar(eng, E):
            c_eps = t_cst[:, 0:1]
            c_rms = t_cst[:, 1:2]
            c_invd = t_cst[:, 2:3]
            c_half = t_cst[:, 3:4]
            for i in range(NT):
                eng.wait_ge(s_v, E(("var1", i)))
                eng.activation(out=t_st[:, 5:6], in_=t_st[:, 3:4], func=AF.Ln,
                               bias=c_eps).then_inc(s_s, 1)
                eng.drain()
                eng.activation(out=t_st[:, 6:7], in_=t_st[:, 5:6], func=AF.Exp,
                               scale=c_half).then_inc(s_s, 1)
                tab.rec(s_s, ("rstd1", i))
                if phase == 3:
                    eng.wait_ge(s_v, E(("var2", i)))
                    eng.drain()
                    eng.activation(out=t_st[:, 12:13], in_=t_st[:, 11:12], func=AF.Ln,
                                   bias=c_eps).then_inc(s_s, 1)
                    eng.drain()
                    eng.activation(out=t_st[:, 13:14], in_=t_st[:, 12:13], func=AF.Exp,
                                   scale=c_half).then_inc(s_s, 1)
                    tab.rec(s_s, ("rstd2", i))
                eng.wait_ge(s_v, E(("ssq2", i)))
                eng.drain()
                eng.activation(out=t_st[:, 8:9], in_=t_st[:, 7:8], func=AF.Ln,
                               scale=c_invd, bias=c_rms).then_inc(s_s, 1)
                eng.drain()
                eng.activation(out=t_st[:, 9:10], in_=t_st[:, 8:9], func=AF.Exp,
                               scale=c_half).then_inc(s_s, 1)
                tab.rec(s_s, ("rms2", i))
                if phase == 1:
                    eng.wait_ge(s_v, E(("var2", i)))
                    eng.drain()
                    eng.activation(out=t_st[:, 12:13], in_=t_st[:, 11:12], func=AF.Ln,
                                   bias=c_eps).then_inc(s_s, 1)
                    eng.drain()
                    eng.activation(out=t_st[:, 13:14], in_=t_st[:, 12:13], func=AF.Exp,
                                   scale=c_half).then_inc(s_s, 1)
                    tab.rec(s_s, ("rstd2", i))

        def ln_stats(eng, src_ap, sum_c, ssq_c, negm_c, var_c, i, first_wait=None):
            if first_wait is not None:
                eng.wait_ge(*first_wait)
            eng.tensor_reduce(out=t_st[:, sum_c:sum_c + 1], in_=src_ap,
                              axis=AX.X, op=OP.add).then_inc(s_v, 1)
            eng.scalar_tensor_tensor(out=t_scr[:], in0=src_ap, scalar=1.0,
                                     in1=src_ap, op0=OP.mult, op1=OP.mult,
                                     accum_out=t_st[:, ssq_c:ssq_c + 1]).then_inc(s_v, 1)
            eng.drain()
            eng.tensor_scalar(out=t_st[:, negm_c:negm_c + 1],
                              in0=t_st[:, sum_c:sum_c + 1],
                              scalar1=float(-1.0 / D), scalar2=None,
                              op0=OP.mult).then_inc(s_v, 1)
            eng.drain()
            eng.tensor_scalar(out=t_st[:, var_c + 1:var_c + 2],
                              in0=t_st[:, negm_c:negm_c + 1],
                              scalar1=t_st[:, negm_c:negm_c + 1], scalar2=None,
                              op0=OP.mult).then_inc(s_v, 1)
            eng.drain()
            eng.scalar_tensor_tensor(out=t_st[:, var_c:var_c + 1],
                                     in0=t_st[:, ssq_c:ssq_c + 1],
                                     scalar=float(1.0 / D),
                                     in1=t_st[:, var_c + 1:var_c + 2],
                                     op0=OP.mult, op1=OP.subtract).then_inc(s_v, 1)

        def em_vector(eng, E):
            eng.memset(t_cst[:, 0:1], float(EPS_LN))
            eng.memset(t_cst[:, 1:2], float(EPS_RMS))
            eng.memset(t_cst[:, 2:3], float(1.0 / D))
            eng.memset(t_cst[:, 3:4], -0.5)
            eng.memset(t_ones[:], 1.0).then_inc(s_v, 1)
            tab.rec(s_v, "ones")
            eng.wait_ge(s_g, 16 * (4 if phase == 3 else 2))
            ps_rep = ps_p1 if phase == 1 else ps_f1
            nrep = NK * (2 if phase == 3 else 1)
            for blk in range(nrep):
                dst_row = t_gam if blk < NK else t_dr
                cb = (blk % NK) * 128
                eng.wait_ge(s_t, E(("rep", blk)))
                eng.tensor_copy(out=dst_row[:, cb:cb + 128],
                                in_=ps_rep[:]).then_inc(s_v, 1)
                tab.rec(s_v, ("rev", blk))
            if phase == 1:
                for k in range(NK):
                    eng.wait_ge(s_xp, 16 * (k + 1))
                    eng.tensor_copy(out=t_xpw[:, k * 160:(k + 1) * 160],
                                    in_=t_xpwf[:]).then_inc(s_v, 1)
                    tab.rec(s_v, ("xpc", k))
            for k in range(NK):
                eng.wait_ge(s_wa0 if k % 2 == 0 else s_wa1, 16 * (k // 2 + 1))
                eng.tensor_reduce(out=t_st[:, 16 + (k % 2):17 + (k % 2)],
                                  in_=t_w[:, (k % 2) * D:(k % 2 + 1) * D],
                                  axis=AX.X, op=OP.add,
                                  apply_absolute_value=True).then_inc(s_v, 1)
                eng.drain()
                if k == 0:
                    eng.tensor_copy(out=t_st[:, 18:19],
                                    in_=t_st[:, 16:17]).then_inc(s_v, 1)
                else:
                    eng.tensor_tensor(out=t_st[:, 19:20], in0=t_st[:, 18:19],
                                      in1=t_st[:, 16 + (k % 2):17 + (k % 2)],
                                      op=OP.add).then_inc(s_v, 1)
                    eng.drain()
                    eng.tensor_copy(out=t_st[:, 18:19],
                                    in_=t_st[:, 19:20]).then_inc(s_v, 1)
                tab.rec(s_v, ("wstat", k))
            eng.drain()
            eng.wait_ge(s_t, E(("wsum",)))
            eng.tensor_copy(out=t_st[:, 18:19], in_=ps_rep[:, 0:1]).then_inc(s_v, 1)
            tab.rec(s_v, ("wsev",))
            eng.drain()
            eng.tensor_scalar(out=t_st[:, 20:21], in0=t_st[:, 18:19],
                              scalar1=float(1.0 / (D * D)), scalar2=float(EPS_Q),
                              op0=OP.mult, op1=OP.add).then_inc(s_v, 1)
            eng.drain()
            eng.reciprocal(out=t_st[:, 21:22], in_=t_st[:, 20:21]).then_inc(s_v, 1)
            eng.drain()
            tab.rec(s_v, ("ws",))
            for k in range(NK):
                eng.wait_ge(s_wb0 if k % 2 == 0 else s_wb1, 16 * (k // 2 + 1))
                eng.tensor_scalar(out=t_q[:], in0=t_w[:, (k % 2) * D:(k % 2 + 1) * D],
                                  scalar1=t_st[:, 21:22], scalar2=MAGIC,
                                  op0=OP.mult, op1=OP.add).then_inc(s_v, 1)
                eng.drain()
                eng.tensor_scalar(out=t_scr[:], in0=t_q[:],
                                  scalar1=-MAGIC, scalar2=1.0,
                                  op0=OP.add, op1=OP.min).then_inc(s_v, 1)
                eng.drain()
                tab.rec(s_v, ("wq", k))
                if k >= 1:
                    eng.wait_ge(s_t, E(("wtr", k - 1, NK - 1)))
                eng.tensor_scalar(out=t_xq[:], in0=t_scr[:],
                                  scalar1=-1.0, scalar2=None,
                                  op0=OP.max).then_inc(s_v, 1)
                tab.rec(s_v, ("wq3", k))
                for b in range(NK):
                    un = k * NK + b
                    pst = ps_t1 if un % 2 == 0 else ps_t2
                    eng.wait_ge(s_t, E(("wtr", k, b)))
                    eng.tensor_copy(out=t_wq[:, b * D + k * 128:b * D + (k + 1) * 128],
                                    in_=pst[:]).then_inc(s_v, 1)
                    tab.rec(s_v, ("wev", k, b))
            for i in range(NT):
                sx = s_x0 if i % 2 == 0 else s_x1
                xs = t_x[:, (i % 2) * D:(i % 2 + 1) * D]
                ln_stats(eng, xs, 0, 1, 2, 3, i,
                         first_wait=(sx, 16 * (i // 2 + 1)))
                tab.rec(s_v, ("var1", i))
                eng.wait_ge(s_s, E(("rstd1", i)))
                eng.drain()
                eng.tensor_scalar(out=t_xln[:], in0=xs,
                                  scalar1=t_st[:, 2:3], scalar2=t_st[:, 6:7],
                                  op0=OP.add, op1=OP.mult).then_inc(s_v, 1)
                tab.rec(s_v, ("xln1", i))
                if phase == 3:
                    for b in range(NK):
                        un = i * NK + b
                        pst = ps_f1 if un % 2 == 0 else ps_f2
                        tab.rec(s_v, ("yev_prev", i, b))
                        eng.wait_ge(s_t, E(("ytr", i, b)))
                        eng.tensor_copy(out=t_yt[:, b * 128:(b + 1) * 128],
                                        in_=pst[:]).then_inc(s_v, 1)
                        tab.rec(s_v, ("yev", i, b))
                    eng.drain()
                    eng.tensor_tensor(out=t_scr[:], in0=t_xln[:], in1=t_dr[:],
                                      op=OP.mult).then_inc(s_v, 1)
                    eng.drain()
                    eng.tensor_tensor(out=t_r2[:], in0=xs, in1=t_yt[:],
                                      op=OP.add).then_inc(s_v, 1)
                    eng.drain()
                    eng.tensor_tensor(out=t_yt[:], in0=t_r2[:], in1=t_scr[:],
                                      op=OP.add).then_inc(s_v, 1)
                    tab.rec(s_v, ("r2", i))
                    eng.drain()
                    ln_stats(eng, t_yt[:], 0, 10, 14, 11, i)
                    tab.rec(s_v, ("var2", i))
                    eng.wait_ge(s_s, E(("rstd2", i)))
                    eng.drain()
                    eng.tensor_scalar(out=t_xln[:], in0=t_yt[:],
                                      scalar1=t_st[:, 14:15], scalar2=t_st[:, 13:14],
                                      op0=OP.add, op1=OP.mult).then_inc(s_v, 1)
                resid = xs if phase == 1 else t_yt[:]
                eng.drain()
                eng.scalar_tensor_tensor(out=t_scr[:], in0=t_xln[:], scalar=1.0,
                                         in1=t_xln[:], op0=OP.mult, op1=OP.mult,
                                         accum_out=t_st[:, 7:8]).then_inc(s_v, 1)
                tab.rec(s_v, ("ssq2", i))
                eng.wait_ge(s_s, E(("rms2", i)))
                eng.drain()
                eng.scalar_tensor_tensor(out=t_scr[:], in0=t_xln[:],
                                         scalar=t_st[:, 9:10], in1=t_gam[:],
                                         op0=OP.mult, op1=OP.mult).then_inc(s_v, 1)
                eng.drain()
                eng.tensor_reduce(out=t_st[:, 15:16], in_=t_scr[:], axis=AX.X,
                                  op=OP.max, apply_absolute_value=True).then_inc(s_v, 1)
                eng.drain()
                eng.tensor_scalar(out=t_st[:, 22:23], in0=t_st[:, 15:16],
                                  scalar1=float(EPS_Q), scalar2=None,
                                  op0=OP.add).then_inc(s_v, 1)
                eng.drain()
                eng.reciprocal(out=t_st[:, 23:24], in_=t_st[:, 22:23]).then_inc(s_v, 1)
                eng.drain()
                eng.tensor_scalar(out=t_st[:, 10:11], in0=t_st[:, 23:24],
                                  scalar1=127.0, scalar2=None,
                                  op0=OP.mult).then_inc(s_v, 1)
                eng.tensor_scalar(out=t_st[:, 11:12], in0=t_st[:, 22:23],
                                  scalar1=float(0.5 / 127.0), scalar2=t_st[:, 20:21],
                                  op0=OP.mult, op1=OP.mult).then_inc(s_v, 1)
                tab.rec(s_v, ("qs", i))
                eng.drain()
                eng.tensor_scalar(out=t_q[:], in0=t_scr[:],
                                  scalar1=t_st[:, 10:11], scalar2=MAGIC,
                                  op0=OP.mult, op1=OP.add).then_inc(s_v, 1)
                eng.drain()
                if i > 0:
                    eng.wait_ge(s_t, E(("xqtr", i - 1, NK - 1)))
                else:
                    eng.wait_ge(s_t, E(("wtr", NK - 1, NK - 1)))
                eng.tensor_scalar(out=t_xq[:], in0=t_q[:],
                                  scalar1=-MAGIC, scalar2=None,
                                  op0=OP.add).then_inc(s_v, 1)
                tab.rec(s_v, ("xq", i))
                for b in range(NK):
                    un = i * NK + b
                    pst = ps_t1 if un % 2 == 0 else ps_t2
                    tab.rec(s_v, ("xqev_prev", i, b))
                    eng.wait_ge(s_t, E(("xqtr", i, b)))
                    eng.tensor_copy(out=t_xqT[:, b * 128:(b + 1) * 128],
                                    in_=pst[:]).then_inc(s_v, 1)
                    tab.rec(s_v, ("xqev", i, b))
                eng.wait_ge(s_t, E(("mm", i)))
                if i > 0:
                    eng.wait_ge(s_or, 16 * i)
                eng.drain()
                eng.scalar_tensor_tensor(out=t_r1[:], in0=ps_m[:],
                                         scalar=t_st[:, 11:12], in1=resid,
                                         op0=OP.mult, op1=OP.add).then_inc(s_v, 1)
                tab.rec(s_v, ("ep", i))
                if phase == 1:
                    eng.drain()
                    ln_stats(eng, t_r1[:], 0, 10, 14, 11, i)
                    tab.rec(s_v, ("var2", i))
                    eng.wait_ge(s_s, E(("rstd2", i)))
                    eng.drain()
                    eng.tensor_scalar(out=t_xln[:], in0=t_r1[:],
                                      scalar1=t_st[:, 14:15], scalar2=t_st[:, 13:14],
                                      op0=OP.add, op1=OP.mult).then_inc(s_v, 1)
                    eng.drain()
                    eng.tensor_copy(out=t_xlb[:], in_=t_xln[:]).then_inc(s_v, 1)
                    tab.rec(s_v, ("xlb", i))
                    for b in range(NK):
                        un = i * NK + b
                        pst = ps_t1 if un % 2 == 0 else ps_t2
                        tab.rec(s_v, ("xtev_prev", i, b))
                        eng.wait_ge(s_t, E(("xltr", i, b)))
                        if b == 0 and i > 0:
                            eng.wait_ge(s_ob, 16 * NK * i)
                        eng.tensor_copy(out=t_xt[:, b * 128:(b + 1) * 128],
                                        in_=pst[:]).then_inc(s_v, 1)
                        tab.rec(s_v, ("xtev", i, b))
                    eng.wait_ge(s_t, E(("xpt", i)))
                    if i > 0:
                        eng.wait_ge(s_odt, 32 * i)
                    eng.tensor_copy(out=t_pd[:], in_=ps_p1[:]).then_inc(s_v, 1)
                    eng.tensor_copy(out=t_pb[:], in_=ps_p2[:]).then_inc(s_v, 1)
                    tab.rec(s_v, ("xp", i))

        dry = _DryEng(tab.cnt)
        E = lambda name: tab.ev.get(name, 0) if tab.dry else tab.ev[name]
        for _ in range(2):
            tab.cnt.clear()
            em_vector(dry, E)
            em_scalar(dry, E)
            em_tensor(dry, E)
            em_sync(dry, E)
            em_gp_full(dry, E)
        tab.dry = False

        block = es.enter_context(nc.Block())
        tab.cnt.clear()

        @block.sync
        def _(eng):
            em_sync(eng, E)

        @block.gpsimd
        def _(eng):
            em_gp_full(eng, E)

        @block.tensor
        def _(eng):
            em_tensor(eng, E)

        @block.scalar
        def _(eng):
            em_scalar(eng, E)

        @block.vector
        def _(eng):
            em_vector(eng, E)

    return nc



def kernel_unfused(L0_attn_x, L0_attn_A_log, L0_attn_D_param, L0_attn_x_proj_w,
           L0_attn_dt_proj_w, L0_attn_dt_proj_b, L0_ffn_gamma, L0_ffn_W):
    f32 = np.float32
    x = np.ascontiguousarray(np.asarray(L0_attn_x, dtype=f32))
    W = np.ascontiguousarray(np.asarray(L0_ffn_W, dtype=f32))
    gamma = np.asarray(L0_ffn_gamma, dtype=f32)
    A_log = np.ascontiguousarray(np.asarray(L0_attn_A_log, dtype=f32))
    D_par = np.asarray(L0_attn_D_param, dtype=f32)
    xpw = np.asarray(L0_attn_x_proj_w, dtype=f32)
    dtw = np.asarray(L0_attn_dt_proj_w, dtype=f32)
    dtb = np.asarray(L0_attn_dt_proj_b, dtype=f32)
    ident_b = np.eye(128, dtype=f32).astype(bf16)
    ident_f = np.eye(128, dtype=f32)
    selmat = np.repeat(np.eye(N, dtype=f32), 128, axis=1).astype(bf16)
    xpwT = np.ascontiguousarray(xpw.T)

    nc1 = build_token_prog(1)
    im1 = []
    for c in range(8):
        b, h = c >> 1, c & 1
        im1.append({
            "Win": W, "gam": gamma[None, :], "ident": ident_b, "xpwT": xpwT,
            "xin": np.ascontiguousarray(x[b, h * TOK:(h + 1) * TOK, :]),
        })
    r1 = run_bass_kernel_spmd(nc1, im1, core_ids=list(range(8)))
    R1 = [r1.results[c]["r1o"] for c in range(8)]
    XLT = [r1.results[c]["xlnT"] for c in range(8)]
    DTV = [r1.results[c]["dtv"] for c in range(8)]
    BCV = [r1.results[c]["bcv"] for c in range(8)]

    nc2 = build_scan_prog()
    im2 = []
    for c in range(8):
        b, r = c >> 1, c & 1
        rs = slice(r * DH, (r + 1) * DH)
        im2.append({
            "xln": np.ascontiguousarray(
                np.concatenate([XLT[2 * b][rs, :], XLT[2 * b + 1][rs, :]], axis=1)),
            "dt": np.ascontiguousarray(
                np.concatenate([DTV[2 * b], DTV[2 * b + 1]], axis=1)),
            "Bm": np.ascontiguousarray(
                np.concatenate([BCV[2 * b][:N, :], BCV[2 * b + 1][:N, :]], axis=1)),
            "Cm": np.ascontiguousarray(
                np.concatenate([BCV[2 * b][N:, :], BCV[2 * b + 1][N:, :]], axis=1)),
            "dtwT": np.ascontiguousarray(dtw[rs, :].T),
            "dtbias": np.ascontiguousarray(dtb[rs, None]),
            "Alog": np.ascontiguousarray(A_log[rs, :]),
            "sel": selmat,
        })
    r2 = run_bass_kernel_spmd(nc2, im2, core_ids=list(range(8)))
    Y = [r2.results[c]["yout"] for c in range(8)]

    nc3 = build_token_prog(3)
    im3 = []
    for c in range(8):
        b, h = c >> 1, c & 1
        ts = slice(h * TOK, (h + 1) * TOK)
        im3.append({
            "Win": W, "gam": gamma[None, :], "ident": ident_b, "identf": ident_f,
            "xin": np.ascontiguousarray(R1[c]),
            "yTin": np.ascontiguousarray(
                np.concatenate([Y[2 * b][:, ts], Y[2 * b + 1][:, ts]], axis=0)),
            "dpar": D_par[None, :],
        })
    r3 = run_bass_kernel_spmd(nc3, im3, core_ids=list(range(8)))
    out = np.empty((B, L, D), dtype=f32)
    for c in range(8):
        b, h = c >> 1, c & 1
        out[b, h * TOK:(h + 1) * TOK, :] = r3.results[c]["outo"]
    return out



def build_fused():
    NT = TOK // 128
    NK = D // 128
    EPS_LN, EPS_RMS, EPS_Q = 1e-5, 1e-6, 1e-5
    nc = bass.Bass("TRN2", target_bir_lowering=False, debug=False)
    xin = nc.dram_tensor("xin", [TOK, D], F32, kind="ExternalInput").ap()
    Win = nc.dram_tensor("Win", [D, D], F32, kind="ExternalInput").ap()
    gam = nc.dram_tensor("gam", [1, D], F32, kind="ExternalInput").ap()
    ident = nc.dram_tensor("ident", [128, 128], BF16, kind="ExternalInput").ap()
    identf = nc.dram_tensor("identf", [128, 128], F32, kind="ExternalInput").ap()
    xpwTin = nc.dram_tensor("xpwT", [D, RT + 2 * N], F32, kind="ExternalInput").ap()
    dtwT = nc.dram_tensor("dtwT", [RT, DH], F32, kind="ExternalInput").ap()
    dtbias = nc.dram_tensor("dtbias", [DH, 1], F32, kind="ExternalInput").ap()
    Alog = nc.dram_tensor("Alog", [DH, N], F32, kind="ExternalInput").ap()
    selin = nc.dram_tensor("sel", [N, N * 128], BF16, kind="ExternalInput").ap()
    dpar = nc.dram_tensor("dpar", [1, D], F32, kind="ExternalInput").ap()
    mask = nc.dram_tensor("mask", [128, 2], F32, kind="ExternalInput").ap()
    outo = nc.dram_tensor("outo", [TOK, D], F32, kind="ExternalOutput").ap()
    r1d = nc.dram_tensor("r1d", [TOK, D], F32, kind="Internal").ap()
    wqd = nc.dram_tensor("wqd", [D, D], BF16, kind="Internal").ap()
    xci_h0 = nc.dram_tensor("xci_h0", [D, TOK // 2], BF16, kind="Internal").ap()
    xci_h1 = nc.dram_tensor("xci_h1", [D, TOK // 2], BF16, kind="Internal").ap()
    xco_h0 = nc.dram_tensor("xco_h0", [2, D, TOK // 2], BF16, kind="Internal").ap()
    xco_h1 = nc.dram_tensor("xco_h1", [2, D, TOK // 2], BF16, kind="Internal").ap()
    dtci = nc.dram_tensor("dtci", [RT, TOK], BF16, kind="Internal").ap()
    dtco = nc.dram_tensor("dtco", [2, RT, TOK], BF16, kind="Internal").ap()
    bcci = nc.dram_tensor("bcci", [2 * N, TOK], F32, kind="Internal").ap()
    bcco = nc.dram_tensor("bcco", [2, 2 * N, TOK], F32, kind="Internal").ap()
    yci_a = nc.dram_tensor("yci_a", [DH // 2, L], BF16, kind="Internal").ap()
    yci_b = nc.dram_tensor("yci_b", [DH // 2, L], BF16, kind="Internal").ap()
    yco_a = nc.dram_tensor("yco_a", [2, DH // 2, L], BF16, kind="Internal").ap()
    yco_b = nc.dram_tensor("yco_b", [2, DH // 2, L], BF16, kind="Internal").ap()
    brep_d = nc.dram_tensor("brep_d", [N * 128, L], BF16, kind="Internal").ap()
    crep_d = nc.dram_tensor("crep_d", [N * 128, L], BF16, kind="Internal").ap()
    RG = [[0, 1], [2, 3], [4, 5], [6, 7]]

    with ExitStack() as ces:
        csb = lambda name, shape, dt: ces.enter_context(nc.sbuf_tensor(name, shape, dt))
        t_cst = csb("t_cst", [128, 6], F32)
        t_st = csb("t_st", [128, 48], F32)
        t_msk = csb("t_msk", [128, 2], F32)
        s_cc = ces.enter_context(nc.semaphore("s_cc"))
        s_wqs = ces.enter_context(nc.semaphore("s_wqs"))
        s_r1s = ces.enter_context(nc.semaphore("s_r1s"))
        s_ob2 = ces.enter_context(nc.semaphore("s_ob2"))
        s_odt2 = ces.enter_context(nc.semaphore("s_odt2"))

        block = ces.enter_context(nc.Block())
        tab = EvTab()
        E = lambda name: tab.ev.get(name, 0) if tab.dry else tab.ev[name]
        dry = _DryEng(tab.cnt)

        def token_phase(P, es):
            ph = P["phase"]
            sbt = lambda name, shape, dt: es.enter_context(
                nc.sbuf_tensor(f"p{ph}_{name}", shape, dt))
            pre = f"p{ph}_"
            t_w = sbt("t_w", [128, 2 * D], F32) if ph == 1 else None
            t_wq = sbt("t_wq", [128, NK * D], BF16)
            t_x = sbt("t_x", [128, 2 * D], F32)
            t_xln = sbt("t_xln", [128, D], F32)
            t_scr = sbt("t_scr", [128, D], F32)
            t_q = sbt("t_q", [128, D], F32)
            t_xq = sbt("t_xq", [128, D], BF16)
            t_xqT = sbt("t_xqT", [128, NK * 128], BF16)
            t_gam = sbt("t_gam", [128, D], F32)
            t_r1 = sbt("t_r1", [128, D], F32)
            t_id = sbt("t_id", [128, 128], BF16)
            t_ones = sbt("t_ones", [128, 128], F32)
            if ph == 1:
                t_xlb = sbt("t_xlb", [128, D], BF16)
                t_xt = sbt("t_xt", [128, NK * 128], BF16)
                t_xpw = sbt("t_xpw", [128, NK * (RT + 2 * N)], BF16)
                t_xpwf = sbt("t_xpwf", [128, RT + 2 * N], F32)
                t_pd = sbt("t_pd", [128, 128], BF16)
                t_pb = sbt("t_pb", [32, 128], F32)
            else:
                t_idf = sbt("t_idf", [128, 128], F32)
                t_y = sbt("t_y", [128, 2 * D], BF16)
                t_y2 = sbt("t_y2", [128, 2 * D], BF16)
                t_yt = sbt("t_yt", [128, D], BF16)
                t_ytk = sbt("t_ytk", [128, D], BF16)
                t_r2 = sbt("t_r2", [128, 2 * D], F32)
                t_dr = sbt("t_dr", [128, D], F32)
            ps_m = es.enter_context(nc.psum_tensor(pre + "ps_m", [128, D], F32))
            ps_t1 = es.enter_context(nc.psum_tensor(pre + "ps_t1", [128, 512], BF16))
            ps_t2 = es.enter_context(nc.psum_tensor(pre + "ps_t2", [128, 512], BF16))
            if ph == 1:
                ps_p1 = es.enter_context(nc.psum_tensor(pre + "ps_p1", [128, 128], F32))
                ps_p2 = es.enter_context(nc.psum_tensor(pre + "ps_p2", [32, 128], F32))
                ps_rep = ps_p1
            else:
                ps_f1 = es.enter_context(nc.psum_tensor(pre + "ps_f1", [128, 512], BF16))
                ps_f2 = es.enter_context(nc.psum_tensor(pre + "ps_f2", [128, 512], BF16))
                ps_rep = ps_m
            sm = {}
            for nm in ["s_wa0", "s_wa1", "s_wb0", "s_wb1", "s_x0", "s_x1", "s_g",
                       "s_xp", "s_t", "s_s", "s_v", "s_or", "s_odt", "s_ob",
                       "s_y0", "s_y1"]:
                sm[nm] = es.enter_context(nc.semaphore(pre + nm))
            xsrc = xin if ph == 1 else r1d

            def V(name):
                return (ph, name)

            def em_sync(eng):
                if ph == 3:
                    eng.wait_ge(s_cc, 6)
                eng.dma_start(out=t_gam[0:1, :], in_=gam).then_inc(sm["s_g"], 16)
                eng.dma_start(out=t_id[:], in_=ident).then_inc(sm["s_g"], 16)
                if ph == 3:
                    eng.dma_start(out=t_dr[0:1, :], in_=dpar).then_inc(sm["s_g"], 16)
                    eng.dma_start(out=t_idf[:], in_=identf).then_inc(sm["s_g"], 16)
                if ph == 1:
                    for k in range(NK):
                        if k >= 1:
                            eng.wait_ge(sm["s_v"], E(V(("xpc", k - 1))))
                        eng.dma_start(out=t_xpwf[:],
                                      in_=xpwTin[k * 128:(k + 1) * 128, :]).then_inc(sm["s_xp"], 16)
                    for k in range(NK):
                        if k >= 2:
                            eng.wait_ge(sm["s_v"], E(V(("wstat", k - 2))))
                        eng.dma_start(out=t_w[:, (k % 2) * D:(k % 2 + 1) * D],
                                      in_=Win[k * 128:(k + 1) * 128, :]).then_inc(
                            sm["s_wa0"] if k % 2 == 0 else sm["s_wa1"], 16)
                    for k in range(NK):
                        eng.wait_ge(sm["s_v"], E(V(("wstat", NK - 2 + k))) if k < 2
                                    else E(V(("wq", k - 2))))
                        eng.dma_start(out=t_w[:, (k % 2) * D:(k % 2 + 1) * D],
                                      in_=Win[k * 128:(k + 1) * 128, :]).then_inc(
                            sm["s_wb0"] if k % 2 == 0 else sm["s_wb1"], 16)
                else:
                    eng.wait_ge(s_wqs, 16 * NK)
                    for b in range(NK):
                        eng.dma_start(out=t_wq[:, b * D:(b + 1) * D],
                                      in_=wqd[b * 128:(b + 1) * 128, :]).then_inc(
                            sm["s_g"], 16)
                    eng.wait_ge(s_r1s, 16 * NT)
                for i in range(NT):
                    if i >= 2:
                        eng.wait_ge(sm["s_v"], E(V(("ep", i - 2))))
                    eng.dma_start(out=t_x[:, (i % 2) * D:(i % 2 + 1) * D],
                                  in_=xsrc[i * 128:(i + 1) * 128, :]).then_inc(
                        sm["s_x0"] if i % 2 == 0 else sm["s_x1"], 16)
                    if ph == 3:
                        if i >= 2:
                            eng.wait_ge(sm["s_v"], E(V(("r2", i - 2))))
                        for b in range(NK):
                            half, bb = b // (NK // 2), b % (NK // 2)
                            yct = yco_a if bb < NK // 4 else yco_b
                            bb2 = bb % (NK // 4)
                            src0 = yct[half, bb2 * 128:(bb2 + 1) * 128,
                                       0 * TOK + i * 128:0 * TOK + (i + 1) * 128]
                            src1 = yct[half, bb2 * 128:(bb2 + 1) * 128,
                                       1 * TOK + i * 128:1 * TOK + (i + 1) * 128]
                            eng.dma_start(
                                out=t_y[:, (i % 2) * D + b * 128:(i % 2) * D + (b + 1) * 128],
                                in_=src0).then_inc(
                                sm["s_y0"] if i % 2 == 0 else sm["s_y1"], 16)
                            eng.dma_start(
                                out=t_y2[:, (i % 2) * D + b * 128:(i % 2) * D + (b + 1) * 128],
                                in_=src1).then_inc(
                                sm["s_y0"] if i % 2 == 0 else sm["s_y1"], 16)

            def em_gp(eng):
                if ph == 3:
                    for i in range(NT):
                        eng.wait_ge(sm["s_v"], E(V(("ep", i))))
                        eng.dma_start(out=outo[i * 128:(i + 1) * 128, :],
                                      in_=t_r1[:]).then_inc(sm["s_or"], 16)
                    eng.wait_ge(sm["s_or"], 16 * NT)
                else:
                    for b in range(NK):
                        eng.wait_ge(sm["s_v"], E(V(("wev", NK - 1, b // 4))))
                        eng.dma_start(out=wqd[b * 128:(b + 1) * 128, :],
                                      in_=t_wq[:, b * D:(b + 1) * D]).then_inc(s_wqs, 16)
                    for i in range(NT):
                        eng.wait_ge(sm["s_v"], E(V(("ep", i))))
                        eng.dma_start(out=r1d[i * 128:(i + 1) * 128, :],
                                      in_=t_r1[:]).then_inc(s_r1s, 16)
                        for b in range(NK):
                            eng.wait_ge(sm["s_s"], E(V(("xtev", i, b // 4))))
                            xci_t = xci_h0 if i < NT // 2 else xci_h1
                            ii = i % (NT // 2)
                            eng.dma_start(
                                out=xci_t[b * 128:(b + 1) * 128, ii * 128:(ii + 1) * 128],
                                in_=t_xt[:, b * 128:(b + 1) * 128]).then_inc(s_ob2, 16)
                        eng.wait_ge(sm["s_s"], E(V(("xp", i))))
                        eng.dma_start(out=dtci[:, i * 128:(i + 1) * 128],
                                      in_=t_pd[:]).then_inc(s_odt2, 16)
                        eng.dma_start(out=bcci[:, i * 128:(i + 1) * 128],
                                      in_=t_pb[:]).then_inc(s_odt2, 16)
                        if i == NT // 2 - 1:
                            eng.wait_ge(s_ob2, 16 * NK * (NT // 2))
                            eng.collective_compute(
                                "AllGather", OP.bypass, ins=[xci_h0],
                                outs=[xco_h0], replica_groups=RG).then_inc(s_cc, 1)
                    eng.wait_ge(s_odt2, 32 * NT)
                    eng.collective_compute("AllGather", OP.bypass, ins=[bcci],
                                           outs=[bcco], replica_groups=RG).then_inc(s_cc, 1)
                    eng.collective_compute("AllGather", OP.bypass, ins=[dtci],
                                           outs=[dtco], replica_groups=RG).then_inc(s_cc, 1)
                    eng.wait_ge(s_ob2, 16 * NT * NK)
                    eng.collective_compute("AllGather", OP.bypass, ins=[xci_h1],
                                           outs=[xco_h1], replica_groups=RG).then_inc(s_cc, 1)

            def em_tensor(eng):
                if ph == 3:
                    eng.wait_ge(s_cc, 6)
                eng.wait_ge(sm["s_g"], 16 * ((4 + NK) if ph == 3 else 2))
                eng.wait_ge(sm["s_v"], E(V("ones")))
                nrep = NK * (2 if ph == 3 else 1)
                for blk in range(nrep):
                    src_row = t_gam if blk < NK else t_dr
                    cb = (blk % NK) * 128
                    if blk > 0:
                        eng.wait_ge(sm["s_v"], E(V(("rev", blk - 1))))
                    eng.matmul(out=ps_rep[:, 0:128], lhsT=t_ones[0:1, :],
                               rhs=src_row[0:1, cb:cb + 128],
                               start=True, stop=True).then_inc(sm["s_t"], 1)
                    tab.rec(sm["s_t"], V(("rep", blk)))
                if ph == 1:
                    eng.wait_ge(sm["s_v"], E(V(("wstat", NK - 1))))
                    eng.wait_ge(sm["s_v"], E(V(("rev", nrep - 1))))
                    eng.matmul(out=ps_rep[:, 0:1], lhsT=t_ones[:],
                               rhs=t_st[:, 18:19], start=True, stop=True).then_inc(sm["s_t"], 1)
                    tab.rec(sm["s_t"], V(("wsum",)))
                    NG = NK // 4
                    for k in range(NK):
                        eng.wait_ge(sm["s_v"], E(V(("wq3", k))))
                        for g in range(NG):
                            un = k * NG + g
                            pst = ps_t1 if un % 2 == 0 else ps_t2
                            if un >= 2:
                                eng.wait_ge(sm["s_v"], E(V(("wev", (un - 2) // NG,
                                                            (un - 2) % NG))))
                            for q in range(4):
                                b = g * 4 + q
                                eng.transpose(out=pst[:, q * 128:(q + 1) * 128],
                                              in_=t_xq[:, b * 128:(b + 1) * 128],
                                              identity=t_id[:]).then_inc(sm["s_t"], 1)
                            tab.rec(sm["s_t"], V(("wtr", k, g)))

                def emit_xpt(i):
                    eng.wait_ge(sm["s_s"], E(V(("xtev", i, NK // 4 - 1))))
                    if i > 0:
                        eng.wait_ge(sm["s_s"], E(V(("xp", i - 1))))
                    else:
                        eng.wait_ge(sm["s_v"], E(V(("wsev",))))
                    lastd = lastb = None
                    for k in range(NK):
                        lastd = eng.matmul(out=ps_p1[:],
                                           lhsT=t_xpw[:, k * 160:k * 160 + 128],
                                           rhs=t_xt[:, k * 128:(k + 1) * 128],
                                           start=(k == 0), stop=(k == NK - 1))
                        lastb = eng.matmul(out=ps_p2[:],
                                           lhsT=t_xpw[:, k * 160 + 128:k * 160 + 160],
                                           rhs=t_xt[:, k * 128:(k + 1) * 128],
                                           start=(k == 0), stop=(k == NK - 1))
                    lastd.then_inc(sm["s_t"], 1)
                    lastb.then_inc(sm["s_t"], 1)
                    tab.rec(sm["s_t"], V(("xpt", i)))

                for i in range(NT):
                    if ph == 3:
                        eng.wait_ge(sm["s_v"], E(V(("ysel", i))))
                        for g in range(NK // 4):
                            un = i * (NK // 4) + g
                            pst = ps_f1 if un % 2 == 0 else ps_f2
                            if g >= 2:
                                eng.wait_ge(sm["s_s"], E(V(("yev", i, g - 2))))
                            elif i > 0:
                                eng.wait_ge(sm["s_s"], E(V(("yev", i - 1, NK // 4 - 2 + g))))
                            for q in range(4):
                                b = g * 4 + q
                                eng.transpose(out=pst[:, q * 128:(q + 1) * 128],
                                              in_=t_yt[:, b * 128:(b + 1) * 128],
                                              identity=t_id[:]).then_inc(sm["s_t"], 1)
                            tab.rec(sm["s_t"], V(("ytr", i, g)))
                    eng.wait_ge(sm["s_v"], E(V(("xq", i))))
                    for g in range(NK // 4):
                        un = i * (NK // 4) + g
                        pst = ps_t1 if un % 2 == 0 else ps_t2
                        eng.wait_ge(sm["s_s"], E(V(("xqev_prev", i, g))))
                        for q in range(4):
                            b = g * 4 + q
                            eng.transpose(out=pst[:, q * 128:(q + 1) * 128],
                                          in_=t_xq[:, b * 128:(b + 1) * 128],
                                          identity=t_id[:]).then_inc(sm["s_t"], 1)
                        tab.rec(sm["s_t"], V(("xqtr", i, g)))
                    eng.wait_ge(sm["s_s"], E(V(("xqev", i, NK // 4 - 1))))
                    if i > 0:
                        eng.wait_ge(sm["s_v"], E(V(("ep", i - 1))))
                    for ob in range(4):
                        last = None
                        for k in range(NK):
                            last = eng.matmul(out=ps_m[:, ob * 512:(ob + 1) * 512],
                                              lhsT=t_xqT[:, k * 128:(k + 1) * 128],
                                              rhs=t_wq[:, k * D + ob * 512:k * D + (ob + 1) * 512],
                                              start=(k == 0), stop=(k == NK - 1))
                        last.then_inc(sm["s_t"], 1)
                    tab.rec(sm["s_t"], V(("mm", i)))
                    if ph == 1:
                        if i > 0:
                            emit_xpt(i - 1)
                        eng.wait_ge(sm["s_v"], E(V(("xlb", i))))
                        for g in range(NK // 4):
                            un = i * (NK // 4) + g
                            pst = ps_t1 if un % 2 == 0 else ps_t2
                            eng.wait_ge(sm["s_s"], E(V(("xtev_prev", i, g))))
                            for q in range(4):
                                b = g * 4 + q
                                eng.transpose(out=pst[:, q * 128:(q + 1) * 128],
                                              in_=t_xlb[:, b * 128:(b + 1) * 128],
                                              identity=t_id[:]).then_inc(sm["s_t"], 1)
                            tab.rec(sm["s_t"], V(("xltr", i, g)))
                if ph == 1:
                    emit_xpt(NT - 1)

            def em_scalar(eng):
                if ph == 3:
                    eng.wait_ge(s_cc, 6)
                c_eps = t_cst[:, 0:1]
                c_rms = t_cst[:, 1:2]
                c_invd = t_cst[:, 2:3]
                c_half = t_cst[:, 3:4]
                QS = [{"ssq2": 7, "lnw": 8, "rms2": 9, "amax": 15, "ame": 22,
                       "rec": 23, "a": 10, "resid": 11},
                      {"ssq2": 31, "lnw": 32, "rms2": 33, "amax": 34, "ame": 35,
                       "rec": 36, "a": 37, "resid": 38}]

                def rstd1(i):
                    pb = 24 * (i % 2)
                    eng.wait_ge(sm["s_v"], E(V(("var1", i))))
                    eng.drain()
                    eng.activation(out=t_st[:, pb + 5:pb + 6],
                                   in_=t_st[:, pb + 3:pb + 4],
                                   func=AF.Ln, bias=c_eps).then_inc(sm["s_s"], 1)
                    eng.drain()
                    eng.activation(out=t_st[:, pb + 6:pb + 7],
                                   in_=t_st[:, pb + 5:pb + 6],
                                   func=AF.Exp, scale=c_half).then_inc(sm["s_s"], 1)
                    tab.rec(sm["s_s"], V(("rstd1", i)))

                def rstd2(i):
                    eng.wait_ge(sm["s_v"], E(V(("var2", i))))
                    eng.drain()
                    eng.activation(out=t_st[:, 43:44], in_=t_st[:, 41:42], func=AF.Ln,
                                   bias=c_eps).then_inc(sm["s_s"], 1)
                    eng.drain()
                    eng.activation(out=t_st[:, 44:45], in_=t_st[:, 43:44], func=AF.Exp,
                                   scale=c_half).then_inc(sm["s_s"], 1)
                    tab.rec(sm["s_s"], V(("rstd2", i)))

                def s_xqev(i):
                    for g in range(NK // 4):
                        un = i * (NK // 4) + g
                        pst = ps_t1 if un % 2 == 0 else ps_t2
                        tab.rec(sm["s_s"], V(("xqev_prev", i, g)))
                        eng.wait_ge(sm["s_t"], E(V(("xqtr", i, g))))
                        eng.drain()
                        eng.activation(out=t_xqT[:, g * 512:(g + 1) * 512],
                                       in_=pst[:], func=AF.Copy).then_inc(sm["s_s"], 1)
                        tab.rec(sm["s_s"], V(("xqev", i, g)))

                def s_yev(i):
                    for g in range(NK // 4):
                        eng.wait_ge(sm["s_t"], E(V(("ytr", i, g))))
                        pst = ps_f1 if (i * (NK // 4) + g) % 2 == 0 else ps_f2
                        eng.drain()
                        eng.activation(out=t_ytk[:, g * 512:(g + 1) * 512],
                                       in_=pst[:], func=AF.Copy).then_inc(sm["s_s"], 1)
                        tab.rec(sm["s_s"], V(("yev", i, g)))

                def s_xtev(i):
                    for g in range(NK // 4):
                        tab.rec(sm["s_s"], V(("xtev_prev", i, g)))
                        eng.wait_ge(sm["s_t"], E(V(("xltr", i, g))))
                        if g == 0 and i > 0:
                            eng.wait_ge(s_ob2, 16 * NK * i)
                        pst = ps_t1 if (i * (NK // 4) + g) % 2 == 0 else ps_t2
                        eng.drain()
                        eng.activation(out=t_xt[:, g * 512:(g + 1) * 512],
                                       in_=pst[:], func=AF.Copy).then_inc(sm["s_s"], 1)
                        tab.rec(sm["s_s"], V(("xtev", i, g)))

                def s_xp(i):
                    eng.wait_ge(sm["s_t"], E(V(("xpt", i))))
                    if i > 0:
                        eng.wait_ge(s_odt2, 32 * i)
                    eng.drain()
                    eng.activation(out=t_pd[:], in_=ps_p1[:],
                                   func=AF.Copy).then_inc(sm["s_s"], 1)
                    eng.activation(out=t_pb[:], in_=ps_p2[:],
                                   func=AF.Copy).then_inc(sm["s_s"], 1)
                    tab.rec(sm["s_s"], V(("xp", i)))

                def rms2(i):
                    q = QS[i % 2]
                    eng.wait_ge(sm["s_v"], E(V(("ssq2", i))))
                    eng.drain()
                    eng.activation(out=t_st[:, q["lnw"]:q["lnw"] + 1],
                                   in_=t_st[:, q["ssq2"]:q["ssq2"] + 1], func=AF.Ln,
                                   scale=c_invd, bias=c_rms).then_inc(sm["s_s"], 1)
                    eng.drain()
                    eng.activation(out=t_st[:, q["rms2"]:q["rms2"] + 1],
                                   in_=t_st[:, q["lnw"]:q["lnw"] + 1], func=AF.Exp,
                                   scale=c_half).then_inc(sm["s_s"], 1)
                    tab.rec(sm["s_s"], V(("rms2", i)))

                rstd1(0)
                if ph == 1:
                    rms2(0)
                    s_xqev(0)
                    rstd1(1)
                    rms2(1)
                    for i in range(NT):
                        rstd2(i)
                        if i + 2 < NT:
                            rstd1(i + 2)
                            rms2(i + 2)
                        s_xtev(i)
                        if i + 1 < NT:
                            s_xqev(i + 1)
                        s_xp(i)
                else:
                    for i in range(NT):
                        s_yev(i)
                        rstd2(i)
                        rms2(i)
                        if i + 1 < NT:
                            rstd1(i + 1)
                        s_xqev(i)

            def ln_stats(eng, src_ap, sum_c, ssq_c, negm_c, var_c, dummy=None):
                dum = t_scr if dummy is None else dummy
                eng.tensor_reduce(out=t_st[:, sum_c:sum_c + 1], in_=src_ap,
                                  axis=AX.X, op=OP.add).then_inc(sm["s_v"], 1)
                eng.scalar_tensor_tensor(out=dum[:], in0=src_ap, scalar=1.0,
                                         in1=src_ap, op0=OP.mult, op1=OP.mult,
                                         accum_out=t_st[:, ssq_c:ssq_c + 1]).then_inc(sm["s_v"], 1)
                eng.drain()
                eng.tensor_scalar(out=t_st[:, negm_c:negm_c + 1],
                                  in0=t_st[:, sum_c:sum_c + 1],
                                  scalar1=float(-1.0 / D), scalar2=None,
                                  op0=OP.mult).then_inc(sm["s_v"], 1)
                eng.drain()
                eng.tensor_scalar(out=t_st[:, var_c + 1:var_c + 2],
                                  in0=t_st[:, negm_c:negm_c + 1],
                                  scalar1=t_st[:, negm_c:negm_c + 1], scalar2=None,
                                  op0=OP.mult).then_inc(sm["s_v"], 1)
                eng.drain()
                eng.scalar_tensor_tensor(out=t_st[:, var_c:var_c + 1],
                                         in0=t_st[:, ssq_c:ssq_c + 1],
                                         scalar=float(1.0 / D),
                                         in1=t_st[:, var_c + 1:var_c + 2],
                                         op0=OP.mult, op1=OP.subtract).then_inc(sm["s_v"], 1)

            def em_vector(eng):
                if ph == 3:
                    eng.wait_ge(s_cc, 6)
                if ph == 1:
                    eng.memset(t_cst[:, 0:1], float(EPS_LN))
                    eng.memset(t_cst[:, 1:2], float(EPS_RMS))
                    eng.memset(t_cst[:, 2:3], float(1.0 / D))
                    eng.memset(t_cst[:, 3:4], -0.5)
                eng.memset(t_ones[:], 1.0).then_inc(sm["s_v"], 1)
                tab.rec(sm["s_v"], V("ones"))
                eng.wait_ge(sm["s_g"], 16 * ((4 + NK) if ph == 3 else 2))
                nrep = NK * (2 if ph == 3 else 1)
                for blk in range(nrep):
                    dst_row = t_gam if blk < NK else t_dr
                    cb = (blk % NK) * 128
                    eng.wait_ge(sm["s_t"], E(V(("rep", blk))))
                    eng.tensor_copy(out=dst_row[:, cb:cb + 128],
                                    in_=ps_rep[:, 0:128]).then_inc(sm["s_v"], 1)
                    tab.rec(sm["s_v"], V(("rev", blk)))
                if ph == 1:
                    for k in range(NK):
                        eng.wait_ge(sm["s_xp"], 16 * (k + 1))
                        eng.tensor_copy(out=t_xpw[:, k * 160:(k + 1) * 160],
                                        in_=t_xpwf[:]).then_inc(sm["s_v"], 1)
                        tab.rec(sm["s_v"], V(("xpc", k)))
                    for k in range(NK):
                        eng.wait_ge(sm["s_wa0"] if k % 2 == 0 else sm["s_wa1"],
                                    16 * (k // 2 + 1))
                        eng.tensor_reduce(out=t_st[:, 16 + (k % 2):17 + (k % 2)],
                                          in_=t_w[:, (k % 2) * D:(k % 2 + 1) * D],
                                          axis=AX.X, op=OP.add,
                                          apply_absolute_value=True).then_inc(sm["s_v"], 1)
                        eng.drain()
                        if k == 0:
                            eng.tensor_copy(out=t_st[:, 18:19],
                                            in_=t_st[:, 16:17]).then_inc(sm["s_v"], 1)
                        else:
                            eng.tensor_tensor(out=t_st[:, 19:20], in0=t_st[:, 18:19],
                                              in1=t_st[:, 16 + (k % 2):17 + (k % 2)],
                                              op=OP.add).then_inc(sm["s_v"], 1)
                            eng.drain()
                            eng.tensor_copy(out=t_st[:, 18:19],
                                            in_=t_st[:, 19:20]).then_inc(sm["s_v"], 1)
                        tab.rec(sm["s_v"], V(("wstat", k)))
                    eng.drain()
                    eng.wait_ge(sm["s_t"], E(V(("wsum",))))
                    eng.tensor_copy(out=t_st[:, 18:19], in_=ps_rep[:, 0:1]).then_inc(sm["s_v"], 1)
                    tab.rec(sm["s_v"], V(("wsev",)))
                    eng.drain()
                    eng.tensor_scalar(out=t_st[:, 20:21], in0=t_st[:, 18:19],
                                      scalar1=float(1.0 / (D * D)), scalar2=float(EPS_Q),
                                      op0=OP.mult, op1=OP.add).then_inc(sm["s_v"], 1)
                    eng.drain()
                    eng.reciprocal(out=t_st[:, 21:22], in_=t_st[:, 20:21]).then_inc(sm["s_v"], 1)
                    eng.drain()
                    tab.rec(sm["s_v"], V(("ws",)))
                    for k in range(NK):
                        eng.wait_ge(sm["s_wb0"] if k % 2 == 0 else sm["s_wb1"],
                                    16 * (k // 2 + 1))
                        eng.tensor_scalar(out=t_q[:], in0=t_w[:, (k % 2) * D:(k % 2 + 1) * D],
                                          scalar1=t_st[:, 21:22], scalar2=MAGIC,
                                          op0=OP.mult, op1=OP.add).then_inc(sm["s_v"], 1)
                        eng.drain()
                        eng.tensor_scalar(out=t_scr[:], in0=t_q[:],
                                          scalar1=-MAGIC, scalar2=1.0,
                                          op0=OP.add, op1=OP.min).then_inc(sm["s_v"], 1)
                        eng.drain()
                        tab.rec(sm["s_v"], V(("wq", k)))
                        if k >= 1:
                            eng.wait_ge(sm["s_t"], E(V(("wtr", k - 1, NK // 4 - 1))))
                        eng.tensor_scalar(out=t_xq[:], in0=t_scr[:],
                                          scalar1=-1.0, scalar2=None,
                                          op0=OP.max).then_inc(sm["s_v"], 1)
                        tab.rec(sm["s_v"], V(("wq3", k)))
                        for g in range(NK // 4):
                            un = k * (NK // 4) + g
                            pst = ps_t1 if un % 2 == 0 else ps_t2
                            eng.wait_ge(sm["s_t"], E(V(("wtr", k, g))))
                            for q in range(4):
                                b = g * 4 + q
                                eng.tensor_copy(
                                    out=t_wq[:, b * D + k * 128:b * D + (k + 1) * 128],
                                    in_=pst[:, q * 128:(q + 1) * 128]).then_inc(sm["s_v"], 1)
                            tab.rec(sm["s_v"], V(("wev", k, g)))
                QS = [{"ssq2": 7, "lnw": 8, "rms2": 9, "amax": 15, "ame": 22,
                       "rec": 23, "a": 10, "resid": 11},
                      {"ssq2": 31, "lnw": 32, "rms2": 33, "amax": 34, "ame": 35,
                       "rec": 36, "a": 37, "resid": 38}]

                def xcol(c):
                    return t_st[:, c:c + 1]

                def pre(i):
                    pb = 24 * (i % 2)
                    q = QS[i % 2]
                    xs = t_x[:, (i % 2) * D:(i % 2 + 1) * D]
                    if i == 0:
                        eng.wait_ge(sm["s_x0"], 16)
                        ln_stats(eng, xs, pb + 0, pb + 1, pb + 2, pb + 3)
                        tab.rec(sm["s_v"], V(("var1", i)))
                    eng.wait_ge(sm["s_s"], E(V(("rstd1", i))))
                    eng.drain()
                    eng.tensor_scalar(out=t_xln[:], in0=xs,
                                      scalar1=xcol(pb + 2), scalar2=xcol(pb + 6),
                                      op0=OP.add, op1=OP.mult).then_inc(sm["s_v"], 1)
                    tab.rec(sm["s_v"], V(("xln1", i)))
                    if ph == 3:
                        r2s = t_r2[:, (i % 2) * D:(i % 2 + 1) * D]
                        eng.wait_ge(sm["s_y0"] if i % 2 == 0 else sm["s_y1"],
                                    32 * NK * (i // 2 + 1))
                        eng.drain()
                        eng.tensor_scalar(out=t_yt[:], in0=t_y[:, (i % 2) * D:(i % 2 + 1) * D],
                                          scalar1=t_msk[:, 1:2], scalar2=None,
                                          op0=OP.mult).then_inc(sm["s_v"], 1)
                        eng.drain()
                        eng.scalar_tensor_tensor(out=t_yt[:],
                                                 in0=t_y2[:, (i % 2) * D:(i % 2 + 1) * D],
                                                 scalar=t_msk[:, 0:1], in1=t_yt[:],
                                                 op0=OP.mult, op1=OP.add).then_inc(sm["s_v"], 1)
                        tab.rec(sm["s_v"], V(("ysel", i)))
                        eng.drain()
                        eng.tensor_tensor(out=t_scr[:], in0=t_xln[:], in1=t_dr[:],
                                          op=OP.mult).then_inc(sm["s_v"], 1)
                        eng.drain()
                        eng.wait_ge(sm["s_s"], E(V(("yev", i, NK // 4 - 1))))
                        eng.tensor_tensor(out=t_q[:], in0=xs, in1=t_ytk[:],
                                          op=OP.add).then_inc(sm["s_v"], 1)
                        eng.drain()
                        eng.tensor_tensor(out=r2s, in0=t_q[:], in1=t_scr[:],
                                          op=OP.add).then_inc(sm["s_v"], 1)
                        tab.rec(sm["s_v"], V(("r2", i)))
                        eng.drain()
                        ln_stats(eng, r2s, 19, 39, 40, 41)
                        tab.rec(sm["s_v"], V(("var2", i)))
                        eng.wait_ge(sm["s_s"], E(V(("rstd2", i))))
                        eng.drain()
                        eng.tensor_scalar(out=t_xln[:], in0=r2s,
                                          scalar1=xcol(40), scalar2=xcol(44),
                                          op0=OP.add, op1=OP.mult).then_inc(sm["s_v"], 1)
                    eng.drain()
                    eng.scalar_tensor_tensor(out=t_scr[:], in0=t_xln[:], scalar=1.0,
                                             in1=t_xln[:], op0=OP.mult, op1=OP.mult,
                                             accum_out=xcol(q["ssq2"])).then_inc(sm["s_v"], 1)
                    tab.rec(sm["s_v"], V(("ssq2", i)))
                    eng.wait_ge(sm["s_s"], E(V(("rms2", i))))
                    eng.drain()
                    eng.scalar_tensor_tensor(out=t_scr[:], in0=t_xln[:],
                                             scalar=xcol(q["rms2"]), in1=t_gam[:],
                                             op0=OP.mult, op1=OP.mult).then_inc(sm["s_v"], 1)
                    eng.drain()
                    eng.tensor_reduce(out=xcol(q["amax"]), in_=t_scr[:], axis=AX.X,
                                      op=OP.max, apply_absolute_value=True).then_inc(sm["s_v"], 1)
                    eng.drain()
                    eng.tensor_scalar(out=xcol(q["ame"]), in0=xcol(q["amax"]),
                                      scalar1=float(EPS_Q), scalar2=None,
                                      op0=OP.add).then_inc(sm["s_v"], 1)
                    eng.drain()
                    eng.reciprocal(out=xcol(q["rec"]), in_=xcol(q["ame"])).then_inc(sm["s_v"], 1)
                    eng.drain()
                    eng.tensor_scalar(out=xcol(q["a"]), in0=xcol(q["rec"]),
                                      scalar1=127.0, scalar2=None,
                                      op0=OP.mult).then_inc(sm["s_v"], 1)
                    eng.tensor_scalar(out=xcol(q["resid"]), in0=xcol(q["ame"]),
                                      scalar1=float(0.5 / 127.0), scalar2=xcol(20),
                                      op0=OP.mult, op1=OP.mult).then_inc(sm["s_v"], 1)
                    tab.rec(sm["s_v"], V(("qs", i)))
                    eng.drain()
                    eng.tensor_scalar(out=t_q[:], in0=t_scr[:],
                                      scalar1=xcol(q["a"]), scalar2=MAGIC,
                                      op0=OP.mult, op1=OP.add).then_inc(sm["s_v"], 1)
                    eng.drain()
                    if i > 0:
                        eng.wait_ge(sm["s_t"], E(V(("xqtr", i - 1, NK // 4 - 1))))
                    elif ph == 1:
                        eng.wait_ge(sm["s_t"], E(V(("wtr", NK - 1, NK // 4 - 1))))
                    eng.tensor_scalar(out=t_xq[:], in0=t_q[:],
                                      scalar1=-MAGIC, scalar2=None,
                                      op0=OP.add).then_inc(sm["s_v"], 1)
                    tab.rec(sm["s_v"], V(("xq", i)))
                    if i == 0:
                        hoist_stats(1)

                def hoist_stats(k):
                    pbk = 24 * (k % 2)
                    sxk = sm["s_x0"] if k % 2 == 0 else sm["s_x1"]
                    xsk = t_x[:, (k % 2) * D:(k % 2 + 1) * D]
                    eng.wait_ge(sxk, 16 * (k // 2 + 1))
                    eng.drain()
                    ln_stats(eng, xsk, pbk + 0, pbk + 1, pbk + 2, pbk + 3,
                             dummy=t_xln)
                    tab.rec(sm["s_v"], V(("var1", k)))

                pre(0)
                for i in range(NT):
                    q = QS[i % 2]
                    xs = t_x[:, (i % 2) * D:(i % 2 + 1) * D]
                    resid = xs if ph == 1 else t_r2[:, (i % 2) * D:(i % 2 + 1) * D]
                    if i + 1 < NT:
                        pre(i + 1)
                    eng.wait_ge(sm["s_t"], E(V(("mm", i))))
                    if i > 0:
                        eng.wait_ge(sm["s_or"] if ph == 3 else s_r1s, 16 * i)
                    eng.drain()
                    eng.scalar_tensor_tensor(out=t_r1[:], in0=ps_m[:],
                                             scalar=xcol(q["resid"]), in1=resid,
                                             op0=OP.mult, op1=OP.add).then_inc(sm["s_v"], 1)
                    tab.rec(sm["s_v"], V(("ep", i)))
                    if i + 2 < NT:
                        hoist_stats(i + 2)
                    if ph == 1:
                        eng.drain()
                        ln_stats(eng, t_r1[:], 19, 39, 40, 41)
                        tab.rec(sm["s_v"], V(("var2", i)))
                        eng.wait_ge(sm["s_s"], E(V(("rstd2", i))))
                        eng.drain()
                        eng.tensor_scalar(out=t_xlb[:], in0=t_r1[:],
                                          scalar1=xcol(40), scalar2=xcol(44),
                                          op0=OP.add, op1=OP.mult).then_inc(sm["s_v"], 1)
                        tab.rec(sm["s_v"], V(("xlb", i)))

            return dict(sync=em_sync, gp=em_gp, tensor=em_tensor,
                        scalar=em_scalar, vector=em_vector)

        def scan_phase(es):
            NT_T = TOK // 128
            NK = D // 128
            NKT = NT_T * NK
            sbt = lambda name, shape, dt: es.enter_context(
                nc.sbuf_tensor("p2_" + name, shape, dt))
            t_dtwb = sbt("t_dtwb", [RT, DH], BF16)
            t_dtb = sbt("t_dtb", [128, NDT], F32)
            t_alog = sbt("t_alog", [128, NDT * N], F32)
            t_a = sbt("t_a", [128, NDT * N], F32)
            t_an = sbt("t_an", [128, NDT * N], F32)
            t_bc = sbt("t_bc", [N, TOK], F32)
            t_bcb = sbt("t_bcb", [N, L], BF16)
            t_sel = sbt("t_sel", [N, N * 128], BF16)
            t_dt = sbt("t_dt", [RT, L], BF16)
            t_xln = sbt("t_xln", [128, 2 * L], BF16)
            t_xl2 = sbt("t_xl2", [128, L], BF16)
            t_delta = sbt("t_delta", [128, L], BF16)
            t_dx = sbt("t_dx", [128, L], BF16)
            t_sp1 = sbt("t_sp1", [128, HL], F32)
            t_sp2 = sbt("t_sp2", [128, HL], F32)
            t_sp3 = sbt("t_sp3", [128, HL], F32)
            t_sp4 = sbt("t_sp4", [128, HL], F32)
            t_da = sbt("t_da", [128, L], F32)
            t_u = sbt("t_u", [128, L], BF16)
            t_h = sbt("t_h", [128, L], BF16)
            t_yp = sbt("t_yp", [128, L], BF16)
            t_mg = sbt("t_mg", [128, L], BF16)
            t_p1 = sbt("t_p1", [128, L], BF16)
            t_p2 = sbt("t_p2", [128, L], BF16)
            t_p3 = sbt("t_p3", [128, L], BF16)
            t_p4 = sbt("t_p4", [128, L], BF16)
            t_brep = sbt("t_brep", [128, 2 * L], BF16)
            t_crep = sbt("t_crep", [128, L], BF16)
            ps_a = es.enter_context(nc.psum_tensor("p2_ps_a", [128, 2048], F32))
            ps_b = es.enter_context(nc.psum_tensor("p2_ps_b", [128, 2048], F32))
            sm = {}
            for nm in ["s_in", "s_bc", "s_xl0", "s_xl1", "s_br0", "s_br1",
                       "s_brc", "s_bs0", "s_bs1", "s_t", "s_s", "s_v", "s_y",
                       "s_dt"]:
                sm[nm] = es.enter_context(nc.semaphore("p2_" + nm))
            NPARAM = 1 + NDT * 2 + 1 + 1

            def V(name):
                return (2, name)

            def em_sync(eng):
                eng.wait_ge(s_wqs, 16 * NK)
                eng.wait_ge(s_r1s, 16 * NT_T)
                eng.wait_ge(s_ob2, 16 * NKT)
                eng.wait_ge(s_odt2, 32 * NT_T)
                eng.dma_start(out=t_msk[:], in_=mask).then_inc(sm["s_in"], 16)
                eng.dma_start(out=t_sp1[:, 0:DH], in_=dtwT).then_inc(sm["s_in"], 16)
                for j in range(NDT):
                    eng.dma_start(out=t_dtb[:, j:j + 1],
                                  in_=dtbias[j * 128:(j + 1) * 128, :]).then_inc(sm["s_in"], 16)
                    eng.dma_start(out=t_alog[:, j * N:(j + 1) * N],
                                  in_=Alog[j * 128:(j + 1) * 128, :]).then_inc(sm["s_in"], 16)
                eng.dma_start(out=t_sel[:], in_=selin).then_inc(sm["s_in"], 16)
                eng.wait_ge(s_cc, 3)
                eng.dma_start(out=t_dt[:, 0:TOK], in_=dtco[0]).then_inc(sm["s_dt"], 16)
                eng.dma_start(out=t_dt[:, TOK:L], in_=dtco[1]).then_inc(sm["s_dt"], 16)
                eng.wait_ge(s_cc, 2)
                for r in range(4):
                    which, th = r // 2, r % 2
                    if r >= 1:
                        eng.wait_ge(sm["s_v"], E(V(("bccast", r - 1))))
                    eng.dma_start(out=t_bc[:],
                                  in_=bcco[th, which * N:(which + 1) * N, :]).then_inc(sm["s_bc"], 16)
                eng.wait_ge(s_cc, 1)
                HT = TOK // 2
                for j in range(NDT):
                    if j >= 2:
                        eng.wait_ge(sm["s_v"], E(V(("dx", j - 2))))
                    if j >= 1:
                        eng.wait_ge(sm["s_v"], E(V(("xsel", j - 1))))
                    if j == 0:
                        pass
                    sl = (j % 2) * L
                    sx = sm["s_xl0"] if j % 2 == 0 else sm["s_xl1"]
                    for th in range(2):
                        for hh in range(2):
                            xsrc2 = xco_h0 if hh == 0 else xco_h1
                            if hh == 1 and j == 0 and th == 0:
                                eng.wait_ge(s_cc, 4)
                            co = sl + th * TOK + hh * HT
                            eng.dma_start(out=t_xln[:, co:co + HT],
                                          in_=xsrc2[th, j * 128:(j + 1) * 128, :]).then_inc(sx, 16)
                            eng.dma_start(out=t_xl2[:, th * TOK + hh * HT:th * TOK + hh * HT + HT],
                                          in_=xsrc2[th, DH + j * 128:DH + (j + 1) * 128, :]).then_inc(sx, 16)
                    for n in range(N):
                        sn = n % 2
                        if j == 0 and n == 0:
                            eng.wait_ge(sm["s_bs0"], 16 * (N * 2))
                            eng.wait_ge(sm["s_bs1"], 16 * (N * 2))
                        if (j, n) > (0, 1):
                            pj, pn = (j, n - 2) if n >= 2 else (j - 1, N - 2 + n)
                            eng.wait_ge(sm["s_v"], E(V(("u", pj, pn))))
                        eng.dma_start(out=t_brep[:, sn * L:(sn + 1) * L],
                                      in_=brep_d[n * 128:(n + 1) * 128, :]).then_inc(
                            sm["s_br0"] if sn == 0 else sm["s_br1"], 16)
                        if (j, n) != (0, 0):
                            pj, pn = (j, n - 1) if n >= 1 else (j - 1, N - 1)
                            eng.wait_ge(sm["s_v"], E(V(("ym", pj, pn))))
                        eng.dma_start(out=t_crep[:],
                                      in_=crep_d[n * 128:(n + 1) * 128, :]).then_inc(sm["s_brc"], 16)

            def em_gp(eng):
                eng.wait_ge(s_wqs, 16 * NK)
                eng.wait_ge(s_r1s, 16 * NT_T)
                eng.wait_ge(s_ob2, 16 * NKT)
                eng.wait_ge(s_odt2, 32 * NT_T)
                for which in range(2):
                    for n in range(N):
                        for half in range(2):
                            un = which * (N * 2) + n * 2 + half
                            eng.wait_ge(sm["s_v"], E(V(("evac", un))))
                            srcap = t_u[:, half * HL:(half + 1) * HL]
                            dst = (brep_d if which == 0 else crep_d)[
                                n * 128:(n + 1) * 128, half * HL:(half + 1) * HL]
                            eng.dma_start(out=dst, in_=srcap).then_inc(
                                sm["s_bs0"] if un % 2 == 0 else sm["s_bs1"], 16)
                for j in range(NDT):
                    eng.wait_ge(sm["s_v"], E(V(("ysumb", j))))
                    ydst = yci_a if j < NDT // 2 else yci_b
                    jj = j % (NDT // 2)
                    eng.dma_start(out=ydst[jj * 128:(jj + 1) * 128, :],
                                  in_=t_p1[:]).then_inc(sm["s_y"], 16)
                    if j == NDT // 2 - 1:
                        eng.wait_ge(sm["s_y"], 16 * (NDT // 2))
                        eng.collective_compute(
                            "AllGather", OP.bypass, ins=[yci_a],
                            outs=[yco_a], replica_groups=RG).then_inc(s_cc, 1)
                eng.wait_ge(sm["s_y"], 16 * NDT)
                eng.collective_compute("AllGather", OP.bypass, ins=[yci_b],
                                       outs=[yco_b], replica_groups=RG).then_inc(s_cc, 1)

            def em_tensor(eng):
                eng.wait_ge(s_wqs, 16 * NK)
                eng.wait_ge(s_r1s, 16 * NT_T)
                eng.wait_ge(s_ob2, 16 * NKT)
                eng.wait_ge(s_odt2, 32 * NT_T)
                eng.wait_ge(sm["s_in"], 16 * NPARAM)
                eng.wait_ge(sm["s_dt"], 32)
                eng.wait_ge(sm["s_v"], E(V("setup")))
                for which in range(2):
                    if which == 0:
                        eng.wait_ge(sm["s_v"], E(V("castB")))
                    else:
                        eng.wait_ge(sm["s_v"], E(V("castC")))
                    for n in range(N):
                        for half in range(2):
                            un = which * (N * 2) + n * 2 + half
                            pst = ps_a if un % 2 == 0 else ps_b
                            if un >= 2:
                                eng.wait_ge(sm["s_v"], E(V(("evac", un - 2))))
                            last = None
                            for blk in range(4):
                                last = eng.matmul(
                                    out=pst[:, blk * 512:(blk + 1) * 512],
                                    lhsT=t_sel[:, n * 128:(n + 1) * 128],
                                    rhs=t_bcb[:, half * HL + blk * 512:
                                              half * HL + (blk + 1) * 512],
                                    start=True, stop=True)
                            last.then_inc(sm["s_t"], 1)
                            tab.rec(sm["s_t"], V(("bld", un)))
                for j in range(NDT):
                    for half in range(2):
                        un = j * 2 + half
                        pst = ps_a if un % 2 == 0 else ps_b
                        if un < 2:
                            eng.wait_ge(sm["s_v"], E(V(("evac", N * 4 - 2 + un))))
                        else:
                            eng.wait_ge(sm["s_s"], E(V(("psfree", (un - 2) // 2, (un - 2) % 2))))
                        last = None
                        for blk in range(4):
                            last = eng.matmul(
                                out=pst[:, blk * 512:(blk + 1) * 512],
                                lhsT=t_dtwb[:, j * 128:(j + 1) * 128],
                                rhs=t_dt[:, half * HL + blk * 512:half * HL + (blk + 1) * 512],
                                start=True, stop=True)
                        last.then_inc(sm["s_t"], 1)
                        tab.rec(sm["s_t"], V(("dtproj", j, half)))

            def em_scalar(eng):
                eng.wait_ge(s_wqs, 16 * NK)
                eng.wait_ge(s_r1s, 16 * NT_T)
                eng.wait_ge(s_ob2, 16 * NKT)
                eng.wait_ge(s_odt2, 32 * NT_T)
                eng.wait_ge(sm["s_in"], 16 * NPARAM)
                eng.drain()
                eng.activation(out=t_a[:], in_=t_alog[:], func=AF.Exp).then_inc(sm["s_s"], 1)
                tab.rec(sm["s_s"], V("aexp"))
                for j in range(NDT):
                    for half in range(2):
                        un = j * 2 + half
                        pst = ps_a if un % 2 == 0 else ps_b
                        eng.wait_ge(sm["s_t"], E(V(("dtproj", j, half))))
                        if un >= 1:
                            eng.wait_ge(sm["s_v"], E(V(("delta", (un - 1) // 2, (un - 1) % 2))))
                        bias_ap = t_dtb[:, j:j + 1]
                        eng.drain()
                        eng.activation(out=t_sp1[:], in_=pst[:], func=AF.Abs,
                                       bias=bias_ap).then_inc(sm["s_s"], 1)
                        eng.activation(out=t_sp2[:], in_=pst[:], func=AF.Relu,
                                       bias=bias_ap).then_inc(sm["s_s"], 1)
                        tab.rec(sm["s_s"], V(("psfree", j, half)))
                        eng.drain()
                        eng.activation(out=t_sp3[:], in_=t_sp1[:], func=AF.Exp,
                                       scale=-1.0).then_inc(sm["s_s"], 1)
                        eng.drain()
                        eng.activation(out=t_sp4[:], in_=t_sp3[:], func=AF.Ln,
                                       bias=1.0).then_inc(sm["s_s"], 1)
                        tab.rec(sm["s_s"], V(("sp", j, half)))
                    for n in range(N):
                        if (j, n) != (0, 0):
                            pj, pn = (j, n - 1) if n >= 1 else (j - 1, N - 1)
                            eng.wait_ge(sm["s_v"], E(V(("tts", pj, pn))))
                        if n == 0 and j > 0:
                            eng.wait_ge(sm["s_v"], E(V(("ysumb", j - 1))))
                        eng.wait_ge(sm["s_v"], E(V(("dx", j))))
                        eng.drain()
                        eng.activation(out=t_da[:], in_=t_delta[:], func=AF.Exp,
                                       scale=t_an[:, j * N + n:j * N + n + 1]).then_inc(sm["s_s"], 1)
                        tab.rec(sm["s_s"], V(("da", j, n)))

            def em_vector(eng):
                eng.wait_ge(s_wqs, 16 * NK)
                eng.wait_ge(s_r1s, 16 * NT_T)
                eng.wait_ge(s_ob2, 16 * NKT)
                eng.wait_ge(s_odt2, 32 * NT_T)
                eng.wait_ge(sm["s_in"], 16 * NPARAM)
                eng.drain()
                eng.tensor_copy(out=t_dtwb[:], in_=t_sp1[:, 0:DH]).then_inc(sm["s_v"], 1)
                eng.wait_ge(sm["s_s"], E(V("aexp")))
                eng.tensor_scalar(out=t_an[:], in0=t_a[:], scalar1=-1.0, scalar2=None,
                                  op0=OP.mult).then_inc(sm["s_v"], 1)
                tab.rec(sm["s_v"], V("setup"))
                for r in range(2):
                    eng.wait_ge(sm["s_bc"], 16 * (r + 1))
                    eng.drain()
                    eng.tensor_copy(out=t_bcb[:, (r % 2) * TOK:(r % 2 + 1) * TOK],
                                    in_=t_bc[:]).then_inc(sm["s_v"], 1)
                    tab.rec(sm["s_v"], V(("bccast", r)))
                tab.rec(sm["s_v"], V("castB"))
                for which in range(2):
                    for n in range(N):
                        for half in range(2):
                            un = which * (N * 2) + n * 2 + half
                            pst = ps_a if un % 2 == 0 else ps_b
                            eng.wait_ge(sm["s_t"], E(V(("bld", un))))
                            if un >= 2:
                                eng.wait_ge(sm["s_bs0"] if un % 2 == 0 else sm["s_bs1"],
                                            16 * (un // 2))
                            eng.drain()
                            eng.tensor_copy(out=t_u[:, half * HL:(half + 1) * HL],
                                            in_=pst[:]).then_inc(sm["s_v"], 1)
                            tab.rec(sm["s_v"], V(("evac", un)))
                            if un == N * 2 - 1:
                                for r in range(2, 4):
                                    eng.wait_ge(sm["s_bc"], 16 * (r + 1))
                                    eng.drain()
                                    eng.tensor_copy(out=t_bcb[:, (r % 2) * TOK:(r % 2 + 1) * TOK],
                                                    in_=t_bc[:]).then_inc(sm["s_v"], 1)
                                    tab.rec(sm["s_v"], V(("bccast", r)))
                                tab.rec(sm["s_v"], V("castC"))
                for j in range(NDT):
                    sl = (j % 2) * L
                    sx = sm["s_xl0"] if j % 2 == 0 else sm["s_xl1"]
                    for half in range(2):
                        eng.wait_ge(sm["s_s"], E(V(("sp", j, half))))
                        if j > 0 and half == 0:
                            eng.wait_ge(sm["s_s"], E(V(("da", j - 1, N - 1))))
                        eng.drain()
                        eng.tensor_tensor(out=t_delta[:, half * HL:(half + 1) * HL],
                                          in0=t_sp2[:], in1=t_sp4[:], op=OP.add).then_inc(sm["s_v"], 1)
                        tab.rec(sm["s_v"], V(("delta", j, half)))
                    eng.wait_ge(sx, 16 * 8 * (j // 2 + 1))
                    eng.drain()
                    eng.tensor_scalar(out=t_xln[:, sl:sl + L], in0=t_xln[:, sl:sl + L],
                                      scalar1=t_msk[:, 1:2], scalar2=None,
                                      op0=OP.mult).then_inc(sm["s_v"], 1)
                    eng.drain()
                    eng.scalar_tensor_tensor(out=t_xln[:, sl:sl + L],
                                             in0=t_xl2[:],
                                             scalar=t_msk[:, 0:1],
                                             in1=t_xln[:, sl:sl + L],
                                             op0=OP.mult, op1=OP.add).then_inc(sm["s_v"], 1)
                    tab.rec(sm["s_v"], V(("xsel", j)))
                    eng.drain()
                    eng.tensor_tensor(out=t_dx[:], in0=t_delta[:],
                                      in1=t_xln[:, sl:sl + L],
                                      op=OP.mult).then_inc(sm["s_v"], 1)
                    tab.rec(sm["s_v"], V(("dx", j)))
                    eng.drain()
                    for n in range(N):
                        sn = n % 2
                        idx = j * N + n
                        if j == 0 and n == 0:
                            eng.wait_ge(sm["s_bs0"], 16 * (N * 2))
                            eng.wait_ge(sm["s_bs1"], 16 * (N * 2))
                        eng.wait_ge(sm["s_br0"] if sn == 0 else sm["s_br1"],
                                    16 * (idx // 2 + 1))
                        eng.drain()
                        eng.tensor_tensor(out=t_u[:], in0=t_dx[:],
                                          in1=t_brep[:, sn * L:(sn + 1) * L],
                                          op=OP.mult).then_inc(sm["s_v"], 1)
                        tab.rec(sm["s_v"], V(("u", j, n)))
                        eng.wait_ge(sm["s_s"], E(V(("da", j, n))))
                        eng.drain()
                        eng.tensor_tensor_scan(out=t_h[:], data0=t_da[:], data1=t_u[:],
                                               initial=0.0, op0=OP.mult,
                                               op1=OP.add).then_inc(sm["s_v"], 1)
                        tab.rec(sm["s_v"], V(("tts", j, n)))
                        ym_dst = t_p1 if n % 2 == 0 else t_yp
                        if n == 0 and j > 0:
                            eng.wait_ge(sm["s_y"], 16 * j)
                        eng.wait_ge(sm["s_brc"], 16 * (idx + 1))
                        eng.drain()
                        eng.tensor_tensor(out=ym_dst[:], in0=t_h[:],
                                          in1=t_crep[:],
                                          op=OP.mult).then_inc(sm["s_v"], 1)
                        tab.rec(sm["s_v"], V(("ym", j, n)))
                        if n % 2 == 1:
                            k = n + 1
                            chain = 0
                            kk = k
                            while kk % 2 == 0 and kk > 1:
                                chain += 1
                                kk //= 2
                            bufs = [t_mg, t_u]
                            cur = None
                            lvslots = [t_p2, t_p3, t_p4]
                            for c in range(chain):
                                src0 = t_p1 if c == 0 else cur
                                src1 = t_yp if c == 0 else lvslots[c - 1]
                                if c == chain - 1:
                                    if k == N:
                                        dst = t_da
                                    else:
                                        dst = lvslots[c]
                                else:
                                    dst = bufs[c % 2]
                                eng.drain()
                                eng.tensor_tensor(out=dst[:], in0=src0[:], in1=src1[:],
                                                  op=OP.add).then_inc(sm["s_v"], 1)
                                cur = dst
                    tab.rec(sm["s_v"], V(("ysum", j)))
                    eng.drain()
                    eng.tensor_copy(out=t_p1[:], in_=t_da[:]).then_inc(sm["s_v"], 1)
                    tab.rec(sm["s_v"], V(("ysumb", j)))

            return dict(sync=em_sync, gp=em_gp, tensor=em_tensor,
                        scalar=em_scalar, vector=em_vector)

        p1es = ExitStack()
        p1 = token_phase({"phase": 1}, p1es)
        for _ in range(2):
            tab.cnt.clear()
            p1["vector"](dry); p1["scalar"](dry); p1["tensor"](dry)
            p1["sync"](dry); p1["gp"](dry)
        tab.dry = False

        @block.sync
        def _(eng):
            p1["sync"](eng)

        @block.gpsimd
        def _(eng):
            p1["gp"](eng)

        @block.tensor
        def _(eng):
            p1["tensor"](eng)

        @block.scalar
        def _(eng):
            p1["scalar"](eng)

        @block.vector
        def _(eng):
            p1["vector"](eng)

        p1es.close()

        p2es = ExitStack()
        tab.dry = True
        p2 = scan_phase(p2es)
        for _ in range(2):
            tab.cnt.clear()
            p2["vector"](dry); p2["scalar"](dry); p2["tensor"](dry)
            p2["sync"](dry); p2["gp"](dry)
        tab.dry = False

        @block.sync
        def _(eng):
            p2["sync"](eng)

        @block.gpsimd
        def _(eng):
            p2["gp"](eng)

        @block.tensor
        def _(eng):
            p2["tensor"](eng)

        @block.scalar
        def _(eng):
            p2["scalar"](eng)

        @block.vector
        def _(eng):
            p2["vector"](eng)

        p2es.close()

        p3es = ExitStack()
        tab.dry = True
        p3 = token_phase({"phase": 3}, p3es)
        for _ in range(2):
            tab.cnt.clear()
            p3["vector"](dry); p3["scalar"](dry); p3["tensor"](dry)
            p3["sync"](dry); p3["gp"](dry)
        tab.dry = False

        @block.sync
        def _(eng):
            p3["sync"](eng)

        @block.gpsimd
        def _(eng):
            p3["gp"](eng)

        @block.tensor
        def _(eng):
            p3["tensor"](eng)

        @block.scalar
        def _(eng):
            p3["scalar"](eng)

        @block.vector
        def _(eng):
            p3["vector"](eng)

        p3es.close()
    return nc


def kernel_fused(L0_attn_x, L0_attn_A_log, L0_attn_D_param, L0_attn_x_proj_w,
                 L0_attn_dt_proj_w, L0_attn_dt_proj_b, L0_ffn_gamma, L0_ffn_W):
    f32 = np.float32
    x = np.ascontiguousarray(np.asarray(L0_attn_x, dtype=f32))
    W = np.ascontiguousarray(np.asarray(L0_ffn_W, dtype=f32))
    gamma = np.asarray(L0_ffn_gamma, dtype=f32)
    A_log = np.ascontiguousarray(np.asarray(L0_attn_A_log, dtype=f32))
    D_par = np.asarray(L0_attn_D_param, dtype=f32)
    xpw = np.asarray(L0_attn_x_proj_w, dtype=f32)
    dtw = np.asarray(L0_attn_dt_proj_w, dtype=f32)
    dtb = np.asarray(L0_attn_dt_proj_b, dtype=f32)
    ident_b = np.eye(128, dtype=f32).astype(bf16)
    ident_f = np.eye(128, dtype=f32)
    selmat = np.repeat(np.eye(N, dtype=f32), 128, axis=1).astype(bf16)
    xpwT = np.ascontiguousarray(xpw.T)
    nc = build_fused()
    ims = []
    for c in range(8):
        b, h = c >> 1, c & 1
        rs = slice(h * DH, (h + 1) * DH)
        msk = np.zeros((128, 2), dtype=f32)
        msk[:, 0] = float(h)
        msk[:, 1] = float(1 - h)
        ims.append({
            "xin": np.ascontiguousarray(x[b, h * TOK:(h + 1) * TOK, :]),
            "Win": W, "gam": gamma[None, :], "ident": ident_b, "identf": ident_f,
            "xpwT": xpwT,
            "dtwT": np.ascontiguousarray(dtw[rs, :].T),
            "dtbias": np.ascontiguousarray(dtb[rs, None]),
            "Alog": np.ascontiguousarray(A_log[rs, :]),
            "sel": selmat, "dpar": D_par[None, :], "mask": msk,
        })
    res = run_bass_kernel_spmd(nc, ims, core_ids=list(range(8)))
    out = np.empty((B, L, D), dtype=f32)
    for c in range(8):
        b, h = c >> 1, c & 1
        out[b, h * TOK:(h + 1) * TOK, :] = res.results[c]["outo"]
    return out


kernel = kernel_fused


# version 17
# speedup vs baseline: 1.0499x; 1.0499x over previous
from contextlib import ExitStack
import numpy as np
import ml_dtypes

import concourse.bass as bass
import concourse.mybir as mybir
from concourse.bass_utils import run_bass_kernel_spmd
from concourse.alu_op_type import AluOpType as OP
from bass_rust import ReduceOp
from concourse import library_config

F32 = mybir.dt.float32
BF16 = mybir.dt.bfloat16
AF = mybir.ActivationFunctionType
AX = mybir.AxisListType
bf16 = ml_dtypes.bfloat16

B, L, D, N, RT = 4, 4096, 2048, 16, 128
TOK = L // 2
DH = D // 2
NDT = DH // 128
HL = L // 2
MAGIC = float(3 * 2 ** 22)



def build_scan_prog():
    nc = bass.Bass("TRN2", target_bir_lowering=False, debug=False)
    xln = nc.dram_tensor("xln", [DH, L], BF16, kind="ExternalInput").ap()
    dtin = nc.dram_tensor("dt", [RT, L], BF16, kind="ExternalInput").ap()
    Bm = nc.dram_tensor("Bm", [N, L], F32, kind="ExternalInput").ap()
    Cm = nc.dram_tensor("Cm", [N, L], F32, kind="ExternalInput").ap()
    dtwT = nc.dram_tensor("dtwT", [RT, DH], F32, kind="ExternalInput").ap()
    dtbias = nc.dram_tensor("dtbias", [DH, 1], F32, kind="ExternalInput").ap()
    Alog = nc.dram_tensor("Alog", [DH, N], F32, kind="ExternalInput").ap()
    selin = nc.dram_tensor("sel", [N, N * 128], BF16, kind="ExternalInput").ap()
    yout = nc.dram_tensor("yout", [DH, L], F32, kind="ExternalOutput").ap()
    brep_d = nc.dram_tensor("brep_d", [N * 128, L], BF16, kind="Internal").ap()
    crep_d = nc.dram_tensor("crep_d", [N * 128, L], BF16, kind="Internal").ap()

    with ExitStack() as es:
        sbt = lambda name, shape, dt: es.enter_context(nc.sbuf_tensor(name, shape, dt))
        t_dtwb = sbt("t_dtwb", [RT, DH], BF16)
        t_dtb = sbt("t_dtb", [128, NDT], F32)
        t_alog = sbt("t_alog", [128, NDT * N], F32)
        t_a = sbt("t_a", [128, NDT * N], F32)
        t_an = sbt("t_an", [128, NDT * N], F32)
        t_bc = sbt("t_bc", [N, L], F32)
        t_bcb = sbt("t_bcb", [N, L], BF16)
        t_sel = sbt("t_sel", [N, N * 128], BF16)
        t_dt = sbt("t_dt", [RT, L], BF16)
        t_xln = sbt("t_xln", [128, 2 * L], BF16)
        t_delta = sbt("t_delta", [128, L], BF16)
        t_dx = sbt("t_dx", [128, L], BF16)
        t_sp1 = sbt("t_sp1", [128, HL], F32)
        t_sp2 = sbt("t_sp2", [128, HL], F32)
        t_sp3 = sbt("t_sp3", [128, HL], F32)
        t_sp4 = sbt("t_sp4", [128, HL], F32)
        t_da = sbt("t_da", [128, L], F32)
        t_u = sbt("t_u", [128, L], BF16)
        t_h = sbt("t_h", [128, L], BF16)
        t_yp = sbt("t_yp", [128, L], BF16)
        t_mg = sbt("t_mg", [128, L], BF16)
        t_p1 = sbt("t_p1", [128, L], BF16)
        t_p2 = sbt("t_p2", [128, L], BF16)
        t_p3 = sbt("t_p3", [128, L], BF16)
        t_p4 = sbt("t_p4", [128, L], BF16)
        t_brep = sbt("t_brep", [128, 2 * L], BF16)
        t_crep = sbt("t_crep", [128, L], BF16)
        ps_a = es.enter_context(nc.psum_tensor("ps_a", [128, 2048], F32))
        ps_b = es.enter_context(nc.psum_tensor("ps_b", [128, 2048], F32))

        s_in = es.enter_context(nc.semaphore("s_in"))
        s_bc = es.enter_context(nc.semaphore("s_bc"))
        s_xl0 = es.enter_context(nc.semaphore("s_xl0"))
        s_xl1 = es.enter_context(nc.semaphore("s_xl1"))
        s_br0 = es.enter_context(nc.semaphore("s_br0"))
        s_br1 = es.enter_context(nc.semaphore("s_br1"))
        s_brc = es.enter_context(nc.semaphore("s_brc"))
        s_bs0 = es.enter_context(nc.semaphore("s_bs0"))
        s_bs1 = es.enter_context(nc.semaphore("s_bs1"))
        s_t = es.enter_context(nc.semaphore("s_t"))
        s_s = es.enter_context(nc.semaphore("s_s"))
        s_v = es.enter_context(nc.semaphore("s_v"))
        s_y = es.enter_context(nc.semaphore("s_y"))

        ev_v = {}
        v = 0
        v += 1
        v += 1
        v += 1
        ev_v["castB"] = v
        ev_v["setup"] = v
        for un in range(N * 2):
            v += 1
            ev_v[("evac", un)] = v
            if un == N * 2 - 1:
                v += 1
                ev_v["castC"] = v
        for un in range(N * 2, N * 4):
            v += 1
            ev_v[("evac", un)] = v
        for j in range(NDT):
            for half in range(2):
                v += 1
                ev_v[("delta", j, half)] = v
            v += 1
            ev_v[("dx", j)] = v
            for n in range(N):
                v += 1
                ev_v[("u", j, n)] = v
                v += 1
                ev_v[("tts", j, n)] = v
                v += 1
                ev_v[("ym", j, n)] = v
                nm = bin(n + 1).count("1")
                extra = 0
                if n % 2 == 1:
                    k = n + 1
                    extra = 0
                    kk = k
                    while kk % 2 == 0 and kk > 1:
                        extra += 1
                        kk //= 2
                v += extra
                ev_v[("unit", j, n)] = v
            ev_v[("ysum", j)] = v
        V_TOTAL = v

        ev_s = {}
        s = 0
        s += 2
        s -= 1
        ev_s["setup"] = s
        for j in range(NDT):
            for half in range(2):
                s += 2
                ev_s[("psfree", j, half)] = s
                s += 2
                ev_s[("sp", j, half)] = s
            for n in range(N):
                s += 1
                ev_s[("da", j, n)] = s
        S_TOTAL = s

        ev_t = {}
        t = 0
        for un in range(N * 2 * 2):
            t += 1
            ev_t[("bld", un)] = t
        for j in range(NDT):
            for half in range(2):
                t += 1
                ev_t[("dtproj", j, half)] = t
        T_TOTAL = t

        NPARAM = 1 + NDT * 2 + 1 + 1
        ev_bs = {}
        bs = 0
        for un in range(N * 2 * 2):
            bs += 16
            ev_bs[("bst", un)] = bs
        ev_br = {}
        br = 0
        for j in range(NDT):
            for n in range(N):
                br += 32
                ev_br[("brl", j, n)] = br
        ev_xl = {}
        for j in range(NDT):
            ev_xl[j] = 16 * (j + 1)

        block = es.enter_context(nc.Block())

        @block.sync
        def _(eng):
            eng.dma_start(out=t_sp1[:, 0:DH], in_=dtwT).then_inc(s_in, 16)
            for j in range(NDT):
                eng.dma_start(out=t_dtb[:, j:j + 1],
                              in_=dtbias[j * 128:(j + 1) * 128, :]).then_inc(s_in, 16)
                eng.dma_start(out=t_alog[:, j * N:(j + 1) * N],
                              in_=Alog[j * 128:(j + 1) * 128, :]).then_inc(s_in, 16)
            eng.dma_start(out=t_dt[:], in_=dtin).then_inc(s_in, 16)
            eng.dma_start(out=t_sel[:], in_=selin).then_inc(s_in, 16)
            eng.dma_start(out=t_bc[:], in_=Bm).then_inc(s_bc, 16)
            eng.wait_ge(s_v, ev_v["castB"])
            eng.dma_start(out=t_bc[:], in_=Cm).then_inc(s_bc, 16)
            for j in range(NDT):
                if j >= 2:
                    eng.wait_ge(s_v, ev_v[("dx", j - 2)])
                eng.dma_start(out=t_xln[:, (j % 2) * L:(j % 2 + 1) * L],
                              in_=xln[j * 128:(j + 1) * 128, :]).then_inc(
                    s_xl0 if j % 2 == 0 else s_xl1, 16)

        @block.gpsimd
        def _(eng):
            for which in range(2):
                for n in range(N):
                    for half in range(2):
                        un = which * (N * 2) + n * 2 + half
                        eng.wait_ge(s_v, ev_v[("evac", un)])
                        srcap = t_u[:, half * HL:(half + 1) * HL]
                        dst = (brep_d if which == 0 else crep_d)[
                            n * 128:(n + 1) * 128, half * HL:(half + 1) * HL]
                        eng.dma_start(out=dst, in_=srcap).then_inc(
                            s_bs0 if un % 2 == 0 else s_bs1, 16)
            for j in range(NDT):
                for n in range(N):
                    sl = n % 2
                    idx = j * N + n
                    if j == 0 and n == 0:
                        eng.wait_ge(s_bs0, 16 * (N * 2))
                        eng.wait_ge(s_bs1, 16 * (N * 2))
                    if (j, n) > (0, 1):
                        pj, pn = (j, n - 2) if n >= 2 else (j - 1, N - 2 + n)
                        eng.wait_ge(s_v, ev_v[("u", pj, pn)])
                    eng.dma_start(out=t_brep[:, sl * L:(sl + 1) * L],
                                  in_=brep_d[n * 128:(n + 1) * 128, :]).then_inc(
                        s_br0 if sl == 0 else s_br1, 16)
                    if (j, n) != (0, 0):
                        pj, pn = (j, n - 1) if n >= 1 else (j - 1, N - 1)
                        eng.wait_ge(s_v, ev_v[("ym", pj, pn)])
                    eng.dma_start(out=t_crep[:],
                                  in_=crep_d[n * 128:(n + 1) * 128, :]).then_inc(s_brc, 16)
                eng.wait_ge(s_v, ev_v[("ysum", j)])
                eng.dma_start(out=yout[j * 128:(j + 1) * 128, :],
                              in_=t_da[:]).then_inc(s_y, 16)
            eng.wait_ge(s_y, 16 * NDT)

        @block.tensor
        def _(eng):
            eng.wait_ge(s_in, 16 * NPARAM)
            eng.wait_ge(s_v, ev_v["setup"])
            for which in range(2):
                if which == 1:
                    eng.wait_ge(s_v, ev_v["castC"])
                for n in range(N):
                    for half in range(2):
                        un = which * (N * 2) + n * 2 + half
                        pst = ps_a if un % 2 == 0 else ps_b
                        if un >= 2:
                            eng.wait_ge(s_v, ev_v[("evac", un - 2)])
                        last = None
                        for blk in range(4):
                            last = eng.matmul(
                                out=pst[:, blk * 512:(blk + 1) * 512],
                                lhsT=t_sel[:, n * 128:(n + 1) * 128],
                                rhs=t_bcb[:, half * HL + blk * 512:
                                          half * HL + (blk + 1) * 512],
                                start=True, stop=True)
                        last.then_inc(s_t, 1)
            for j in range(NDT):
                for half in range(2):
                    un = j * 2 + half
                    pst = ps_a if un % 2 == 0 else ps_b
                    if un < 2:
                        eng.wait_ge(s_v, ev_v[("evac", N * 4 - 2 + un)])
                    else:
                        eng.wait_ge(s_s, ev_s[("psfree", (un - 2) // 2, (un - 2) % 2)])
                    last = None
                    for blk in range(4):
                        last = eng.matmul(
                            out=pst[:, blk * 512:(blk + 1) * 512],
                            lhsT=t_dtwb[:, j * 128:(j + 1) * 128],
                            rhs=t_dt[:, half * HL + blk * 512:half * HL + (blk + 1) * 512],
                            start=True, stop=True)
                    last.then_inc(s_t, 1)

        @block.scalar
        def _(eng):
            eng.wait_ge(s_in, 16 * NPARAM)
            eng.activation(out=t_a[:], in_=t_alog[:], func=AF.Exp).then_inc(s_s, 1)
            for j in range(NDT):
                for half in range(2):
                    un = j * 2 + half
                    pst = ps_a if un % 2 == 0 else ps_b
                    eng.wait_ge(s_t, ev_t[("dtproj", j, half)])
                    if un >= 1:
                        eng.wait_ge(s_v, ev_v[("delta", (un - 1) // 2, (un - 1) % 2)])
                    bias_ap = t_dtb[:, j:j + 1]
                    eng.activation(out=t_sp1[:], in_=pst[:], func=AF.Abs,
                                   bias=bias_ap).then_inc(s_s, 1)
                    eng.activation(out=t_sp2[:], in_=pst[:], func=AF.Relu,
                                   bias=bias_ap).then_inc(s_s, 1)
                    eng.drain()
                    eng.activation(out=t_sp3[:], in_=t_sp1[:], func=AF.Exp,
                                   scale=-1.0).then_inc(s_s, 1)
                    eng.drain()
                    eng.activation(out=t_sp4[:], in_=t_sp3[:], func=AF.Ln,
                                   bias=1.0).then_inc(s_s, 1)
                for n in range(N):
                    if (j, n) != (0, 0):
                        pj, pn = (j, n - 1) if n >= 1 else (j - 1, N - 1)
                        eng.wait_ge(s_v, ev_v[("tts", pj, pn)])
                    if n == 0 and j > 0:
                        eng.wait_ge(s_v, ev_v[("ysum", j - 1)])
                        eng.wait_ge(s_y, 16 * j)
                    eng.wait_ge(s_v, ev_v[("dx", j)])
                    eng.activation(out=t_da[:], in_=t_delta[:], func=AF.Exp,
                                   scale=t_an[:, j * N + n:j * N + n + 1]).then_inc(s_s, 1)

        @block.vector
        def _(eng):
            eng.wait_ge(s_in, 16 * NPARAM)
            eng.tensor_copy(out=t_dtwb[:], in_=t_sp1[:, 0:DH]).then_inc(s_v, 1)
            eng.wait_ge(s_s, 1)
            eng.tensor_scalar(out=t_an[:], in0=t_a[:], scalar1=-1.0, scalar2=None,
                              op0=OP.mult).then_inc(s_v, 1)
            eng.wait_ge(s_bc, 16)
            eng.tensor_copy(out=t_bcb[:], in_=t_bc[:]).then_inc(s_v, 1)
            for which in range(2):
                for n in range(N):
                    for half in range(2):
                        un = which * (N * 2) + n * 2 + half
                        pst = ps_a if un % 2 == 0 else ps_b
                        eng.wait_ge(s_t, ev_t[("bld", un)])
                        if un >= 2:
                            eng.wait_ge(s_bs0 if un % 2 == 0 else s_bs1,
                                        16 * (un // 2))
                        eng.tensor_copy(out=t_u[:, half * HL:(half + 1) * HL],
                                        in_=pst[:]).then_inc(s_v, 1)
                        if un == N * 2 - 1:
                            eng.wait_ge(s_bc, 32)
                            eng.tensor_copy(out=t_bcb[:], in_=t_bc[:]).then_inc(s_v, 1)
            for j in range(NDT):
                for half in range(2):
                    eng.wait_ge(s_s, ev_s[("sp", j, half)])
                    if j > 0 and half == 0:
                        eng.wait_ge(s_s, ev_s[("da", j - 1, N - 1)])
                    eng.tensor_tensor(out=t_delta[:, half * HL:(half + 1) * HL],
                                      in0=t_sp2[:], in1=t_sp4[:], op=OP.add).then_inc(s_v, 1)
                eng.wait_ge(s_xl0 if j % 2 == 0 else s_xl1, 16 * (j // 2 + 1))
                eng.drain()
                eng.tensor_tensor(out=t_dx[:], in0=t_delta[:],
                                  in1=t_xln[:, (j % 2) * L:(j % 2 + 1) * L],
                                  op=OP.mult).then_inc(s_v, 1)
                eng.drain()
                for n in range(N):
                    sl = n % 2
                    idx = j * N + n
                    if j == 0 and n == 0:
                        eng.wait_ge(s_bs0, 16 * (N * 2))
                        eng.wait_ge(s_bs1, 16 * (N * 2))
                    eng.wait_ge(s_br0 if sl == 0 else s_br1, 16 * (idx // 2 + 1))
                    eng.drain()
                    eng.tensor_tensor(out=t_u[:], in0=t_dx[:],
                                      in1=t_brep[:, sl * L:(sl + 1) * L],
                                      op=OP.mult).then_inc(s_v, 1)
                    eng.wait_ge(s_s, ev_s[("da", j, n)])
                    eng.drain()
                    eng.tensor_tensor_scan(out=t_h[:], data0=t_da[:], data1=t_u[:],
                                           initial=0.0, op0=OP.mult,
                                           op1=OP.add).then_inc(s_v, 1)
                    ym_dst = t_p1 if n % 2 == 0 else t_yp
                    eng.wait_ge(s_brc, 16 * (idx + 1))
                    eng.drain()
                    eng.tensor_tensor(out=ym_dst[:], in0=t_h[:],
                                      in1=t_crep[:],
                                      op=OP.mult).then_inc(s_v, 1)
                    if n % 2 == 1:
                        k = n + 1
                        chain = 0
                        kk = k
                        while kk % 2 == 0 and kk > 1:
                            chain += 1
                            kk //= 2
                        bufs = [t_mg, t_u]
                        cur = None
                        lvslots = [t_p2, t_p3, t_p4]
                        for c in range(chain):
                            src0 = t_p1 if c == 0 else cur
                            src1 = t_yp if c == 0 else lvslots[c - 1]
                            if c == chain - 1:
                                if k == N:
                                    if j > 0:
                                        eng.wait_ge(s_y, 16 * j)
                                    dst = t_da
                                else:
                                    dst = lvslots[c]
                            else:
                                dst = bufs[c % 2]
                            eng.drain()
                            eng.tensor_tensor(out=dst[:], in0=src0[:], in1=src1[:],
                                              op=OP.add).then_inc(s_v, 1)
                            cur = dst

    return nc


def _scan_in_maps(inputs):
    raise NotImplementedError


def kernel(**inputs):
    raise NotImplementedError



class _DryInst:
    def __init__(self, cnt):
        self.cnt = cnt

    def then_inc(self, sem, k):
        self.cnt[id(sem)] = self.cnt.get(id(sem), 0) + k
        return self


class _DryEng:
    def __init__(self, cnt):
        self._cnt = cnt

    def __getattr__(self, name):
        if name in ("wait_ge", "drain"):
            return lambda *a, **k: None
        return lambda *a, **k: _DryInst(self._cnt)


class EvTab:

    def __init__(self):
        self.cnt = {}
        self.ev = {}
        self.dry = True

    def rec(self, sem, name):
        val = self.cnt.get(id(sem), 0)
        if self.dry:
            self.ev[name] = val
        return self.ev[name]

    def get(self, name):
        return self.ev[name]


def build_token_prog(phase):
    NT = TOK // 128
    NK = D // 128
    EPS_LN = 1e-5
    EPS_RMS = 1e-6
    EPS_Q = 1e-5
    nc = bass.Bass("TRN2", target_bir_lowering=False, debug=False)
    Win = nc.dram_tensor("Win", [D, D], F32, kind="ExternalInput").ap()
    gam = nc.dram_tensor("gam", [1, D], F32, kind="ExternalInput").ap()
    ident = nc.dram_tensor("ident", [128, 128], BF16, kind="ExternalInput").ap()
    xin = nc.dram_tensor("xin", [TOK, D], F32, kind="ExternalInput").ap()
    if phase == 1:
        r1o = nc.dram_tensor("r1o", [TOK, D], F32, kind="ExternalOutput").ap()
        xlnT = nc.dram_tensor("xlnT", [D, TOK], BF16, kind="ExternalOutput").ap()
        dtv = nc.dram_tensor("dtv", [RT, TOK], BF16, kind="ExternalOutput").ap()
        bcv = nc.dram_tensor("bcv", [2 * N, TOK], F32, kind="ExternalOutput").ap()
        xpwTin = nc.dram_tensor("xpwT", [D, RT + 2 * N], F32, kind="ExternalInput").ap()
    else:
        identf = nc.dram_tensor("identf", [128, 128], F32, kind="ExternalInput").ap()
        yTin = nc.dram_tensor("yTin", [D, TOK], F32, kind="ExternalInput").ap()
        dpar = nc.dram_tensor("dpar", [1, D], F32, kind="ExternalInput").ap()
        outo = nc.dram_tensor("outo", [TOK, D], F32, kind="ExternalOutput").ap()

    with ExitStack() as es:
        sbt = lambda name, shape, dt: es.enter_context(nc.sbuf_tensor(name, shape, dt))
        t_w = sbt("t_w", [128, 2 * D], F32)
        t_wq = sbt("t_wq", [128, NK * D], BF16)
        t_x = sbt("t_x", [128, 2 * D], F32)
        t_xln = sbt("t_xln", [128, D], F32)
        t_scr = sbt("t_scr", [128, D], F32)
        t_q = sbt("t_q", [128, D], F32)
        t_xq = sbt("t_xq", [128, D], BF16)
        t_xqT = sbt("t_xqT", [128, NK * 128], BF16)
        t_gam = sbt("t_gam", [128, D], F32)
        t_r1 = sbt("t_r1", [128, D], F32)
        t_st = sbt("t_st", [128, 24], F32)
        t_cst = sbt("t_cst", [128, 4], F32)
        t_ones = sbt("t_ones", [128, 128], F32)
        t_id = sbt("t_id", [128, 128], BF16)
        if phase == 1:
            t_xlb = sbt("t_xlb", [128, D], BF16)
            t_xt = sbt("t_xt", [128, NK * 128], BF16)
            t_xpw = sbt("t_xpw", [128, NK * (RT + 2 * N)], BF16)
            t_xpwf = sbt("t_xpwf", [128, RT + 2 * N], F32)
            t_pd = sbt("t_pd", [128, 128], BF16)
            t_pb = sbt("t_pb", [32, 128], F32)
        else:
            t_idf = sbt("t_idf", [128, 128], F32)
            t_y = sbt("t_y", [128, 2 * D], F32)
            t_yt = sbt("t_yt", [128, D], F32)
            t_r2 = sbt("t_r2", [128, D], F32)
            t_dr = sbt("t_dr", [128, D], F32)
        ps_m = es.enter_context(nc.psum_tensor("ps_m", [128, D], F32))
        ps_t1 = es.enter_context(nc.psum_tensor("ps_t1", [128, 128], BF16))
        ps_t2 = es.enter_context(nc.psum_tensor("ps_t2", [128, 128], BF16))
        if phase == 1:
            ps_p1 = es.enter_context(nc.psum_tensor("ps_p1", [128, 128], F32))
            ps_p2 = es.enter_context(nc.psum_tensor("ps_p2", [32, 128], F32))
        else:
            ps_f1 = es.enter_context(nc.psum_tensor("ps_f1", [128, 128], F32))
            ps_f2 = es.enter_context(nc.psum_tensor("ps_f2", [128, 128], F32))

        sems = {}
        for nm in ["s_wa0", "s_wa1", "s_wb0", "s_wb1", "s_x0", "s_x1", "s_g",
                   "s_xp", "s_t", "s_s", "s_v", "s_g2", "s_or", "s_odt", "s_ob",
                   "s_y0", "s_y1"]:
            sems[nm] = es.enter_context(nc.semaphore(nm))
        (s_wa0, s_wa1, s_wb0, s_wb1, s_x0, s_x1, s_g, s_xp, s_t, s_s, s_v,
         s_g2, s_or, s_odt, s_ob, s_y0, s_y1) = (
            sems[k] for k in ["s_wa0", "s_wa1", "s_wb0", "s_wb1", "s_x0", "s_x1",
                              "s_g", "s_xp", "s_t", "s_s", "s_v", "s_g2", "s_or",
                              "s_odt", "s_ob", "s_y0", "s_y1"])

        tab = EvTab()

        def ix(sem):
            return tab.cnt.get(id(sem), 0)

        def em_sync(eng, E):
            eng.dma_start(out=t_gam[0:1, :], in_=gam).then_inc(s_g, 16)
            eng.dma_start(out=t_id[:], in_=ident).then_inc(s_g, 16)
            if phase == 3:
                eng.dma_start(out=t_dr[0:1, :], in_=dpar).then_inc(s_g, 16)
                eng.dma_start(out=t_idf[:], in_=identf).then_inc(s_g, 16)
            if phase == 1:
                for k in range(NK):
                    if k >= 1:
                        eng.wait_ge(s_v, E(("xpc", k - 1)))
                    eng.dma_start(out=t_xpwf[:],
                                  in_=xpwTin[k * 128:(k + 1) * 128, :]).then_inc(s_xp, 16)
            for k in range(NK):
                if k >= 2:
                    eng.wait_ge(s_v, E(("wstat", k - 2)))
                eng.dma_start(out=t_w[:, (k % 2) * D:(k % 2 + 1) * D],
                              in_=Win[k * 128:(k + 1) * 128, :]).then_inc(
                    s_wa0 if k % 2 == 0 else s_wa1, 16)
            for k in range(NK):
                eng.wait_ge(s_v, E(("wstat", NK - 2 + k)) if k < 2 else E(("wq", k - 2)))
                eng.dma_start(out=t_w[:, (k % 2) * D:(k % 2 + 1) * D],
                              in_=Win[k * 128:(k + 1) * 128, :]).then_inc(
                    s_wb0 if k % 2 == 0 else s_wb1, 16)
            for i in range(NT):
                if i >= 2:
                    eng.wait_ge(s_v, E(("ep", i - 2)))
                eng.dma_start(out=t_x[:, (i % 2) * D:(i % 2 + 1) * D],
                              in_=xin[i * 128:(i + 1) * 128, :]).then_inc(
                    s_x0 if i % 2 == 0 else s_x1, 16)

        def em_gp_full(eng, E):
            if phase == 3:
                for i in range(NT + 1):
                    if i < NT:
                        if i >= 2:
                            eng.wait_ge(s_v, E(("r2", i - 2)))
                        for b in range(NK):
                            eng.dma_start(
                                out=t_y[:, (i % 2) * D + b * 128:(i % 2) * D + (b + 1) * 128],
                                in_=yTin[b * 128:(b + 1) * 128, i * 128:(i + 1) * 128]
                            ).then_inc(s_y0 if i % 2 == 0 else s_y1, 16)
                    if i >= 1:
                        eng.wait_ge(s_v, E(("ep", i - 1)))
                        eng.dma_start(out=outo[(i - 1) * 128:i * 128, :],
                                      in_=t_r1[:]).then_inc(s_or, 16)
            else:
                for i in range(NT):
                    eng.wait_ge(s_v, E(("ep", i)))
                    eng.dma_start(out=r1o[i * 128:(i + 1) * 128, :],
                                  in_=t_r1[:]).then_inc(s_or, 16)
                    for b in range(NK):
                        eng.wait_ge(s_v, E(("xtev", i, b)))
                        eng.dma_start(
                            out=xlnT[b * 128:(b + 1) * 128, i * 128:(i + 1) * 128],
                            in_=t_xt[:, b * 128:(b + 1) * 128]).then_inc(s_ob, 16)
                    eng.wait_ge(s_v, E(("xp", i)))
                    eng.dma_start(out=dtv[:, i * 128:(i + 1) * 128],
                                  in_=t_pd[:]).then_inc(s_odt, 16)
                    eng.dma_start(out=bcv[:, i * 128:(i + 1) * 128],
                                  in_=t_pb[:]).then_inc(s_odt, 16)
            eng.wait_ge(s_or, 16 * NT)
            if phase == 1:
                eng.wait_ge(s_ob, 16 * NT * NK)
                eng.wait_ge(s_odt, 32 * NT)

        def em_tensor(eng, E):
            ps_rep = ps_p1 if phase == 1 else ps_f1
            eng.wait_ge(s_g, 16 * (4 if phase == 3 else 2))
            eng.wait_ge(s_v, E("ones"))
            nrep = NK * (2 if phase == 3 else 1)
            for blk in range(nrep):
                src_row = t_gam if blk < NK else t_dr
                cb = (blk % NK) * 128
                if blk > 0:
                    eng.wait_ge(s_v, E(("rev", blk - 1)))
                eng.matmul(out=ps_rep[:], lhsT=t_ones[0:1, :],
                           rhs=src_row[0:1, cb:cb + 128],
                           start=True, stop=True).then_inc(s_t, 1)
                tab.rec(s_t, ("rep", blk))
            eng.wait_ge(s_v, E(("wstat", NK - 1)))
            eng.wait_ge(s_v, E(("rev", nrep - 1)))
            eng.matmul(out=ps_rep[:, 0:1], lhsT=t_ones[:],
                       rhs=t_st[:, 18:19], start=True, stop=True).then_inc(s_t, 1)
            tab.rec(s_t, ("wsum",))
            for k in range(NK):
                eng.wait_ge(s_v, E(("wq3", k)))
                for b in range(NK):
                    pst = ps_t1 if (k * NK + b) % 2 == 0 else ps_t2
                    if k * NK + b >= 2:
                        eng.wait_ge(s_v, E(("wev", (k * NK + b - 2) // NK,
                                            (k * NK + b - 2) % NK)))
                    eng.transpose(out=pst[:], in_=t_xq[:, b * 128:(b + 1) * 128],
                                  identity=t_id[:]).then_inc(s_t, 1)
                    tab.rec(s_t, ("wtr", k, b))
            for i in range(NT):
                if phase == 3:
                    eng.wait_ge(s_y0 if i % 2 == 0 else s_y1, 16 * NK * (i // 2 + 1))
                    for b in range(NK):
                        un = i * NK + b
                        pst = ps_f1 if un % 2 == 0 else ps_f2
                        eng.wait_ge(s_v, E(("yev_prev", i, b)))
                        eng.transpose(out=pst[:],
                                      in_=t_y[:, (i % 2) * D + b * 128:(i % 2) * D + (b + 1) * 128],
                                      identity=t_idf[:]).then_inc(s_t, 1)
                        tab.rec(s_t, ("ytr", i, b))
                eng.wait_ge(s_v, E(("xq", i)))
                for b in range(NK):
                    un = i * NK + b
                    pst = ps_t1 if un % 2 == 0 else ps_t2
                    eng.wait_ge(s_v, E(("xqev_prev", i, b)))
                    eng.transpose(out=pst[:], in_=t_xq[:, b * 128:(b + 1) * 128],
                                  identity=t_id[:]).then_inc(s_t, 1)
                    tab.rec(s_t, ("xqtr", i, b))
                eng.wait_ge(s_v, E(("xqev", i, NK - 1)))
                if i > 0:
                    eng.wait_ge(s_v, E(("ep", i - 1)))
                for ob in range(4):
                    last = None
                    for k in range(NK):
                        last = eng.matmul(out=ps_m[:, ob * 512:(ob + 1) * 512],
                                          lhsT=t_xqT[:, k * 128:(k + 1) * 128],
                                          rhs=t_wq[:, k * D + ob * 512:k * D + (ob + 1) * 512],
                                          start=(k == 0), stop=(k == NK - 1))
                    last.then_inc(s_t, 1)
                tab.rec(s_t, ("mm", i))
                if phase == 1:
                    eng.wait_ge(s_v, E(("xlb", i)))
                    for b in range(NK):
                        un = i * NK + b
                        pst = ps_t1 if un % 2 == 0 else ps_t2
                        eng.wait_ge(s_v, E(("xtev_prev", i, b)))
                        eng.transpose(out=pst[:], in_=t_xlb[:, b * 128:(b + 1) * 128],
                                      identity=t_id[:]).then_inc(s_t, 1)
                        tab.rec(s_t, ("xltr", i, b))
                    if i > 0:
                        eng.wait_ge(s_v, E(("xp", i - 1)))
                    else:
                        eng.wait_ge(s_v, E(("wsev",)))
                    lastd = lastb = None
                    for k in range(NK):
                        lastd = eng.matmul(out=ps_p1[:],
                                           lhsT=t_xpw[:, k * 160:k * 160 + 128],
                                           rhs=t_xt[:, k * 128:(k + 1) * 128],
                                           start=(k == 0), stop=(k == NK - 1))
                        lastb = eng.matmul(out=ps_p2[:],
                                           lhsT=t_xpw[:, k * 160 + 128:k * 160 + 160],
                                           rhs=t_xt[:, k * 128:(k + 1) * 128],
                                           start=(k == 0), stop=(k == NK - 1))
                    lastd.then_inc(s_t, 1)
                    lastb.then_inc(s_t, 1)
                    tab.rec(s_t, ("xpt", i))

        def em_scalar(eng, E):
            c_eps = t_cst[:, 0:1]
            c_rms = t_cst[:, 1:2]
            c_invd = t_cst[:, 2:3]
            c_half = t_cst[:, 3:4]
            for i in range(NT):
                eng.wait_ge(s_v, E(("var1", i)))
                eng.activation(out=t_st[:, 5:6], in_=t_st[:, 3:4], func=AF.Ln,
                               bias=c_eps).then_inc(s_s, 1)
                eng.drain()
                eng.activation(out=t_st[:, 6:7], in_=t_st[:, 5:6], func=AF.Exp,
                               scale=c_half).then_inc(s_s, 1)
                tab.rec(s_s, ("rstd1", i))
                if phase == 3:
                    eng.wait_ge(s_v, E(("var2", i)))
                    eng.drain()
                    eng.activation(out=t_st[:, 12:13], in_=t_st[:, 11:12], func=AF.Ln,
                                   bias=c_eps).then_inc(s_s, 1)
                    eng.drain()
                    eng.activation(out=t_st[:, 13:14], in_=t_st[:, 12:13], func=AF.Exp,
                                   scale=c_half).then_inc(s_s, 1)
                    tab.rec(s_s, ("rstd2", i))
                eng.wait_ge(s_v, E(("ssq2", i)))
                eng.drain()
                eng.activation(out=t_st[:, 8:9], in_=t_st[:, 7:8], func=AF.Ln,
                               scale=c_invd, bias=c_rms).then_inc(s_s, 1)
                eng.drain()
                eng.activation(out=t_st[:, 9:10], in_=t_st[:, 8:9], func=AF.Exp,
                               scale=c_half).then_inc(s_s, 1)
                tab.rec(s_s, ("rms2", i))
                if phase == 1:
                    eng.wait_ge(s_v, E(("var2", i)))
                    eng.drain()
                    eng.activation(out=t_st[:, 12:13], in_=t_st[:, 11:12], func=AF.Ln,
                                   bias=c_eps).then_inc(s_s, 1)
                    eng.drain()
                    eng.activation(out=t_st[:, 13:14], in_=t_st[:, 12:13], func=AF.Exp,
                                   scale=c_half).then_inc(s_s, 1)
                    tab.rec(s_s, ("rstd2", i))

        def ln_stats(eng, src_ap, sum_c, ssq_c, negm_c, var_c, i, first_wait=None):
            if first_wait is not None:
                eng.wait_ge(*first_wait)
            eng.tensor_reduce(out=t_st[:, sum_c:sum_c + 1], in_=src_ap,
                              axis=AX.X, op=OP.add).then_inc(s_v, 1)
            eng.scalar_tensor_tensor(out=t_scr[:], in0=src_ap, scalar=1.0,
                                     in1=src_ap, op0=OP.mult, op1=OP.mult,
                                     accum_out=t_st[:, ssq_c:ssq_c + 1]).then_inc(s_v, 1)
            eng.drain()
            eng.tensor_scalar(out=t_st[:, negm_c:negm_c + 1],
                              in0=t_st[:, sum_c:sum_c + 1],
                              scalar1=float(-1.0 / D), scalar2=None,
                              op0=OP.mult).then_inc(s_v, 1)
            eng.drain()
            eng.tensor_scalar(out=t_st[:, var_c + 1:var_c + 2],
                              in0=t_st[:, negm_c:negm_c + 1],
                              scalar1=t_st[:, negm_c:negm_c + 1], scalar2=None,
                              op0=OP.mult).then_inc(s_v, 1)
            eng.drain()
            eng.scalar_tensor_tensor(out=t_st[:, var_c:var_c + 1],
                                     in0=t_st[:, ssq_c:ssq_c + 1],
                                     scalar=float(1.0 / D),
                                     in1=t_st[:, var_c + 1:var_c + 2],
                                     op0=OP.mult, op1=OP.subtract).then_inc(s_v, 1)

        def em_vector(eng, E):
            eng.memset(t_cst[:, 0:1], float(EPS_LN))
            eng.memset(t_cst[:, 1:2], float(EPS_RMS))
            eng.memset(t_cst[:, 2:3], float(1.0 / D))
            eng.memset(t_cst[:, 3:4], -0.5)
            eng.memset(t_ones[:], 1.0).then_inc(s_v, 1)
            tab.rec(s_v, "ones")
            eng.wait_ge(s_g, 16 * (4 if phase == 3 else 2))
            ps_rep = ps_p1 if phase == 1 else ps_f1
            nrep = NK * (2 if phase == 3 else 1)
            for blk in range(nrep):
                dst_row = t_gam if blk < NK else t_dr
                cb = (blk % NK) * 128
                eng.wait_ge(s_t, E(("rep", blk)))
                eng.tensor_copy(out=dst_row[:, cb:cb + 128],
                                in_=ps_rep[:]).then_inc(s_v, 1)
                tab.rec(s_v, ("rev", blk))
            if phase == 1:
                for k in range(NK):
                    eng.wait_ge(s_xp, 16 * (k + 1))
                    eng.tensor_copy(out=t_xpw[:, k * 160:(k + 1) * 160],
                                    in_=t_xpwf[:]).then_inc(s_v, 1)
                    tab.rec(s_v, ("xpc", k))
            for k in range(NK):
                eng.wait_ge(s_wa0 if k % 2 == 0 else s_wa1, 16 * (k // 2 + 1))
                eng.tensor_reduce(out=t_st[:, 16 + (k % 2):17 + (k % 2)],
                                  in_=t_w[:, (k % 2) * D:(k % 2 + 1) * D],
                                  axis=AX.X, op=OP.add,
                                  apply_absolute_value=True).then_inc(s_v, 1)
                eng.drain()
                if k == 0:
                    eng.tensor_copy(out=t_st[:, 18:19],
                                    in_=t_st[:, 16:17]).then_inc(s_v, 1)
                else:
                    eng.tensor_tensor(out=t_st[:, 19:20], in0=t_st[:, 18:19],
                                      in1=t_st[:, 16 + (k % 2):17 + (k % 2)],
                                      op=OP.add).then_inc(s_v, 1)
                    eng.drain()
                    eng.tensor_copy(out=t_st[:, 18:19],
                                    in_=t_st[:, 19:20]).then_inc(s_v, 1)
                tab.rec(s_v, ("wstat", k))
            eng.drain()
            eng.wait_ge(s_t, E(("wsum",)))
            eng.tensor_copy(out=t_st[:, 18:19], in_=ps_rep[:, 0:1]).then_inc(s_v, 1)
            tab.rec(s_v, ("wsev",))
            eng.drain()
            eng.tensor_scalar(out=t_st[:, 20:21], in0=t_st[:, 18:19],
                              scalar1=float(1.0 / (D * D)), scalar2=float(EPS_Q),
                              op0=OP.mult, op1=OP.add).then_inc(s_v, 1)
            eng.drain()
            eng.reciprocal(out=t_st[:, 21:22], in_=t_st[:, 20:21]).then_inc(s_v, 1)
            eng.drain()
            tab.rec(s_v, ("ws",))
            for k in range(NK):
                eng.wait_ge(s_wb0 if k % 2 == 0 else s_wb1, 16 * (k // 2 + 1))
                eng.tensor_scalar(out=t_q[:], in0=t_w[:, (k % 2) * D:(k % 2 + 1) * D],
                                  scalar1=t_st[:, 21:22], scalar2=MAGIC,
                                  op0=OP.mult, op1=OP.add).then_inc(s_v, 1)
                eng.drain()
                eng.tensor_scalar(out=t_scr[:], in0=t_q[:],
                                  scalar1=-MAGIC, scalar2=1.0,
                                  op0=OP.add, op1=OP.min).then_inc(s_v, 1)
                eng.drain()
                tab.rec(s_v, ("wq", k))
                if k >= 1:
                    eng.wait_ge(s_t, E(("wtr", k - 1, NK - 1)))
                eng.tensor_scalar(out=t_xq[:], in0=t_scr[:],
                                  scalar1=-1.0, scalar2=None,
                                  op0=OP.max).then_inc(s_v, 1)
                tab.rec(s_v, ("wq3", k))
                for b in range(NK):
                    un = k * NK + b
                    pst = ps_t1 if un % 2 == 0 else ps_t2
                    eng.wait_ge(s_t, E(("wtr", k, b)))
                    eng.tensor_copy(out=t_wq[:, b * D + k * 128:b * D + (k + 1) * 128],
                                    in_=pst[:]).then_inc(s_v, 1)
                    tab.rec(s_v, ("wev", k, b))
            for i in range(NT):
                sx = s_x0 if i % 2 == 0 else s_x1
                xs = t_x[:, (i % 2) * D:(i % 2 + 1) * D]
                ln_stats(eng, xs, 0, 1, 2, 3, i,
                         first_wait=(sx, 16 * (i // 2 + 1)))
                tab.rec(s_v, ("var1", i))
                eng.wait_ge(s_s, E(("rstd1", i)))
                eng.drain()
                eng.tensor_scalar(out=t_xln[:], in0=xs,
                                  scalar1=t_st[:, 2:3], scalar2=t_st[:, 6:7],
                                  op0=OP.add, op1=OP.mult).then_inc(s_v, 1)
                tab.rec(s_v, ("xln1", i))
                if phase == 3:
                    for b in range(NK):
                        un = i * NK + b
                        pst = ps_f1 if un % 2 == 0 else ps_f2
                        tab.rec(s_v, ("yev_prev", i, b))
                        eng.wait_ge(s_t, E(("ytr", i, b)))
                        eng.tensor_copy(out=t_yt[:, b * 128:(b + 1) * 128],
                                        in_=pst[:]).then_inc(s_v, 1)
                        tab.rec(s_v, ("yev", i, b))
                    eng.drain()
                    eng.tensor_tensor(out=t_scr[:], in0=t_xln[:], in1=t_dr[:],
                                      op=OP.mult).then_inc(s_v, 1)
                    eng.drain()
                    eng.tensor_tensor(out=t_r2[:], in0=xs, in1=t_yt[:],
                                      op=OP.add).then_inc(s_v, 1)
                    eng.drain()
                    eng.tensor_tensor(out=t_yt[:], in0=t_r2[:], in1=t_scr[:],
                                      op=OP.add).then_inc(s_v, 1)
                    tab.rec(s_v, ("r2", i))
                    eng.drain()
                    ln_stats(eng, t_yt[:], 0, 10, 14, 11, i)
                    tab.rec(s_v, ("var2", i))
                    eng.wait_ge(s_s, E(("rstd2", i)))
                    eng.drain()
                    eng.tensor_scalar(out=t_xln[:], in0=t_yt[:],
                                      scalar1=t_st[:, 14:15], scalar2=t_st[:, 13:14],
                                      op0=OP.add, op1=OP.mult).then_inc(s_v, 1)
                resid = xs if phase == 1 else t_yt[:]
                eng.drain()
                eng.scalar_tensor_tensor(out=t_scr[:], in0=t_xln[:], scalar=1.0,
                                         in1=t_xln[:], op0=OP.mult, op1=OP.mult,
                                         accum_out=t_st[:, 7:8]).then_inc(s_v, 1)
                tab.rec(s_v, ("ssq2", i))
                eng.wait_ge(s_s, E(("rms2", i)))
                eng.drain()
                eng.scalar_tensor_tensor(out=t_scr[:], in0=t_xln[:],
                                         scalar=t_st[:, 9:10], in1=t_gam[:],
                                         op0=OP.mult, op1=OP.mult).then_inc(s_v, 1)
                eng.drain()
                eng.tensor_reduce(out=t_st[:, 15:16], in_=t_scr[:], axis=AX.X,
                                  op=OP.max, apply_absolute_value=True).then_inc(s_v, 1)
                eng.drain()
                eng.tensor_scalar(out=t_st[:, 22:23], in0=t_st[:, 15:16],
                                  scalar1=float(EPS_Q), scalar2=None,
                                  op0=OP.add).then_inc(s_v, 1)
                eng.drain()
                eng.reciprocal(out=t_st[:, 23:24], in_=t_st[:, 22:23]).then_inc(s_v, 1)
                eng.drain()
                eng.tensor_scalar(out=t_st[:, 10:11], in0=t_st[:, 23:24],
                                  scalar1=127.0, scalar2=None,
                                  op0=OP.mult).then_inc(s_v, 1)
                eng.tensor_scalar(out=t_st[:, 11:12], in0=t_st[:, 22:23],
                                  scalar1=float(0.5 / 127.0), scalar2=t_st[:, 20:21],
                                  op0=OP.mult, op1=OP.mult).then_inc(s_v, 1)
                tab.rec(s_v, ("qs", i))
                eng.drain()
                eng.tensor_scalar(out=t_q[:], in0=t_scr[:],
                                  scalar1=t_st[:, 10:11], scalar2=MAGIC,
                                  op0=OP.mult, op1=OP.add).then_inc(s_v, 1)
                eng.drain()
                if i > 0:
                    eng.wait_ge(s_t, E(("xqtr", i - 1, NK - 1)))
                else:
                    eng.wait_ge(s_t, E(("wtr", NK - 1, NK - 1)))
                eng.tensor_scalar(out=t_xq[:], in0=t_q[:],
                                  scalar1=-MAGIC, scalar2=None,
                                  op0=OP.add).then_inc(s_v, 1)
                tab.rec(s_v, ("xq", i))
                for b in range(NK):
                    un = i * NK + b
                    pst = ps_t1 if un % 2 == 0 else ps_t2
                    tab.rec(s_v, ("xqev_prev", i, b))
                    eng.wait_ge(s_t, E(("xqtr", i, b)))
                    eng.tensor_copy(out=t_xqT[:, b * 128:(b + 1) * 128],
                                    in_=pst[:]).then_inc(s_v, 1)
                    tab.rec(s_v, ("xqev", i, b))
                eng.wait_ge(s_t, E(("mm", i)))
                if i > 0:
                    eng.wait_ge(s_or, 16 * i)
                eng.drain()
                eng.scalar_tensor_tensor(out=t_r1[:], in0=ps_m[:],
                                         scalar=t_st[:, 11:12], in1=resid,
                                         op0=OP.mult, op1=OP.add).then_inc(s_v, 1)
                tab.rec(s_v, ("ep", i))
                if phase == 1:
                    eng.drain()
                    ln_stats(eng, t_r1[:], 0, 10, 14, 11, i)
                    tab.rec(s_v, ("var2", i))
                    eng.wait_ge(s_s, E(("rstd2", i)))
                    eng.drain()
                    eng.tensor_scalar(out=t_xln[:], in0=t_r1[:],
                                      scalar1=t_st[:, 14:15], scalar2=t_st[:, 13:14],
                                      op0=OP.add, op1=OP.mult).then_inc(s_v, 1)
                    eng.drain()
                    eng.tensor_copy(out=t_xlb[:], in_=t_xln[:]).then_inc(s_v, 1)
                    tab.rec(s_v, ("xlb", i))
                    for b in range(NK):
                        un = i * NK + b
                        pst = ps_t1 if un % 2 == 0 else ps_t2
                        tab.rec(s_v, ("xtev_prev", i, b))
                        eng.wait_ge(s_t, E(("xltr", i, b)))
                        if b == 0 and i > 0:
                            eng.wait_ge(s_ob, 16 * NK * i)
                        eng.tensor_copy(out=t_xt[:, b * 128:(b + 1) * 128],
                                        in_=pst[:]).then_inc(s_v, 1)
                        tab.rec(s_v, ("xtev", i, b))
                    eng.wait_ge(s_t, E(("xpt", i)))
                    if i > 0:
                        eng.wait_ge(s_odt, 32 * i)
                    eng.tensor_copy(out=t_pd[:], in_=ps_p1[:]).then_inc(s_v, 1)
                    eng.tensor_copy(out=t_pb[:], in_=ps_p2[:]).then_inc(s_v, 1)
                    tab.rec(s_v, ("xp", i))

        dry = _DryEng(tab.cnt)
        E = lambda name: tab.ev.get(name, 0) if tab.dry else tab.ev[name]
        for _ in range(2):
            tab.cnt.clear()
            em_vector(dry, E)
            em_scalar(dry, E)
            em_tensor(dry, E)
            em_sync(dry, E)
            em_gp_full(dry, E)
        tab.dry = False

        block = es.enter_context(nc.Block())
        tab.cnt.clear()

        @block.sync
        def _(eng):
            em_sync(eng, E)

        @block.gpsimd
        def _(eng):
            em_gp_full(eng, E)

        @block.tensor
        def _(eng):
            em_tensor(eng, E)

        @block.scalar
        def _(eng):
            em_scalar(eng, E)

        @block.vector
        def _(eng):
            em_vector(eng, E)

    return nc



def kernel_unfused(L0_attn_x, L0_attn_A_log, L0_attn_D_param, L0_attn_x_proj_w,
           L0_attn_dt_proj_w, L0_attn_dt_proj_b, L0_ffn_gamma, L0_ffn_W):
    f32 = np.float32
    x = np.ascontiguousarray(np.asarray(L0_attn_x, dtype=f32))
    W = np.ascontiguousarray(np.asarray(L0_ffn_W, dtype=f32))
    gamma = np.asarray(L0_ffn_gamma, dtype=f32)
    A_log = np.ascontiguousarray(np.asarray(L0_attn_A_log, dtype=f32))
    D_par = np.asarray(L0_attn_D_param, dtype=f32)
    xpw = np.asarray(L0_attn_x_proj_w, dtype=f32)
    dtw = np.asarray(L0_attn_dt_proj_w, dtype=f32)
    dtb = np.asarray(L0_attn_dt_proj_b, dtype=f32)
    ident_b = np.eye(128, dtype=f32).astype(bf16)
    ident_f = np.eye(128, dtype=f32)
    selmat = np.repeat(np.eye(N, dtype=f32), 128, axis=1).astype(bf16)
    xpwT = np.ascontiguousarray(xpw.T)

    nc1 = build_token_prog(1)
    im1 = []
    for c in range(8):
        b, h = c >> 1, c & 1
        im1.append({
            "Win": W, "gam": gamma[None, :], "ident": ident_b, "xpwT": xpwT,
            "xin": np.ascontiguousarray(x[b, h * TOK:(h + 1) * TOK, :]),
        })
    r1 = run_bass_kernel_spmd(nc1, im1, core_ids=list(range(8)))
    R1 = [r1.results[c]["r1o"] for c in range(8)]
    XLT = [r1.results[c]["xlnT"] for c in range(8)]
    DTV = [r1.results[c]["dtv"] for c in range(8)]
    BCV = [r1.results[c]["bcv"] for c in range(8)]

    nc2 = build_scan_prog()
    im2 = []
    for c in range(8):
        b, r = c >> 1, c & 1
        rs = slice(r * DH, (r + 1) * DH)
        im2.append({
            "xln": np.ascontiguousarray(
                np.concatenate([XLT[2 * b][rs, :], XLT[2 * b + 1][rs, :]], axis=1)),
            "dt": np.ascontiguousarray(
                np.concatenate([DTV[2 * b], DTV[2 * b + 1]], axis=1)),
            "Bm": np.ascontiguousarray(
                np.concatenate([BCV[2 * b][:N, :], BCV[2 * b + 1][:N, :]], axis=1)),
            "Cm": np.ascontiguousarray(
                np.concatenate([BCV[2 * b][N:, :], BCV[2 * b + 1][N:, :]], axis=1)),
            "dtwT": np.ascontiguousarray(dtw[rs, :].T),
            "dtbias": np.ascontiguousarray(dtb[rs, None]),
            "Alog": np.ascontiguousarray(A_log[rs, :]),
            "sel": selmat,
        })
    r2 = run_bass_kernel_spmd(nc2, im2, core_ids=list(range(8)))
    Y = [r2.results[c]["yout"] for c in range(8)]

    nc3 = build_token_prog(3)
    im3 = []
    for c in range(8):
        b, h = c >> 1, c & 1
        ts = slice(h * TOK, (h + 1) * TOK)
        im3.append({
            "Win": W, "gam": gamma[None, :], "ident": ident_b, "identf": ident_f,
            "xin": np.ascontiguousarray(R1[c]),
            "yTin": np.ascontiguousarray(
                np.concatenate([Y[2 * b][:, ts], Y[2 * b + 1][:, ts]], axis=0)),
            "dpar": D_par[None, :],
        })
    r3 = run_bass_kernel_spmd(nc3, im3, core_ids=list(range(8)))
    out = np.empty((B, L, D), dtype=f32)
    for c in range(8):
        b, h = c >> 1, c & 1
        out[b, h * TOK:(h + 1) * TOK, :] = r3.results[c]["outo"]
    return out



def build_fused():
    NT = TOK // 128
    NK = D // 128
    EPS_LN, EPS_RMS, EPS_Q = 1e-5, 1e-6, 1e-5
    nc = bass.Bass("TRN2", target_bir_lowering=False, debug=False)
    xin = nc.dram_tensor("xin", [TOK, D], F32, kind="ExternalInput").ap()
    Win = nc.dram_tensor("Win", [D, D], F32, kind="ExternalInput").ap()
    gam = nc.dram_tensor("gam", [1, D], F32, kind="ExternalInput").ap()
    ident = nc.dram_tensor("ident", [128, 128], BF16, kind="ExternalInput").ap()
    identf = nc.dram_tensor("identf", [128, 128], F32, kind="ExternalInput").ap()
    xpwTin = nc.dram_tensor("xpwT", [D, RT + 2 * N], F32, kind="ExternalInput").ap()
    dtwT = nc.dram_tensor("dtwT", [RT, DH], F32, kind="ExternalInput").ap()
    dtbias = nc.dram_tensor("dtbias", [DH, 1], F32, kind="ExternalInput").ap()
    Alog = nc.dram_tensor("Alog", [DH, N], F32, kind="ExternalInput").ap()
    selin = nc.dram_tensor("sel", [N, N * 128], BF16, kind="ExternalInput").ap()
    dpar = nc.dram_tensor("dpar", [1, D], F32, kind="ExternalInput").ap()
    mask = nc.dram_tensor("mask", [128, 2], F32, kind="ExternalInput").ap()
    outo = nc.dram_tensor("outo", [TOK, D], F32, kind="ExternalOutput").ap()
    r1d = nc.dram_tensor("r1d", [TOK, D], F32, kind="Internal").ap()
    wqd = nc.dram_tensor("wqd", [D, D], BF16, kind="Internal").ap()
    xci_h0 = nc.dram_tensor("xci_h0", [D, TOK // 2], BF16, kind="Internal").ap()
    xci_h1 = nc.dram_tensor("xci_h1", [D, TOK // 2], BF16, kind="Internal").ap()
    xco_h0 = nc.dram_tensor("xco_h0", [2, D, TOK // 2], BF16, kind="Internal").ap()
    xco_h1 = nc.dram_tensor("xco_h1", [2, D, TOK // 2], BF16, kind="Internal").ap()
    dtci = nc.dram_tensor("dtci", [RT, TOK], BF16, kind="Internal").ap()
    dtco = nc.dram_tensor("dtco", [2, RT, TOK], BF16, kind="Internal").ap()
    bcci = nc.dram_tensor("bcci", [2 * N, TOK], F32, kind="Internal").ap()
    bcco = nc.dram_tensor("bcco", [2, 2 * N, TOK], F32, kind="Internal").ap()
    yci_a = nc.dram_tensor("yci_a", [DH // 2, L], BF16, kind="Internal").ap()
    yci_b = nc.dram_tensor("yci_b", [DH // 2, L], BF16, kind="Internal").ap()
    yco_a = nc.dram_tensor("yco_a", [2, DH // 2, L], BF16, kind="Internal").ap()
    yco_b = nc.dram_tensor("yco_b", [2, DH // 2, L], BF16, kind="Internal").ap()
    brep_d = nc.dram_tensor("brep_d", [N * 128, L], BF16, kind="Internal").ap()
    crep_d = nc.dram_tensor("crep_d", [N * 128, L], BF16, kind="Internal").ap()
    RG = [[0, 1], [2, 3], [4, 5], [6, 7]]

    with ExitStack() as ces:
        csb = lambda name, shape, dt: ces.enter_context(nc.sbuf_tensor(name, shape, dt))
        t_cst = csb("t_cst", [128, 6], F32)
        t_st = csb("t_st", [128, 48], F32)
        t_msk = csb("t_msk", [128, 2], F32)
        s_cc = ces.enter_context(nc.semaphore("s_cc"))
        s_wqs = ces.enter_context(nc.semaphore("s_wqs"))
        s_r1s = ces.enter_context(nc.semaphore("s_r1s"))
        s_ob2 = ces.enter_context(nc.semaphore("s_ob2"))
        s_odt2 = ces.enter_context(nc.semaphore("s_odt2"))

        block = ces.enter_context(nc.Block())
        tab = EvTab()
        E = lambda name: tab.ev.get(name, 0) if tab.dry else tab.ev[name]
        dry = _DryEng(tab.cnt)

        def token_phase(P, es):
            ph = P["phase"]
            sbt = lambda name, shape, dt: es.enter_context(
                nc.sbuf_tensor(f"p{ph}_{name}", shape, dt))
            pre = f"p{ph}_"
            t_w = sbt("t_w", [128, 2 * D], F32) if ph == 1 else None
            t_wq = sbt("t_wq", [128, NK * D], BF16)
            t_x = sbt("t_x", [128, 2 * D], F32)
            t_xln = sbt("t_xln", [128, D], F32)
            t_scr = sbt("t_scr", [128, D], F32)
            t_q = sbt("t_q", [128, D], F32)
            t_xq = sbt("t_xq", [128, D], BF16)
            t_xqT = sbt("t_xqT", [128, NK * 128], BF16)
            t_gam = sbt("t_gam", [128, D], F32)
            t_r1 = sbt("t_r1", [128, D], F32)
            t_id = sbt("t_id", [128, 128], BF16)
            t_ones = sbt("t_ones", [128, 128], F32)
            if ph == 1:
                t_xlb = sbt("t_xlb", [128, D], BF16)
                t_xt = sbt("t_xt", [128, NK * 128], BF16)
                t_xpw = sbt("t_xpw", [128, NK * (RT + 2 * N)], BF16)
                t_xpwf = sbt("t_xpwf", [128, RT + 2 * N], F32)
                t_pd = sbt("t_pd", [128, 128], BF16)
                t_pb = sbt("t_pb", [32, 128], F32)
            else:
                t_idf = sbt("t_idf", [128, 128], F32)
                t_y = sbt("t_y", [128, 2 * D], BF16)
                t_y2 = sbt("t_y2", [128, 2 * D], BF16)
                t_yt = sbt("t_yt", [128, D], BF16)
                t_ytk = sbt("t_ytk", [128, D], BF16)
                t_r2 = sbt("t_r2", [128, 2 * D], F32)
                t_dr = sbt("t_dr", [128, D], F32)
            ps_m = es.enter_context(nc.psum_tensor(pre + "ps_m", [128, D], F32))
            ps_t1 = es.enter_context(nc.psum_tensor(pre + "ps_t1", [128, 512], BF16))
            ps_t2 = es.enter_context(nc.psum_tensor(pre + "ps_t2", [128, 512], BF16))
            if ph == 1:
                ps_p1 = es.enter_context(nc.psum_tensor(pre + "ps_p1", [128, 128], F32))
                ps_p2 = es.enter_context(nc.psum_tensor(pre + "ps_p2", [32, 128], F32))
                ps_rep = ps_p1
            else:
                ps_f1 = es.enter_context(nc.psum_tensor(pre + "ps_f1", [128, 512], BF16))
                ps_f2 = es.enter_context(nc.psum_tensor(pre + "ps_f2", [128, 512], BF16))
                ps_rep = ps_m
            sm = {}
            for nm in ["s_wa0", "s_wa1", "s_wb0", "s_wb1", "s_x0", "s_x1", "s_g",
                       "s_xp", "s_t", "s_s", "s_v", "s_or", "s_odt", "s_ob",
                       "s_y0", "s_y1", "s_y20", "s_y21"]:
                sm[nm] = es.enter_context(nc.semaphore(pre + nm))
            xsrc = xin if ph == 1 else r1d

            def V(name):
                return (ph, name)

            def em_sync(eng):
                if ph == 3:
                    eng.wait_ge(s_cc, 6)
                eng.dma_start(out=t_gam[0:1, :], in_=gam).then_inc(sm["s_g"], 16)
                eng.dma_start(out=t_id[:], in_=ident).then_inc(sm["s_g"], 16)
                if ph == 3:
                    eng.dma_start(out=t_dr[0:1, :], in_=dpar).then_inc(sm["s_g"], 16)
                    eng.dma_start(out=t_idf[:], in_=identf).then_inc(sm["s_g"], 16)
                if ph == 1:
                    for k in range(NK):
                        if k >= 1:
                            eng.wait_ge(sm["s_v"], E(V(("xpc", k - 1))))
                        eng.dma_start(out=t_xpwf[:],
                                      in_=xpwTin[k * 128:(k + 1) * 128, :]).then_inc(sm["s_xp"], 16)
                    for k in range(NK):
                        if k >= 2:
                            eng.wait_ge(sm["s_v"], E(V(("wstat", k - 2))))
                        eng.dma_start(out=t_w[:, (k % 2) * D:(k % 2 + 1) * D],
                                      in_=Win[k * 128:(k + 1) * 128, :]).then_inc(
                            sm["s_wa0"] if k % 2 == 0 else sm["s_wa1"], 16)
                    for k in range(NK):
                        eng.wait_ge(sm["s_v"], E(V(("wstat", NK - 2 + k))) if k < 2
                                    else E(V(("wq", k - 2))))
                        eng.dma_start(out=t_w[:, (k % 2) * D:(k % 2 + 1) * D],
                                      in_=Win[k * 128:(k + 1) * 128, :]).then_inc(
                            sm["s_wb0"] if k % 2 == 0 else sm["s_wb1"], 16)
                else:
                    eng.wait_ge(s_wqs, 16 * NK)
                    for b in range(NK):
                        eng.dma_start(out=t_wq[:, b * D:(b + 1) * D],
                                      in_=wqd[b * 128:(b + 1) * 128, :]).then_inc(
                            sm["s_g"], 16)
                    eng.wait_ge(s_r1s, 16 * NT)
                for i in range(NT):
                    if i >= 2:
                        eng.wait_ge(sm["s_v"], E(V(("ep", i - 2))))
                    eng.dma_start(out=t_x[:, (i % 2) * D:(i % 2 + 1) * D],
                                  in_=xsrc[i * 128:(i + 1) * 128, :]).then_inc(
                        sm["s_x0"] if i % 2 == 0 else sm["s_x1"], 16)
                    if ph == 3:
                        if i >= 2:
                            eng.wait_ge(sm["s_v"], E(V(("r2", i - 2))))
                        for b in range(NK):
                            half, bb = b // (NK // 2), b % (NK // 2)
                            yct = yco_a if bb < NK // 4 else yco_b
                            bb2 = bb % (NK // 4)
                            src0 = yct[half, bb2 * 128:(bb2 + 1) * 128,
                                       0 * TOK + i * 128:0 * TOK + (i + 1) * 128]
                            eng.dma_start(
                                out=t_y[:, (i % 2) * D + b * 128:(i % 2) * D + (b + 1) * 128],
                                in_=src0).then_inc(
                                sm["s_y0"] if i % 2 == 0 else sm["s_y1"], 16)

            def em_gp(eng):
                if ph == 3:
                    eng.wait_ge(s_cc, 6)
                    for i in range(NT):
                        if i >= 2:
                            eng.wait_ge(sm["s_v"], E(V(("r2", i - 2))))
                        for b in range(NK):
                            half, bb = b // (NK // 2), b % (NK // 2)
                            yct = yco_a if bb < NK // 4 else yco_b
                            bb2 = bb % (NK // 4)
                            src1 = yct[half, bb2 * 128:(bb2 + 1) * 128,
                                       1 * TOK + i * 128:1 * TOK + (i + 1) * 128]
                            eng.dma_start(
                                out=t_y2[:, (i % 2) * D + b * 128:(i % 2) * D + (b + 1) * 128],
                                in_=src1).then_inc(
                                sm["s_y20"] if i % 2 == 0 else sm["s_y21"], 16)
                        if i >= 1:
                            eng.wait_ge(sm["s_v"], E(V(("ep", i - 1))))
                            eng.dma_start(out=outo[(i - 1) * 128:i * 128, :],
                                          in_=t_r1[:]).then_inc(sm["s_or"], 16)
                    eng.wait_ge(sm["s_v"], E(V(("ep", NT - 1))))
                    eng.dma_start(out=outo[(NT - 1) * 128:NT * 128, :],
                                  in_=t_r1[:]).then_inc(sm["s_or"], 16)
                    eng.wait_ge(sm["s_or"], 16 * NT)
                else:
                    for b in range(NK):
                        eng.wait_ge(sm["s_v"], E(V(("wev", NK - 1, b // 4))))
                        eng.dma_start(out=wqd[b * 128:(b + 1) * 128, :],
                                      in_=t_wq[:, b * D:(b + 1) * D]).then_inc(s_wqs, 16)
                    for i in range(NT):
                        eng.wait_ge(sm["s_v"], E(V(("ep", i))))
                        eng.dma_start(out=r1d[i * 128:(i + 1) * 128, :],
                                      in_=t_r1[:]).then_inc(s_r1s, 16)
                        for b in range(NK):
                            eng.wait_ge(sm["s_s"], E(V(("xtev", i, b // 4))))
                            xci_t = xci_h0 if i < NT // 2 else xci_h1
                            ii = i % (NT // 2)
                            eng.dma_start(
                                out=xci_t[b * 128:(b + 1) * 128, ii * 128:(ii + 1) * 128],
                                in_=t_xt[:, b * 128:(b + 1) * 128]).then_inc(s_ob2, 16)
                        eng.wait_ge(sm["s_s"], E(V(("xp", i))))
                        eng.dma_start(out=dtci[:, i * 128:(i + 1) * 128],
                                      in_=t_pd[:]).then_inc(s_odt2, 16)
                        eng.dma_start(out=bcci[:, i * 128:(i + 1) * 128],
                                      in_=t_pb[:]).then_inc(s_odt2, 16)
                        if i == NT // 2 - 1:
                            eng.wait_ge(s_ob2, 16 * NK * (NT // 2))
                            eng.collective_compute(
                                "AllGather", OP.bypass, ins=[xci_h0],
                                outs=[xco_h0], replica_groups=RG).then_inc(s_cc, 1)
                    eng.wait_ge(s_odt2, 32 * NT)
                    eng.collective_compute("AllGather", OP.bypass, ins=[bcci],
                                           outs=[bcco], replica_groups=RG).then_inc(s_cc, 1)
                    eng.collective_compute("AllGather", OP.bypass, ins=[dtci],
                                           outs=[dtco], replica_groups=RG).then_inc(s_cc, 1)
                    eng.wait_ge(s_ob2, 16 * NT * NK)
                    eng.collective_compute("AllGather", OP.bypass, ins=[xci_h1],
                                           outs=[xco_h1], replica_groups=RG).then_inc(s_cc, 1)

            def em_tensor(eng):
                if ph == 3:
                    eng.wait_ge(s_cc, 6)
                eng.wait_ge(sm["s_g"], 16 * ((4 + NK) if ph == 3 else 2))
                eng.wait_ge(sm["s_v"], E(V("ones")))
                nrep = NK * (2 if ph == 3 else 1)
                for blk in range(nrep):
                    src_row = t_gam if blk < NK else t_dr
                    cb = (blk % NK) * 128
                    if blk > 0:
                        eng.wait_ge(sm["s_v"], E(V(("rev", blk - 1))))
                    eng.matmul(out=ps_rep[:, 0:128], lhsT=t_ones[0:1, :],
                               rhs=src_row[0:1, cb:cb + 128],
                               start=True, stop=True).then_inc(sm["s_t"], 1)
                    tab.rec(sm["s_t"], V(("rep", blk)))
                if ph == 1:
                    eng.wait_ge(sm["s_v"], E(V(("wstat", NK - 1))))
                    eng.wait_ge(sm["s_v"], E(V(("rev", nrep - 1))))
                    eng.matmul(out=ps_rep[:, 0:1], lhsT=t_ones[:],
                               rhs=t_st[:, 18:19], start=True, stop=True).then_inc(sm["s_t"], 1)
                    tab.rec(sm["s_t"], V(("wsum",)))
                    NG = NK // 4
                    for k in range(NK):
                        eng.wait_ge(sm["s_v"], E(V(("wq3", k))))
                        for g in range(NG):
                            un = k * NG + g
                            pst = ps_t1 if un % 2 == 0 else ps_t2
                            if un >= 2:
                                eng.wait_ge(sm["s_v"], E(V(("wev", (un - 2) // NG,
                                                            (un - 2) % NG))))
                            for q in range(4):
                                b = g * 4 + q
                                eng.transpose(out=pst[:, q * 128:(q + 1) * 128],
                                              in_=t_xq[:, b * 128:(b + 1) * 128],
                                              identity=t_id[:]).then_inc(sm["s_t"], 1)
                            tab.rec(sm["s_t"], V(("wtr", k, g)))

                def emit_xpt(i):
                    eng.wait_ge(sm["s_s"], E(V(("xtev", i, NK // 4 - 1))))
                    if i > 0:
                        eng.wait_ge(sm["s_s"], E(V(("xp", i - 1))))
                    else:
                        eng.wait_ge(sm["s_v"], E(V(("wsev",))))
                    lastd = lastb = None
                    for k in range(NK):
                        lastd = eng.matmul(out=ps_p1[:],
                                           lhsT=t_xpw[:, k * 160:k * 160 + 128],
                                           rhs=t_xt[:, k * 128:(k + 1) * 128],
                                           start=(k == 0), stop=(k == NK - 1))
                        lastb = eng.matmul(out=ps_p2[:],
                                           lhsT=t_xpw[:, k * 160 + 128:k * 160 + 160],
                                           rhs=t_xt[:, k * 128:(k + 1) * 128],
                                           start=(k == 0), stop=(k == NK - 1))
                    lastd.then_inc(sm["s_t"], 1)
                    lastb.then_inc(sm["s_t"], 1)
                    tab.rec(sm["s_t"], V(("xpt", i)))

                for i in range(NT):
                    if ph == 3:
                        eng.wait_ge(sm["s_v"], E(V(("ysel", i))))
                        for g in range(NK // 4):
                            un = i * (NK // 4) + g
                            pst = ps_f1 if un % 2 == 0 else ps_f2
                            if g >= 2:
                                eng.wait_ge(sm["s_s"], E(V(("yev", i, g - 2))))
                            elif i > 0:
                                eng.wait_ge(sm["s_s"], E(V(("yev", i - 1, NK // 4 - 2 + g))))
                            for q in range(4):
                                b = g * 4 + q
                                eng.transpose(out=pst[:, q * 128:(q + 1) * 128],
                                              in_=t_yt[:, b * 128:(b + 1) * 128],
                                              identity=t_id[:]).then_inc(sm["s_t"], 1)
                            tab.rec(sm["s_t"], V(("ytr", i, g)))
                    eng.wait_ge(sm["s_v"], E(V(("xq", i))))
                    for g in range(NK // 4):
                        un = i * (NK // 4) + g
                        pst = ps_t1 if un % 2 == 0 else ps_t2
                        eng.wait_ge(sm["s_s"], E(V(("xqev_prev", i, g))))
                        for q in range(4):
                            b = g * 4 + q
                            eng.transpose(out=pst[:, q * 128:(q + 1) * 128],
                                          in_=t_xq[:, b * 128:(b + 1) * 128],
                                          identity=t_id[:]).then_inc(sm["s_t"], 1)
                        tab.rec(sm["s_t"], V(("xqtr", i, g)))
                    eng.wait_ge(sm["s_s"], E(V(("xqev", i, NK // 4 - 1))))
                    if i > 0:
                        eng.wait_ge(sm["s_v"], E(V(("ep", i - 1))))
                    for ob in range(4):
                        last = None
                        for k in range(NK):
                            last = eng.matmul(out=ps_m[:, ob * 512:(ob + 1) * 512],
                                              lhsT=t_xqT[:, k * 128:(k + 1) * 128],
                                              rhs=t_wq[:, k * D + ob * 512:k * D + (ob + 1) * 512],
                                              start=(k == 0), stop=(k == NK - 1))
                        last.then_inc(sm["s_t"], 1)
                    tab.rec(sm["s_t"], V(("mm", i)))
                    if ph == 1:
                        if i > 0:
                            emit_xpt(i - 1)
                        eng.wait_ge(sm["s_v"], E(V(("xlb", i))))
                        for g in range(NK // 4):
                            un = i * (NK // 4) + g
                            pst = ps_t1 if un % 2 == 0 else ps_t2
                            eng.wait_ge(sm["s_s"], E(V(("xtev_prev", i, g))))
                            for q in range(4):
                                b = g * 4 + q
                                eng.transpose(out=pst[:, q * 128:(q + 1) * 128],
                                              in_=t_xlb[:, b * 128:(b + 1) * 128],
                                              identity=t_id[:]).then_inc(sm["s_t"], 1)
                            tab.rec(sm["s_t"], V(("xltr", i, g)))
                if ph == 1:
                    emit_xpt(NT - 1)

            def em_scalar(eng):
                if ph == 3:
                    eng.wait_ge(s_cc, 6)
                c_eps = t_cst[:, 0:1]
                c_rms = t_cst[:, 1:2]
                c_invd = t_cst[:, 2:3]
                c_half = t_cst[:, 3:4]
                QS = [{"ssq2": 7, "lnw": 8, "rms2": 9, "amax": 15, "ame": 22,
                       "rec": 23, "a": 10, "resid": 11},
                      {"ssq2": 31, "lnw": 32, "rms2": 33, "amax": 34, "ame": 35,
                       "rec": 36, "a": 37, "resid": 38}]

                def rstd1(i):
                    pb = 24 * (i % 2)
                    eng.wait_ge(sm["s_v"], E(V(("var1", i))))
                    eng.drain()
                    eng.activation(out=t_st[:, pb + 5:pb + 6],
                                   in_=t_st[:, pb + 3:pb + 4],
                                   func=AF.Ln, bias=c_eps).then_inc(sm["s_s"], 1)
                    eng.drain()
                    eng.activation(out=t_st[:, pb + 6:pb + 7],
                                   in_=t_st[:, pb + 5:pb + 6],
                                   func=AF.Exp, scale=c_half).then_inc(sm["s_s"], 1)
                    tab.rec(sm["s_s"], V(("rstd1", i)))

                def rstd2(i):
                    eng.wait_ge(sm["s_v"], E(V(("var2", i))))
                    eng.drain()
                    eng.activation(out=t_st[:, 43:44], in_=t_st[:, 41:42], func=AF.Ln,
                                   bias=c_eps).then_inc(sm["s_s"], 1)
                    eng.drain()
                    eng.activation(out=t_st[:, 44:45], in_=t_st[:, 43:44], func=AF.Exp,
                                   scale=c_half).then_inc(sm["s_s"], 1)
                    tab.rec(sm["s_s"], V(("rstd2", i)))

                def s_xqev(i):
                    for g in range(NK // 4):
                        un = i * (NK // 4) + g
                        pst = ps_t1 if un % 2 == 0 else ps_t2
                        tab.rec(sm["s_s"], V(("xqev_prev", i, g)))
                        eng.wait_ge(sm["s_t"], E(V(("xqtr", i, g))))
                        eng.drain()
                        eng.activation(out=t_xqT[:, g * 512:(g + 1) * 512],
                                       in_=pst[:], func=AF.Copy).then_inc(sm["s_s"], 1)
                        tab.rec(sm["s_s"], V(("xqev", i, g)))

                def s_yev(i):
                    for g in range(NK // 4):
                        eng.wait_ge(sm["s_t"], E(V(("ytr", i, g))))
                        pst = ps_f1 if (i * (NK // 4) + g) % 2 == 0 else ps_f2
                        eng.drain()
                        eng.activation(out=t_ytk[:, g * 512:(g + 1) * 512],
                                       in_=pst[:], func=AF.Copy).then_inc(sm["s_s"], 1)
                        tab.rec(sm["s_s"], V(("yev", i, g)))

                def s_xtev(i):
                    for g in range(NK // 4):
                        tab.rec(sm["s_s"], V(("xtev_prev", i, g)))
                        eng.wait_ge(sm["s_t"], E(V(("xltr", i, g))))
                        if g == 0 and i > 0:
                            eng.wait_ge(s_ob2, 16 * NK * i)
                        pst = ps_t1 if (i * (NK // 4) + g) % 2 == 0 else ps_t2
                        eng.drain()
                        eng.activation(out=t_xt[:, g * 512:(g + 1) * 512],
                                       in_=pst[:], func=AF.Copy).then_inc(sm["s_s"], 1)
                        tab.rec(sm["s_s"], V(("xtev", i, g)))

                def s_xp(i):
                    eng.wait_ge(sm["s_t"], E(V(("xpt", i))))
                    if i > 0:
                        eng.wait_ge(s_odt2, 32 * i)
                    eng.drain()
                    eng.activation(out=t_pd[:], in_=ps_p1[:],
                                   func=AF.Copy).then_inc(sm["s_s"], 1)
                    eng.activation(out=t_pb[:], in_=ps_p2[:],
                                   func=AF.Copy).then_inc(sm["s_s"], 1)
                    tab.rec(sm["s_s"], V(("xp", i)))

                def rms2(i):
                    q = QS[i % 2]
                    eng.wait_ge(sm["s_v"], E(V(("ssq2", i))))
                    eng.drain()
                    eng.activation(out=t_st[:, q["lnw"]:q["lnw"] + 1],
                                   in_=t_st[:, q["ssq2"]:q["ssq2"] + 1], func=AF.Ln,
                                   scale=c_invd, bias=c_rms).then_inc(sm["s_s"], 1)
                    eng.drain()
                    eng.activation(out=t_st[:, q["rms2"]:q["rms2"] + 1],
                                   in_=t_st[:, q["lnw"]:q["lnw"] + 1], func=AF.Exp,
                                   scale=c_half).then_inc(sm["s_s"], 1)
                    tab.rec(sm["s_s"], V(("rms2", i)))

                rstd1(0)
                if ph == 1:
                    rms2(0)
                    s_xqev(0)
                    rstd1(1)
                    rms2(1)
                    for i in range(NT):
                        rstd2(i)
                        if i + 2 < NT:
                            rstd1(i + 2)
                            rms2(i + 2)
                        s_xtev(i)
                        if i + 1 < NT:
                            s_xqev(i + 1)
                        s_xp(i)
                else:
                    for i in range(NT):
                        s_yev(i)
                        rstd2(i)
                        rms2(i)
                        s_xqev(i)
                        if i + 1 < NT:
                            rstd1(i + 1)

            def ln_stats(eng, src_ap, sum_c, ssq_c, negm_c, var_c, dummy=None):
                dum = t_scr if dummy is None else dummy
                eng.tensor_reduce(out=t_st[:, sum_c:sum_c + 1], in_=src_ap,
                                  axis=AX.X, op=OP.add).then_inc(sm["s_v"], 1)
                eng.scalar_tensor_tensor(out=dum[:], in0=src_ap, scalar=1.0,
                                         in1=src_ap, op0=OP.mult, op1=OP.mult,
                                         accum_out=t_st[:, ssq_c:ssq_c + 1]).then_inc(sm["s_v"], 1)
                eng.drain()
                eng.tensor_scalar(out=t_st[:, negm_c:negm_c + 1],
                                  in0=t_st[:, sum_c:sum_c + 1],
                                  scalar1=float(-1.0 / D), scalar2=None,
                                  op0=OP.mult).then_inc(sm["s_v"], 1)
                eng.drain()
                eng.tensor_scalar(out=t_st[:, var_c + 1:var_c + 2],
                                  in0=t_st[:, negm_c:negm_c + 1],
                                  scalar1=t_st[:, negm_c:negm_c + 1], scalar2=None,
                                  op0=OP.mult).then_inc(sm["s_v"], 1)
                eng.drain()
                eng.scalar_tensor_tensor(out=t_st[:, var_c:var_c + 1],
                                         in0=t_st[:, ssq_c:ssq_c + 1],
                                         scalar=float(1.0 / D),
                                         in1=t_st[:, var_c + 1:var_c + 2],
                                         op0=OP.mult, op1=OP.subtract).then_inc(sm["s_v"], 1)

            def em_vector(eng):
                if ph == 3:
                    eng.wait_ge(s_cc, 6)
                if ph == 1:
                    eng.memset(t_cst[:, 0:1], float(EPS_LN))
                    eng.memset(t_cst[:, 1:2], float(EPS_RMS))
                    eng.memset(t_cst[:, 2:3], float(1.0 / D))
                    eng.memset(t_cst[:, 3:4], -0.5)
                eng.memset(t_ones[:], 1.0).then_inc(sm["s_v"], 1)
                tab.rec(sm["s_v"], V("ones"))
                eng.wait_ge(sm["s_g"], 16 * ((4 + NK) if ph == 3 else 2))
                nrep = NK * (2 if ph == 3 else 1)
                for blk in range(nrep):
                    dst_row = t_gam if blk < NK else t_dr
                    cb = (blk % NK) * 128
                    eng.wait_ge(sm["s_t"], E(V(("rep", blk))))
                    eng.tensor_copy(out=dst_row[:, cb:cb + 128],
                                    in_=ps_rep[:, 0:128]).then_inc(sm["s_v"], 1)
                    tab.rec(sm["s_v"], V(("rev", blk)))
                if ph == 1:
                    for k in range(NK):
                        eng.wait_ge(sm["s_xp"], 16 * (k + 1))
                        eng.tensor_copy(out=t_xpw[:, k * 160:(k + 1) * 160],
                                        in_=t_xpwf[:]).then_inc(sm["s_v"], 1)
                        tab.rec(sm["s_v"], V(("xpc", k)))
                    for k in range(NK):
                        eng.wait_ge(sm["s_wa0"] if k % 2 == 0 else sm["s_wa1"],
                                    16 * (k // 2 + 1))
                        eng.tensor_reduce(out=t_st[:, 16 + (k % 2):17 + (k % 2)],
                                          in_=t_w[:, (k % 2) * D:(k % 2 + 1) * D],
                                          axis=AX.X, op=OP.add,
                                          apply_absolute_value=True).then_inc(sm["s_v"], 1)
                        eng.drain()
                        if k == 0:
                            eng.tensor_copy(out=t_st[:, 18:19],
                                            in_=t_st[:, 16:17]).then_inc(sm["s_v"], 1)
                        else:
                            eng.tensor_tensor(out=t_st[:, 19:20], in0=t_st[:, 18:19],
                                              in1=t_st[:, 16 + (k % 2):17 + (k % 2)],
                                              op=OP.add).then_inc(sm["s_v"], 1)
                            eng.drain()
                            eng.tensor_copy(out=t_st[:, 18:19],
                                            in_=t_st[:, 19:20]).then_inc(sm["s_v"], 1)
                        tab.rec(sm["s_v"], V(("wstat", k)))
                    eng.drain()
                    eng.wait_ge(sm["s_t"], E(V(("wsum",))))
                    eng.tensor_copy(out=t_st[:, 18:19], in_=ps_rep[:, 0:1]).then_inc(sm["s_v"], 1)
                    tab.rec(sm["s_v"], V(("wsev",)))
                    eng.drain()
                    eng.tensor_scalar(out=t_st[:, 20:21], in0=t_st[:, 18:19],
                                      scalar1=float(1.0 / (D * D)), scalar2=float(EPS_Q),
                                      op0=OP.mult, op1=OP.add).then_inc(sm["s_v"], 1)
                    eng.drain()
                    eng.reciprocal(out=t_st[:, 21:22], in_=t_st[:, 20:21]).then_inc(sm["s_v"], 1)
                    eng.drain()
                    tab.rec(sm["s_v"], V(("ws",)))
                    for k in range(NK):
                        eng.wait_ge(sm["s_wb0"] if k % 2 == 0 else sm["s_wb1"],
                                    16 * (k // 2 + 1))
                        eng.tensor_scalar(out=t_q[:], in0=t_w[:, (k % 2) * D:(k % 2 + 1) * D],
                                          scalar1=t_st[:, 21:22], scalar2=MAGIC,
                                          op0=OP.mult, op1=OP.add).then_inc(sm["s_v"], 1)
                        eng.drain()
                        eng.tensor_scalar(out=t_scr[:], in0=t_q[:],
                                          scalar1=-MAGIC, scalar2=1.0,
                                          op0=OP.add, op1=OP.min).then_inc(sm["s_v"], 1)
                        eng.drain()
                        tab.rec(sm["s_v"], V(("wq", k)))
                        if k >= 1:
                            eng.wait_ge(sm["s_t"], E(V(("wtr", k - 1, NK // 4 - 1))))
                        eng.tensor_scalar(out=t_xq[:], in0=t_scr[:],
                                          scalar1=-1.0, scalar2=None,
                                          op0=OP.max).then_inc(sm["s_v"], 1)
                        tab.rec(sm["s_v"], V(("wq3", k)))
                        for g in range(NK // 4):
                            un = k * (NK // 4) + g
                            pst = ps_t1 if un % 2 == 0 else ps_t2
                            eng.wait_ge(sm["s_t"], E(V(("wtr", k, g))))
                            for q in range(4):
                                b = g * 4 + q
                                eng.tensor_copy(
                                    out=t_wq[:, b * D + k * 128:b * D + (k + 1) * 128],
                                    in_=pst[:, q * 128:(q + 1) * 128]).then_inc(sm["s_v"], 1)
                            tab.rec(sm["s_v"], V(("wev", k, g)))
                QS = [{"ssq2": 7, "lnw": 8, "rms2": 9, "amax": 15, "ame": 22,
                       "rec": 23, "a": 10, "resid": 11},
                      {"ssq2": 31, "lnw": 32, "rms2": 33, "amax": 34, "ame": 35,
                       "rec": 36, "a": 37, "resid": 38}]

                def xcol(c):
                    return t_st[:, c:c + 1]

                def pre(i):
                    pb = 24 * (i % 2)
                    q = QS[i % 2]
                    xs = t_x[:, (i % 2) * D:(i % 2 + 1) * D]
                    if i == 0:
                        eng.wait_ge(sm["s_x0"], 16)
                        ln_stats(eng, xs, pb + 0, pb + 1, pb + 2, pb + 3)
                        tab.rec(sm["s_v"], V(("var1", i)))
                    eng.wait_ge(sm["s_s"], E(V(("rstd1", i))))
                    eng.drain()
                    eng.tensor_scalar(out=t_xln[:], in0=xs,
                                      scalar1=xcol(pb + 2), scalar2=xcol(pb + 6),
                                      op0=OP.add, op1=OP.mult).then_inc(sm["s_v"], 1)
                    tab.rec(sm["s_v"], V(("xln1", i)))
                    if ph == 3:
                        r2s = t_r2[:, (i % 2) * D:(i % 2 + 1) * D]
                        eng.wait_ge(sm["s_y0"] if i % 2 == 0 else sm["s_y1"],
                                    16 * NK * (i // 2 + 1))
                        eng.wait_ge(sm["s_y20"] if i % 2 == 0 else sm["s_y21"],
                                    16 * NK * (i // 2 + 1))
                        eng.drain()
                        eng.tensor_scalar(out=t_yt[:], in0=t_y[:, (i % 2) * D:(i % 2 + 1) * D],
                                          scalar1=t_msk[:, 1:2], scalar2=None,
                                          op0=OP.mult).then_inc(sm["s_v"], 1)
                        eng.drain()
                        eng.scalar_tensor_tensor(out=t_yt[:],
                                                 in0=t_y2[:, (i % 2) * D:(i % 2 + 1) * D],
                                                 scalar=t_msk[:, 0:1], in1=t_yt[:],
                                                 op0=OP.mult, op1=OP.add).then_inc(sm["s_v"], 1)
                        tab.rec(sm["s_v"], V(("ysel", i)))
                        eng.drain()
                        eng.tensor_tensor(out=t_scr[:], in0=t_xln[:], in1=t_dr[:],
                                          op=OP.mult).then_inc(sm["s_v"], 1)
                        eng.drain()
                        eng.wait_ge(sm["s_s"], E(V(("yev", i, NK // 4 - 1))))
                        eng.tensor_tensor(out=t_q[:], in0=xs, in1=t_ytk[:],
                                          op=OP.add).then_inc(sm["s_v"], 1)
                        eng.drain()
                        eng.tensor_tensor(out=r2s, in0=t_q[:], in1=t_scr[:],
                                          op=OP.add).then_inc(sm["s_v"], 1)
                        tab.rec(sm["s_v"], V(("r2", i)))
                        eng.drain()
                        ln_stats(eng, r2s, 19, 39, 40, 41)
                        tab.rec(sm["s_v"], V(("var2", i)))
                        eng.wait_ge(sm["s_s"], E(V(("rstd2", i))))
                        eng.drain()
                        eng.tensor_scalar(out=t_xln[:], in0=r2s,
                                          scalar1=xcol(40), scalar2=xcol(44),
                                          op0=OP.add, op1=OP.mult).then_inc(sm["s_v"], 1)
                    eng.drain()
                    eng.scalar_tensor_tensor(out=t_scr[:], in0=t_xln[:], scalar=1.0,
                                             in1=t_xln[:], op0=OP.mult, op1=OP.mult,
                                             accum_out=xcol(q["ssq2"])).then_inc(sm["s_v"], 1)
                    tab.rec(sm["s_v"], V(("ssq2", i)))
                    eng.wait_ge(sm["s_s"], E(V(("rms2", i))))
                    eng.drain()
                    eng.scalar_tensor_tensor(out=t_scr[:], in0=t_xln[:],
                                             scalar=xcol(q["rms2"]), in1=t_gam[:],
                                             op0=OP.mult, op1=OP.mult).then_inc(sm["s_v"], 1)
                    eng.drain()
                    eng.tensor_reduce(out=xcol(q["amax"]), in_=t_scr[:], axis=AX.X,
                                      op=OP.max, apply_absolute_value=True).then_inc(sm["s_v"], 1)
                    eng.drain()
                    eng.tensor_scalar(out=xcol(q["ame"]), in0=xcol(q["amax"]),
                                      scalar1=float(EPS_Q), scalar2=None,
                                      op0=OP.add).then_inc(sm["s_v"], 1)
                    eng.drain()
                    eng.reciprocal(out=xcol(q["rec"]), in_=xcol(q["ame"])).then_inc(sm["s_v"], 1)
                    eng.drain()
                    eng.tensor_scalar(out=xcol(q["a"]), in0=xcol(q["rec"]),
                                      scalar1=127.0, scalar2=None,
                                      op0=OP.mult).then_inc(sm["s_v"], 1)
                    eng.tensor_scalar(out=xcol(q["resid"]), in0=xcol(q["ame"]),
                                      scalar1=float(0.5 / 127.0), scalar2=xcol(20),
                                      op0=OP.mult, op1=OP.mult).then_inc(sm["s_v"], 1)
                    tab.rec(sm["s_v"], V(("qs", i)))
                    eng.drain()
                    eng.tensor_scalar(out=t_q[:], in0=t_scr[:],
                                      scalar1=xcol(q["a"]), scalar2=MAGIC,
                                      op0=OP.mult, op1=OP.add).then_inc(sm["s_v"], 1)
                    eng.drain()
                    if i > 0:
                        eng.wait_ge(sm["s_t"], E(V(("xqtr", i - 1, NK // 4 - 1))))
                    elif ph == 1:
                        eng.wait_ge(sm["s_t"], E(V(("wtr", NK - 1, NK // 4 - 1))))
                    eng.tensor_scalar(out=t_xq[:], in0=t_q[:],
                                      scalar1=-MAGIC, scalar2=None,
                                      op0=OP.add).then_inc(sm["s_v"], 1)
                    tab.rec(sm["s_v"], V(("xq", i)))
                    if i == 0:
                        hoist_stats(1)

                def hoist_stats(k):
                    pbk = 24 * (k % 2)
                    sxk = sm["s_x0"] if k % 2 == 0 else sm["s_x1"]
                    xsk = t_x[:, (k % 2) * D:(k % 2 + 1) * D]
                    eng.wait_ge(sxk, 16 * (k // 2 + 1))
                    eng.drain()
                    ln_stats(eng, xsk, pbk + 0, pbk + 1, pbk + 2, pbk + 3,
                             dummy=t_xln)
                    tab.rec(sm["s_v"], V(("var1", k)))

                pre(0)
                for i in range(NT):
                    q = QS[i % 2]
                    xs = t_x[:, (i % 2) * D:(i % 2 + 1) * D]
                    resid = xs if ph == 1 else t_r2[:, (i % 2) * D:(i % 2 + 1) * D]
                    if i + 1 < NT:
                        pre(i + 1)
                    eng.wait_ge(sm["s_t"], E(V(("mm", i))))
                    if i > 0:
                        eng.wait_ge(sm["s_or"] if ph == 3 else s_r1s, 16 * i)
                    eng.drain()
                    eng.scalar_tensor_tensor(out=t_r1[:], in0=ps_m[:],
                                             scalar=xcol(q["resid"]), in1=resid,
                                             op0=OP.mult, op1=OP.add).then_inc(sm["s_v"], 1)
                    tab.rec(sm["s_v"], V(("ep", i)))
                    if i + 2 < NT:
                        hoist_stats(i + 2)
                    if ph == 1:
                        eng.drain()
                        ln_stats(eng, t_r1[:], 19, 39, 40, 41)
                        tab.rec(sm["s_v"], V(("var2", i)))
                        eng.wait_ge(sm["s_s"], E(V(("rstd2", i))))
                        eng.drain()
                        eng.tensor_scalar(out=t_xlb[:], in0=t_r1[:],
                                          scalar1=xcol(40), scalar2=xcol(44),
                                          op0=OP.add, op1=OP.mult).then_inc(sm["s_v"], 1)
                        tab.rec(sm["s_v"], V(("xlb", i)))

            return dict(sync=em_sync, gp=em_gp, tensor=em_tensor,
                        scalar=em_scalar, vector=em_vector)

        def scan_phase(es):
            NT_T = TOK // 128
            NK = D // 128
            NKT = NT_T * NK
            sbt = lambda name, shape, dt: es.enter_context(
                nc.sbuf_tensor("p2_" + name, shape, dt))
            t_dtwb = sbt("t_dtwb", [RT, DH], BF16)
            t_dtb = sbt("t_dtb", [128, NDT], F32)
            t_alog = sbt("t_alog", [128, NDT * N], F32)
            t_a = sbt("t_a", [128, NDT * N], F32)
            t_an = sbt("t_an", [128, NDT * N], F32)
            t_bc = sbt("t_bc", [N, TOK], F32)
            t_bcb = sbt("t_bcb", [N, L], BF16)
            t_sel = sbt("t_sel", [N, N * 128], BF16)
            t_dt = sbt("t_dt", [RT, L], BF16)
            t_xln = sbt("t_xln", [128, 2 * L], BF16)
            t_xl2 = sbt("t_xl2", [128, L], BF16)
            t_delta = sbt("t_delta", [128, L], BF16)
            t_dx = sbt("t_dx", [128, L], BF16)
            t_sp1 = sbt("t_sp1", [128, HL], F32)
            t_sp2 = sbt("t_sp2", [128, HL], F32)
            t_sp3 = sbt("t_sp3", [128, HL], F32)
            t_sp4 = sbt("t_sp4", [128, HL], F32)
            t_da = sbt("t_da", [128, L], F32)
            t_u = sbt("t_u", [128, L], BF16)
            t_h = sbt("t_h", [128, L], BF16)
            t_yp = sbt("t_yp", [128, L], BF16)
            t_mg = sbt("t_mg", [128, L], BF16)
            t_p1 = sbt("t_p1", [128, L], BF16)
            t_p2 = sbt("t_p2", [128, L], BF16)
            t_p3 = sbt("t_p3", [128, L], BF16)
            t_p4 = sbt("t_p4", [128, L], BF16)
            t_brep = sbt("t_brep", [128, 2 * L], BF16)
            t_crep = sbt("t_crep", [128, L], BF16)
            ps_a = es.enter_context(nc.psum_tensor("p2_ps_a", [128, 2048], F32))
            ps_b = es.enter_context(nc.psum_tensor("p2_ps_b", [128, 2048], F32))
            sm = {}
            for nm in ["s_in", "s_bc", "s_xl0", "s_xl1", "s_br0", "s_br1",
                       "s_brc", "s_bs0", "s_bs1", "s_t", "s_s", "s_v", "s_y",
                       "s_dt"]:
                sm[nm] = es.enter_context(nc.semaphore("p2_" + nm))
            NPARAM = 1 + NDT * 2 + 1 + 1

            def V(name):
                return (2, name)

            def em_sync(eng):
                eng.wait_ge(s_wqs, 16 * NK)
                eng.wait_ge(s_r1s, 16 * NT_T)
                eng.wait_ge(s_ob2, 16 * NKT)
                eng.wait_ge(s_odt2, 32 * NT_T)
                eng.dma_start(out=t_msk[:], in_=mask).then_inc(sm["s_in"], 16)
                eng.dma_start(out=t_sp1[:, 0:DH], in_=dtwT).then_inc(sm["s_in"], 16)
                for j in range(NDT):
                    eng.dma_start(out=t_dtb[:, j:j + 1],
                                  in_=dtbias[j * 128:(j + 1) * 128, :]).then_inc(sm["s_in"], 16)
                    eng.dma_start(out=t_alog[:, j * N:(j + 1) * N],
                                  in_=Alog[j * 128:(j + 1) * 128, :]).then_inc(sm["s_in"], 16)
                eng.dma_start(out=t_sel[:], in_=selin).then_inc(sm["s_in"], 16)
                eng.wait_ge(s_cc, 3)
                eng.dma_start(out=t_dt[:, 0:TOK], in_=dtco[0]).then_inc(sm["s_dt"], 16)
                eng.dma_start(out=t_dt[:, TOK:L], in_=dtco[1]).then_inc(sm["s_dt"], 16)
                eng.wait_ge(s_cc, 2)
                for r in range(4):
                    which, th = r // 2, r % 2
                    if r >= 1:
                        eng.wait_ge(sm["s_v"], E(V(("bccast", r - 1))))
                    eng.dma_start(out=t_bc[:],
                                  in_=bcco[th, which * N:(which + 1) * N, :]).then_inc(sm["s_bc"], 16)
                eng.wait_ge(s_cc, 1)
                HT = TOK // 2
                for j in range(NDT):
                    if j >= 2:
                        eng.wait_ge(sm["s_v"], E(V(("dx", j - 2))))
                    if j >= 1:
                        eng.wait_ge(sm["s_v"], E(V(("xsel", j - 1))))
                    if j == 0:
                        pass
                    sl = (j % 2) * L
                    sx = sm["s_xl0"] if j % 2 == 0 else sm["s_xl1"]
                    for th in range(2):
                        for hh in range(2):
                            xsrc2 = xco_h0 if hh == 0 else xco_h1
                            if hh == 1 and j == 0 and th == 0:
                                eng.wait_ge(s_cc, 4)
                            co = sl + th * TOK + hh * HT
                            eng.dma_start(out=t_xln[:, co:co + HT],
                                          in_=xsrc2[th, j * 128:(j + 1) * 128, :]).then_inc(sx, 16)
                            eng.dma_start(out=t_xl2[:, th * TOK + hh * HT:th * TOK + hh * HT + HT],
                                          in_=xsrc2[th, DH + j * 128:DH + (j + 1) * 128, :]).then_inc(sx, 16)
                    for n in range(N):
                        sn = n % 2
                        if j == 0 and n == 0:
                            eng.wait_ge(sm["s_bs0"], 16 * (N * 2))
                            eng.wait_ge(sm["s_bs1"], 16 * (N * 2))
                        if (j, n) > (0, 1):
                            pj, pn = (j, n - 2) if n >= 2 else (j - 1, N - 2 + n)
                            eng.wait_ge(sm["s_v"], E(V(("u", pj, pn))))
                        eng.dma_start(out=t_brep[:, sn * L:(sn + 1) * L],
                                      in_=brep_d[n * 128:(n + 1) * 128, :]).then_inc(
                            sm["s_br0"] if sn == 0 else sm["s_br1"], 16)
                        if (j, n) != (0, 0):
                            pj, pn = (j, n - 1) if n >= 1 else (j - 1, N - 1)
                            eng.wait_ge(sm["s_v"], E(V(("ym", pj, pn))))
                        eng.dma_start(out=t_crep[:],
                                      in_=crep_d[n * 128:(n + 1) * 128, :]).then_inc(sm["s_brc"], 16)

            def em_gp(eng):
                eng.wait_ge(s_wqs, 16 * NK)
                eng.wait_ge(s_r1s, 16 * NT_T)
                eng.wait_ge(s_ob2, 16 * NKT)
                eng.wait_ge(s_odt2, 32 * NT_T)
                for which in range(2):
                    for n in range(N):
                        for half in range(2):
                            un = which * (N * 2) + n * 2 + half
                            eng.wait_ge(sm["s_v"], E(V(("evac", un))))
                            srcap = t_u[:, half * HL:(half + 1) * HL]
                            dst = (brep_d if which == 0 else crep_d)[
                                n * 128:(n + 1) * 128, half * HL:(half + 1) * HL]
                            eng.dma_start(out=dst, in_=srcap).then_inc(
                                sm["s_bs0"] if un % 2 == 0 else sm["s_bs1"], 16)
                for j in range(NDT):
                    eng.wait_ge(sm["s_v"], E(V(("ysumb", j))))
                    ydst = yci_a if j < NDT // 2 else yci_b
                    jj = j % (NDT // 2)
                    eng.dma_start(out=ydst[jj * 128:(jj + 1) * 128, :],
                                  in_=t_p1[:]).then_inc(sm["s_y"], 16)
                    if j == NDT // 2 - 1:
                        eng.wait_ge(sm["s_y"], 16 * (NDT // 2))
                        eng.collective_compute(
                            "AllGather", OP.bypass, ins=[yci_a],
                            outs=[yco_a], replica_groups=RG).then_inc(s_cc, 1)
                eng.wait_ge(sm["s_y"], 16 * NDT)
                eng.collective_compute("AllGather", OP.bypass, ins=[yci_b],
                                       outs=[yco_b], replica_groups=RG).then_inc(s_cc, 1)

            def em_tensor(eng):
                eng.wait_ge(s_wqs, 16 * NK)
                eng.wait_ge(s_r1s, 16 * NT_T)
                eng.wait_ge(s_ob2, 16 * NKT)
                eng.wait_ge(s_odt2, 32 * NT_T)
                eng.wait_ge(sm["s_in"], 16 * NPARAM)
                eng.wait_ge(sm["s_dt"], 32)
                eng.wait_ge(sm["s_v"], E(V("setup")))
                for which in range(2):
                    if which == 0:
                        eng.wait_ge(sm["s_v"], E(V("castB")))
                    else:
                        eng.wait_ge(sm["s_v"], E(V("castC")))
                    for n in range(N):
                        for half in range(2):
                            un = which * (N * 2) + n * 2 + half
                            pst = ps_a if un % 2 == 0 else ps_b
                            if un >= 2:
                                eng.wait_ge(sm["s_v"], E(V(("evac", un - 2))))
                            last = None
                            for blk in range(4):
                                last = eng.matmul(
                                    out=pst[:, blk * 512:(blk + 1) * 512],
                                    lhsT=t_sel[:, n * 128:(n + 1) * 128],
                                    rhs=t_bcb[:, half * HL + blk * 512:
                                              half * HL + (blk + 1) * 512],
                                    start=True, stop=True)
                            last.then_inc(sm["s_t"], 1)
                            tab.rec(sm["s_t"], V(("bld", un)))
                for j in range(NDT):
                    for half in range(2):
                        un = j * 2 + half
                        pst = ps_a if un % 2 == 0 else ps_b
                        if un < 2:
                            eng.wait_ge(sm["s_v"], E(V(("evac", N * 4 - 2 + un))))
                        else:
                            eng.wait_ge(sm["s_s"], E(V(("psfree", (un - 2) // 2, (un - 2) % 2))))
                        last = None
                        for blk in range(4):
                            last = eng.matmul(
                                out=pst[:, blk * 512:(blk + 1) * 512],
                                lhsT=t_dtwb[:, j * 128:(j + 1) * 128],
                                rhs=t_dt[:, half * HL + blk * 512:half * HL + (blk + 1) * 512],
                                start=True, stop=True)
                        last.then_inc(sm["s_t"], 1)
                        tab.rec(sm["s_t"], V(("dtproj", j, half)))

            def em_scalar(eng):
                eng.wait_ge(s_wqs, 16 * NK)
                eng.wait_ge(s_r1s, 16 * NT_T)
                eng.wait_ge(s_ob2, 16 * NKT)
                eng.wait_ge(s_odt2, 32 * NT_T)
                eng.wait_ge(sm["s_in"], 16 * NPARAM)
                eng.drain()
                eng.activation(out=t_a[:], in_=t_alog[:], func=AF.Exp).then_inc(sm["s_s"], 1)
                tab.rec(sm["s_s"], V("aexp"))
                for j in range(NDT):
                    for half in range(2):
                        un = j * 2 + half
                        pst = ps_a if un % 2 == 0 else ps_b
                        eng.wait_ge(sm["s_t"], E(V(("dtproj", j, half))))
                        if un >= 1:
                            eng.wait_ge(sm["s_v"], E(V(("delta", (un - 1) // 2, (un - 1) % 2))))
                        bias_ap = t_dtb[:, j:j + 1]
                        eng.drain()
                        eng.activation(out=t_sp1[:], in_=pst[:], func=AF.Abs,
                                       bias=bias_ap).then_inc(sm["s_s"], 1)
                        eng.activation(out=t_sp2[:], in_=pst[:], func=AF.Relu,
                                       bias=bias_ap).then_inc(sm["s_s"], 1)
                        tab.rec(sm["s_s"], V(("psfree", j, half)))
                        eng.drain()
                        eng.activation(out=t_sp3[:], in_=t_sp1[:], func=AF.Exp,
                                       scale=-1.0).then_inc(sm["s_s"], 1)
                        eng.drain()
                        eng.activation(out=t_sp4[:], in_=t_sp3[:], func=AF.Ln,
                                       bias=1.0).then_inc(sm["s_s"], 1)
                        tab.rec(sm["s_s"], V(("sp", j, half)))
                    for n in range(N):
                        if (j, n) != (0, 0):
                            pj, pn = (j, n - 1) if n >= 1 else (j - 1, N - 1)
                            eng.wait_ge(sm["s_v"], E(V(("tts", pj, pn))))
                        if n == 0 and j > 0:
                            eng.wait_ge(sm["s_v"], E(V(("ysumb", j - 1))))
                        eng.wait_ge(sm["s_v"], E(V(("dx", j))))
                        eng.drain()
                        eng.activation(out=t_da[:], in_=t_delta[:], func=AF.Exp,
                                       scale=t_an[:, j * N + n:j * N + n + 1]).then_inc(sm["s_s"], 1)
                        tab.rec(sm["s_s"], V(("da", j, n)))

            def em_vector(eng):
                eng.wait_ge(s_wqs, 16 * NK)
                eng.wait_ge(s_r1s, 16 * NT_T)
                eng.wait_ge(s_ob2, 16 * NKT)
                eng.wait_ge(s_odt2, 32 * NT_T)
                eng.wait_ge(sm["s_in"], 16 * NPARAM)
                eng.drain()
                eng.tensor_copy(out=t_dtwb[:], in_=t_sp1[:, 0:DH]).then_inc(sm["s_v"], 1)
                eng.wait_ge(sm["s_s"], E(V("aexp")))
                eng.tensor_scalar(out=t_an[:], in0=t_a[:], scalar1=-1.0, scalar2=None,
                                  op0=OP.mult).then_inc(sm["s_v"], 1)
                tab.rec(sm["s_v"], V("setup"))
                for r in range(2):
                    eng.wait_ge(sm["s_bc"], 16 * (r + 1))
                    eng.drain()
                    eng.tensor_copy(out=t_bcb[:, (r % 2) * TOK:(r % 2 + 1) * TOK],
                                    in_=t_bc[:]).then_inc(sm["s_v"], 1)
                    tab.rec(sm["s_v"], V(("bccast", r)))
                tab.rec(sm["s_v"], V("castB"))
                for which in range(2):
                    for n in range(N):
                        for half in range(2):
                            un = which * (N * 2) + n * 2 + half
                            pst = ps_a if un % 2 == 0 else ps_b
                            eng.wait_ge(sm["s_t"], E(V(("bld", un))))
                            if un >= 2:
                                eng.wait_ge(sm["s_bs0"] if un % 2 == 0 else sm["s_bs1"],
                                            16 * (un // 2))
                            eng.drain()
                            eng.tensor_copy(out=t_u[:, half * HL:(half + 1) * HL],
                                            in_=pst[:]).then_inc(sm["s_v"], 1)
                            tab.rec(sm["s_v"], V(("evac", un)))
                            if un == N * 2 - 1:
                                for r in range(2, 4):
                                    eng.wait_ge(sm["s_bc"], 16 * (r + 1))
                                    eng.drain()
                                    eng.tensor_copy(out=t_bcb[:, (r % 2) * TOK:(r % 2 + 1) * TOK],
                                                    in_=t_bc[:]).then_inc(sm["s_v"], 1)
                                    tab.rec(sm["s_v"], V(("bccast", r)))
                                tab.rec(sm["s_v"], V("castC"))
                for j in range(NDT):
                    sl = (j % 2) * L
                    sx = sm["s_xl0"] if j % 2 == 0 else sm["s_xl1"]
                    for half in range(2):
                        eng.wait_ge(sm["s_s"], E(V(("sp", j, half))))
                        if j > 0 and half == 0:
                            eng.wait_ge(sm["s_s"], E(V(("da", j - 1, N - 1))))
                        eng.drain()
                        eng.tensor_tensor(out=t_delta[:, half * HL:(half + 1) * HL],
                                          in0=t_sp2[:], in1=t_sp4[:], op=OP.add).then_inc(sm["s_v"], 1)
                        tab.rec(sm["s_v"], V(("delta", j, half)))
                    eng.wait_ge(sx, 16 * 8 * (j // 2 + 1))
                    eng.drain()
                    eng.tensor_scalar(out=t_xln[:, sl:sl + L], in0=t_xln[:, sl:sl + L],
                                      scalar1=t_msk[:, 1:2], scalar2=None,
                                      op0=OP.mult).then_inc(sm["s_v"], 1)
                    eng.drain()
                    eng.scalar_tensor_tensor(out=t_xln[:, sl:sl + L],
                                             in0=t_xl2[:],
                                             scalar=t_msk[:, 0:1],
                                             in1=t_xln[:, sl:sl + L],
                                             op0=OP.mult, op1=OP.add).then_inc(sm["s_v"], 1)
                    tab.rec(sm["s_v"], V(("xsel", j)))
                    eng.drain()
                    eng.tensor_tensor(out=t_dx[:], in0=t_delta[:],
                                      in1=t_xln[:, sl:sl + L],
                                      op=OP.mult).then_inc(sm["s_v"], 1)
                    tab.rec(sm["s_v"], V(("dx", j)))
                    eng.drain()
                    for n in range(N):
                        sn = n % 2
                        idx = j * N + n
                        if j == 0 and n == 0:
                            eng.wait_ge(sm["s_bs0"], 16 * (N * 2))
                            eng.wait_ge(sm["s_bs1"], 16 * (N * 2))
                        eng.wait_ge(sm["s_br0"] if sn == 0 else sm["s_br1"],
                                    16 * (idx // 2 + 1))
                        eng.drain()
                        eng.tensor_tensor(out=t_u[:], in0=t_dx[:],
                                          in1=t_brep[:, sn * L:(sn + 1) * L],
                                          op=OP.mult).then_inc(sm["s_v"], 1)
                        tab.rec(sm["s_v"], V(("u", j, n)))
                        eng.wait_ge(sm["s_s"], E(V(("da", j, n))))
                        eng.drain()
                        eng.tensor_tensor_scan(out=t_h[:], data0=t_da[:], data1=t_u[:],
                                               initial=0.0, op0=OP.mult,
                                               op1=OP.add).then_inc(sm["s_v"], 1)
                        tab.rec(sm["s_v"], V(("tts", j, n)))
                        ym_dst = t_p1 if n % 2 == 0 else t_yp
                        if n == 0 and j > 0:
                            eng.wait_ge(sm["s_y"], 16 * j)
                        eng.wait_ge(sm["s_brc"], 16 * (idx + 1))
                        eng.drain()
                        eng.tensor_tensor(out=ym_dst[:], in0=t_h[:],
                                          in1=t_crep[:],
                                          op=OP.mult).then_inc(sm["s_v"], 1)
                        tab.rec(sm["s_v"], V(("ym", j, n)))
                        if n % 2 == 1:
                            k = n + 1
                            chain = 0
                            kk = k
                            while kk % 2 == 0 and kk > 1:
                                chain += 1
                                kk //= 2
                            bufs = [t_mg, t_u]
                            cur = None
                            lvslots = [t_p2, t_p3, t_p4]
                            for c in range(chain):
                                src0 = t_p1 if c == 0 else cur
                                src1 = t_yp if c == 0 else lvslots[c - 1]
                                if c == chain - 1:
                                    if k == N:
                                        dst = t_da
                                    else:
                                        dst = lvslots[c]
                                else:
                                    dst = bufs[c % 2]
                                eng.drain()
                                eng.tensor_tensor(out=dst[:], in0=src0[:], in1=src1[:],
                                                  op=OP.add).then_inc(sm["s_v"], 1)
                                cur = dst
                    tab.rec(sm["s_v"], V(("ysum", j)))
                    eng.drain()
                    eng.tensor_copy(out=t_p1[:], in_=t_da[:]).then_inc(sm["s_v"], 1)
                    tab.rec(sm["s_v"], V(("ysumb", j)))

            return dict(sync=em_sync, gp=em_gp, tensor=em_tensor,
                        scalar=em_scalar, vector=em_vector)

        p1es = ExitStack()
        p1 = token_phase({"phase": 1}, p1es)
        for _ in range(2):
            tab.cnt.clear()
            p1["vector"](dry); p1["scalar"](dry); p1["tensor"](dry)
            p1["sync"](dry); p1["gp"](dry)
        tab.dry = False

        @block.sync
        def _(eng):
            p1["sync"](eng)

        @block.gpsimd
        def _(eng):
            p1["gp"](eng)

        @block.tensor
        def _(eng):
            p1["tensor"](eng)

        @block.scalar
        def _(eng):
            p1["scalar"](eng)

        @block.vector
        def _(eng):
            p1["vector"](eng)

        p1es.close()

        p2es = ExitStack()
        tab.dry = True
        p2 = scan_phase(p2es)
        for _ in range(2):
            tab.cnt.clear()
            p2["vector"](dry); p2["scalar"](dry); p2["tensor"](dry)
            p2["sync"](dry); p2["gp"](dry)
        tab.dry = False

        @block.sync
        def _(eng):
            p2["sync"](eng)

        @block.gpsimd
        def _(eng):
            p2["gp"](eng)

        @block.tensor
        def _(eng):
            p2["tensor"](eng)

        @block.scalar
        def _(eng):
            p2["scalar"](eng)

        @block.vector
        def _(eng):
            p2["vector"](eng)

        p2es.close()

        p3es = ExitStack()
        tab.dry = True
        p3 = token_phase({"phase": 3}, p3es)
        for _ in range(2):
            tab.cnt.clear()
            p3["vector"](dry); p3["scalar"](dry); p3["tensor"](dry)
            p3["sync"](dry); p3["gp"](dry)
        tab.dry = False

        @block.sync
        def _(eng):
            p3["sync"](eng)

        @block.gpsimd
        def _(eng):
            p3["gp"](eng)

        @block.tensor
        def _(eng):
            p3["tensor"](eng)

        @block.scalar
        def _(eng):
            p3["scalar"](eng)

        @block.vector
        def _(eng):
            p3["vector"](eng)

        p3es.close()
    return nc


def kernel_fused(L0_attn_x, L0_attn_A_log, L0_attn_D_param, L0_attn_x_proj_w,
                 L0_attn_dt_proj_w, L0_attn_dt_proj_b, L0_ffn_gamma, L0_ffn_W):
    f32 = np.float32
    x = np.ascontiguousarray(np.asarray(L0_attn_x, dtype=f32))
    W = np.ascontiguousarray(np.asarray(L0_ffn_W, dtype=f32))
    gamma = np.asarray(L0_ffn_gamma, dtype=f32)
    A_log = np.ascontiguousarray(np.asarray(L0_attn_A_log, dtype=f32))
    D_par = np.asarray(L0_attn_D_param, dtype=f32)
    xpw = np.asarray(L0_attn_x_proj_w, dtype=f32)
    dtw = np.asarray(L0_attn_dt_proj_w, dtype=f32)
    dtb = np.asarray(L0_attn_dt_proj_b, dtype=f32)
    ident_b = np.eye(128, dtype=f32).astype(bf16)
    ident_f = np.eye(128, dtype=f32)
    selmat = np.repeat(np.eye(N, dtype=f32), 128, axis=1).astype(bf16)
    xpwT = np.ascontiguousarray(xpw.T)
    nc = build_fused()
    ims = []
    for c in range(8):
        b, h = c >> 1, c & 1
        rs = slice(h * DH, (h + 1) * DH)
        msk = np.zeros((128, 2), dtype=f32)
        msk[:, 0] = float(h)
        msk[:, 1] = float(1 - h)
        ims.append({
            "xin": np.ascontiguousarray(x[b, h * TOK:(h + 1) * TOK, :]),
            "Win": W, "gam": gamma[None, :], "ident": ident_b, "identf": ident_f,
            "xpwT": xpwT,
            "dtwT": np.ascontiguousarray(dtw[rs, :].T),
            "dtbias": np.ascontiguousarray(dtb[rs, None]),
            "Alog": np.ascontiguousarray(A_log[rs, :]),
            "sel": selmat, "dpar": D_par[None, :], "mask": msk,
        })
    res = run_bass_kernel_spmd(nc, ims, core_ids=list(range(8)))
    out = np.empty((B, L, D), dtype=f32)
    for c in range(8):
        b, h = c >> 1, c & 1
        out[b, h * TOK:(h + 1) * TOK, :] = res.results[c]["outo"]
    return out


kernel = kernel_fused


# version 18
# speedup vs baseline: 1.0596x; 1.0092x over previous
from contextlib import ExitStack
import numpy as np
import ml_dtypes

import concourse.bass as bass
import concourse.mybir as mybir
from concourse.bass_utils import run_bass_kernel_spmd
from concourse.alu_op_type import AluOpType as OP
from bass_rust import ReduceOp
from concourse import library_config

F32 = mybir.dt.float32
BF16 = mybir.dt.bfloat16
AF = mybir.ActivationFunctionType
AX = mybir.AxisListType
bf16 = ml_dtypes.bfloat16

B, L, D, N, RT = 4, 4096, 2048, 16, 128
TOK = L // 2
DH = D // 2
NDT = DH // 128
HL = L // 2
MAGIC = float(3 * 2 ** 22)



def build_scan_prog():
    nc = bass.Bass("TRN2", target_bir_lowering=False, debug=False)
    xln = nc.dram_tensor("xln", [DH, L], BF16, kind="ExternalInput").ap()
    dtin = nc.dram_tensor("dt", [RT, L], BF16, kind="ExternalInput").ap()
    Bm = nc.dram_tensor("Bm", [N, L], F32, kind="ExternalInput").ap()
    Cm = nc.dram_tensor("Cm", [N, L], F32, kind="ExternalInput").ap()
    dtwT = nc.dram_tensor("dtwT", [RT, DH], F32, kind="ExternalInput").ap()
    dtbias = nc.dram_tensor("dtbias", [DH, 1], F32, kind="ExternalInput").ap()
    Alog = nc.dram_tensor("Alog", [DH, N], F32, kind="ExternalInput").ap()
    selin = nc.dram_tensor("sel", [N, N * 128], BF16, kind="ExternalInput").ap()
    yout = nc.dram_tensor("yout", [DH, L], F32, kind="ExternalOutput").ap()
    brep_d = nc.dram_tensor("brep_d", [N * 128, L], BF16, kind="Internal").ap()
    crep_d = nc.dram_tensor("crep_d", [N * 128, L], BF16, kind="Internal").ap()

    with ExitStack() as es:
        sbt = lambda name, shape, dt: es.enter_context(nc.sbuf_tensor(name, shape, dt))
        t_dtwb = sbt("t_dtwb", [RT, DH], BF16)
        t_dtb = sbt("t_dtb", [128, NDT], F32)
        t_alog = sbt("t_alog", [128, NDT * N], F32)
        t_a = sbt("t_a", [128, NDT * N], F32)
        t_an = sbt("t_an", [128, NDT * N], F32)
        t_bc = sbt("t_bc", [N, L], F32)
        t_bcb = sbt("t_bcb", [N, L], BF16)
        t_sel = sbt("t_sel", [N, N * 128], BF16)
        t_dt = sbt("t_dt", [RT, L], BF16)
        t_xln = sbt("t_xln", [128, 2 * L], BF16)
        t_delta = sbt("t_delta", [128, L], BF16)
        t_dx = sbt("t_dx", [128, L], BF16)
        t_sp1 = sbt("t_sp1", [128, HL], F32)
        t_sp2 = sbt("t_sp2", [128, HL], F32)
        t_sp3 = sbt("t_sp3", [128, HL], F32)
        t_sp4 = sbt("t_sp4", [128, HL], F32)
        t_da = sbt("t_da", [128, L], F32)
        t_u = sbt("t_u", [128, L], BF16)
        t_h = sbt("t_h", [128, L], BF16)
        t_yp = sbt("t_yp", [128, L], BF16)
        t_mg = sbt("t_mg", [128, L], BF16)
        t_p1 = sbt("t_p1", [128, L], BF16)
        t_p2 = sbt("t_p2", [128, L], BF16)
        t_p3 = sbt("t_p3", [128, L], BF16)
        t_p4 = sbt("t_p4", [128, L], BF16)
        t_brep = sbt("t_brep", [128, 2 * L], BF16)
        t_crep = sbt("t_crep", [128, L], BF16)
        ps_a = es.enter_context(nc.psum_tensor("ps_a", [128, 2048], F32))
        ps_b = es.enter_context(nc.psum_tensor("ps_b", [128, 2048], F32))

        s_in = es.enter_context(nc.semaphore("s_in"))
        s_bc = es.enter_context(nc.semaphore("s_bc"))
        s_xl0 = es.enter_context(nc.semaphore("s_xl0"))
        s_xl1 = es.enter_context(nc.semaphore("s_xl1"))
        s_br0 = es.enter_context(nc.semaphore("s_br0"))
        s_br1 = es.enter_context(nc.semaphore("s_br1"))
        s_brc = es.enter_context(nc.semaphore("s_brc"))
        s_bs0 = es.enter_context(nc.semaphore("s_bs0"))
        s_bs1 = es.enter_context(nc.semaphore("s_bs1"))
        s_t = es.enter_context(nc.semaphore("s_t"))
        s_s = es.enter_context(nc.semaphore("s_s"))
        s_v = es.enter_context(nc.semaphore("s_v"))
        s_y = es.enter_context(nc.semaphore("s_y"))

        ev_v = {}
        v = 0
        v += 1
        v += 1
        v += 1
        ev_v["castB"] = v
        ev_v["setup"] = v
        for un in range(N * 2):
            v += 1
            ev_v[("evac", un)] = v
            if un == N * 2 - 1:
                v += 1
                ev_v["castC"] = v
        for un in range(N * 2, N * 4):
            v += 1
            ev_v[("evac", un)] = v
        for j in range(NDT):
            for half in range(2):
                v += 1
                ev_v[("delta", j, half)] = v
            v += 1
            ev_v[("dx", j)] = v
            for n in range(N):
                v += 1
                ev_v[("u", j, n)] = v
                v += 1
                ev_v[("tts", j, n)] = v
                v += 1
                ev_v[("ym", j, n)] = v
                nm = bin(n + 1).count("1")
                extra = 0
                if n % 2 == 1:
                    k = n + 1
                    extra = 0
                    kk = k
                    while kk % 2 == 0 and kk > 1:
                        extra += 1
                        kk //= 2
                v += extra
                ev_v[("unit", j, n)] = v
            ev_v[("ysum", j)] = v
        V_TOTAL = v

        ev_s = {}
        s = 0
        s += 2
        s -= 1
        ev_s["setup"] = s
        for j in range(NDT):
            for half in range(2):
                s += 2
                ev_s[("psfree", j, half)] = s
                s += 2
                ev_s[("sp", j, half)] = s
            for n in range(N):
                s += 1
                ev_s[("da", j, n)] = s
        S_TOTAL = s

        ev_t = {}
        t = 0
        for un in range(N * 2 * 2):
            t += 1
            ev_t[("bld", un)] = t
        for j in range(NDT):
            for half in range(2):
                t += 1
                ev_t[("dtproj", j, half)] = t
        T_TOTAL = t

        NPARAM = 1 + NDT * 2 + 1 + 1
        ev_bs = {}
        bs = 0
        for un in range(N * 2 * 2):
            bs += 16
            ev_bs[("bst", un)] = bs
        ev_br = {}
        br = 0
        for j in range(NDT):
            for n in range(N):
                br += 32
                ev_br[("brl", j, n)] = br
        ev_xl = {}
        for j in range(NDT):
            ev_xl[j] = 16 * (j + 1)

        block = es.enter_context(nc.Block())

        @block.sync
        def _(eng):
            eng.dma_start(out=t_sp1[:, 0:DH], in_=dtwT).then_inc(s_in, 16)
            for j in range(NDT):
                eng.dma_start(out=t_dtb[:, j:j + 1],
                              in_=dtbias[j * 128:(j + 1) * 128, :]).then_inc(s_in, 16)
                eng.dma_start(out=t_alog[:, j * N:(j + 1) * N],
                              in_=Alog[j * 128:(j + 1) * 128, :]).then_inc(s_in, 16)
            eng.dma_start(out=t_dt[:], in_=dtin).then_inc(s_in, 16)
            eng.dma_start(out=t_sel[:], in_=selin).then_inc(s_in, 16)
            eng.dma_start(out=t_bc[:], in_=Bm).then_inc(s_bc, 16)
            eng.wait_ge(s_v, ev_v["castB"])
            eng.dma_start(out=t_bc[:], in_=Cm).then_inc(s_bc, 16)
            for j in range(NDT):
                if j >= 2:
                    eng.wait_ge(s_v, ev_v[("dx", j - 2)])
                eng.dma_start(out=t_xln[:, (j % 2) * L:(j % 2 + 1) * L],
                              in_=xln[j * 128:(j + 1) * 128, :]).then_inc(
                    s_xl0 if j % 2 == 0 else s_xl1, 16)

        @block.gpsimd
        def _(eng):
            for which in range(2):
                for n in range(N):
                    for half in range(2):
                        un = which * (N * 2) + n * 2 + half
                        eng.wait_ge(s_v, ev_v[("evac", un)])
                        srcap = t_u[:, half * HL:(half + 1) * HL]
                        dst = (brep_d if which == 0 else crep_d)[
                            n * 128:(n + 1) * 128, half * HL:(half + 1) * HL]
                        eng.dma_start(out=dst, in_=srcap).then_inc(
                            s_bs0 if un % 2 == 0 else s_bs1, 16)
            for j in range(NDT):
                for n in range(N):
                    sl = n % 2
                    idx = j * N + n
                    if j == 0 and n == 0:
                        eng.wait_ge(s_bs0, 16 * (N * 2))
                        eng.wait_ge(s_bs1, 16 * (N * 2))
                    if (j, n) > (0, 1):
                        pj, pn = (j, n - 2) if n >= 2 else (j - 1, N - 2 + n)
                        eng.wait_ge(s_v, ev_v[("u", pj, pn)])
                    eng.dma_start(out=t_brep[:, sl * L:(sl + 1) * L],
                                  in_=brep_d[n * 128:(n + 1) * 128, :]).then_inc(
                        s_br0 if sl == 0 else s_br1, 16)
                    if (j, n) != (0, 0):
                        pj, pn = (j, n - 1) if n >= 1 else (j - 1, N - 1)
                        eng.wait_ge(s_v, ev_v[("ym", pj, pn)])
                    eng.dma_start(out=t_crep[:],
                                  in_=crep_d[n * 128:(n + 1) * 128, :]).then_inc(s_brc, 16)
                eng.wait_ge(s_v, ev_v[("ysum", j)])
                eng.dma_start(out=yout[j * 128:(j + 1) * 128, :],
                              in_=t_da[:]).then_inc(s_y, 16)
            eng.wait_ge(s_y, 16 * NDT)

        @block.tensor
        def _(eng):
            eng.wait_ge(s_in, 16 * NPARAM)
            eng.wait_ge(s_v, ev_v["setup"])
            for which in range(2):
                if which == 1:
                    eng.wait_ge(s_v, ev_v["castC"])
                for n in range(N):
                    for half in range(2):
                        un = which * (N * 2) + n * 2 + half
                        pst = ps_a if un % 2 == 0 else ps_b
                        if un >= 2:
                            eng.wait_ge(s_v, ev_v[("evac", un - 2)])
                        last = None
                        for blk in range(4):
                            last = eng.matmul(
                                out=pst[:, blk * 512:(blk + 1) * 512],
                                lhsT=t_sel[:, n * 128:(n + 1) * 128],
                                rhs=t_bcb[:, half * HL + blk * 512:
                                          half * HL + (blk + 1) * 512],
                                start=True, stop=True)
                        last.then_inc(s_t, 1)
            for j in range(NDT):
                for half in range(2):
                    un = j * 2 + half
                    pst = ps_a if un % 2 == 0 else ps_b
                    if un < 2:
                        eng.wait_ge(s_v, ev_v[("evac", N * 4 - 2 + un)])
                    else:
                        eng.wait_ge(s_s, ev_s[("psfree", (un - 2) // 2, (un - 2) % 2)])
                    last = None
                    for blk in range(4):
                        last = eng.matmul(
                            out=pst[:, blk * 512:(blk + 1) * 512],
                            lhsT=t_dtwb[:, j * 128:(j + 1) * 128],
                            rhs=t_dt[:, half * HL + blk * 512:half * HL + (blk + 1) * 512],
                            start=True, stop=True)
                    last.then_inc(s_t, 1)

        @block.scalar
        def _(eng):
            eng.wait_ge(s_in, 16 * NPARAM)
            eng.activation(out=t_a[:], in_=t_alog[:], func=AF.Exp).then_inc(s_s, 1)
            for j in range(NDT):
                for half in range(2):
                    un = j * 2 + half
                    pst = ps_a if un % 2 == 0 else ps_b
                    eng.wait_ge(s_t, ev_t[("dtproj", j, half)])
                    if un >= 1:
                        eng.wait_ge(s_v, ev_v[("delta", (un - 1) // 2, (un - 1) % 2)])
                    bias_ap = t_dtb[:, j:j + 1]
                    eng.activation(out=t_sp1[:], in_=pst[:], func=AF.Abs,
                                   bias=bias_ap).then_inc(s_s, 1)
                    eng.activation(out=t_sp2[:], in_=pst[:], func=AF.Relu,
                                   bias=bias_ap).then_inc(s_s, 1)
                    eng.drain()
                    eng.activation(out=t_sp3[:], in_=t_sp1[:], func=AF.Exp,
                                   scale=-1.0).then_inc(s_s, 1)
                    eng.drain()
                    eng.activation(out=t_sp4[:], in_=t_sp3[:], func=AF.Ln,
                                   bias=1.0).then_inc(s_s, 1)
                for n in range(N):
                    if (j, n) != (0, 0):
                        pj, pn = (j, n - 1) if n >= 1 else (j - 1, N - 1)
                        eng.wait_ge(s_v, ev_v[("tts", pj, pn)])
                    if n == 0 and j > 0:
                        eng.wait_ge(s_v, ev_v[("ysum", j - 1)])
                        eng.wait_ge(s_y, 16 * j)
                    eng.wait_ge(s_v, ev_v[("dx", j)])
                    eng.activation(out=t_da[:], in_=t_delta[:], func=AF.Exp,
                                   scale=t_an[:, j * N + n:j * N + n + 1]).then_inc(s_s, 1)

        @block.vector
        def _(eng):
            eng.wait_ge(s_in, 16 * NPARAM)
            eng.tensor_copy(out=t_dtwb[:], in_=t_sp1[:, 0:DH]).then_inc(s_v, 1)
            eng.wait_ge(s_s, 1)
            eng.tensor_scalar(out=t_an[:], in0=t_a[:], scalar1=-1.0, scalar2=None,
                              op0=OP.mult).then_inc(s_v, 1)
            eng.wait_ge(s_bc, 16)
            eng.tensor_copy(out=t_bcb[:], in_=t_bc[:]).then_inc(s_v, 1)
            for which in range(2):
                for n in range(N):
                    for half in range(2):
                        un = which * (N * 2) + n * 2 + half
                        pst = ps_a if un % 2 == 0 else ps_b
                        eng.wait_ge(s_t, ev_t[("bld", un)])
                        if un >= 2:
                            eng.wait_ge(s_bs0 if un % 2 == 0 else s_bs1,
                                        16 * (un // 2))
                        eng.tensor_copy(out=t_u[:, half * HL:(half + 1) * HL],
                                        in_=pst[:]).then_inc(s_v, 1)
                        if un == N * 2 - 1:
                            eng.wait_ge(s_bc, 32)
                            eng.tensor_copy(out=t_bcb[:], in_=t_bc[:]).then_inc(s_v, 1)
            for j in range(NDT):
                for half in range(2):
                    eng.wait_ge(s_s, ev_s[("sp", j, half)])
                    if j > 0 and half == 0:
                        eng.wait_ge(s_s, ev_s[("da", j - 1, N - 1)])
                    eng.tensor_tensor(out=t_delta[:, half * HL:(half + 1) * HL],
                                      in0=t_sp2[:], in1=t_sp4[:], op=OP.add).then_inc(s_v, 1)
                eng.wait_ge(s_xl0 if j % 2 == 0 else s_xl1, 16 * (j // 2 + 1))
                eng.drain()
                eng.tensor_tensor(out=t_dx[:], in0=t_delta[:],
                                  in1=t_xln[:, (j % 2) * L:(j % 2 + 1) * L],
                                  op=OP.mult).then_inc(s_v, 1)
                eng.drain()
                for n in range(N):
                    sl = n % 2
                    idx = j * N + n
                    if j == 0 and n == 0:
                        eng.wait_ge(s_bs0, 16 * (N * 2))
                        eng.wait_ge(s_bs1, 16 * (N * 2))
                    eng.wait_ge(s_br0 if sl == 0 else s_br1, 16 * (idx // 2 + 1))
                    eng.drain()
                    eng.tensor_tensor(out=t_u[:], in0=t_dx[:],
                                      in1=t_brep[:, sl * L:(sl + 1) * L],
                                      op=OP.mult).then_inc(s_v, 1)
                    eng.wait_ge(s_s, ev_s[("da", j, n)])
                    eng.drain()
                    eng.tensor_tensor_scan(out=t_h[:], data0=t_da[:], data1=t_u[:],
                                           initial=0.0, op0=OP.mult,
                                           op1=OP.add).then_inc(s_v, 1)
                    ym_dst = t_p1 if n % 2 == 0 else t_yp
                    eng.wait_ge(s_brc, 16 * (idx + 1))
                    eng.drain()
                    eng.tensor_tensor(out=ym_dst[:], in0=t_h[:],
                                      in1=t_crep[:],
                                      op=OP.mult).then_inc(s_v, 1)
                    if n % 2 == 1:
                        k = n + 1
                        chain = 0
                        kk = k
                        while kk % 2 == 0 and kk > 1:
                            chain += 1
                            kk //= 2
                        bufs = [t_mg, t_u]
                        cur = None
                        lvslots = [t_p2, t_p3, t_p4]
                        for c in range(chain):
                            src0 = t_p1 if c == 0 else cur
                            src1 = t_yp if c == 0 else lvslots[c - 1]
                            if c == chain - 1:
                                if k == N:
                                    if j > 0:
                                        eng.wait_ge(s_y, 16 * j)
                                    dst = t_da
                                else:
                                    dst = lvslots[c]
                            else:
                                dst = bufs[c % 2]
                            eng.drain()
                            eng.tensor_tensor(out=dst[:], in0=src0[:], in1=src1[:],
                                              op=OP.add).then_inc(s_v, 1)
                            cur = dst

    return nc


def _scan_in_maps(inputs):
    raise NotImplementedError


def kernel(**inputs):
    raise NotImplementedError



class _DryInst:
    def __init__(self, cnt):
        self.cnt = cnt

    def then_inc(self, sem, k):
        self.cnt[id(sem)] = self.cnt.get(id(sem), 0) + k
        return self


class _DryEng:
    def __init__(self, cnt):
        self._cnt = cnt

    def __getattr__(self, name):
        if name in ("wait_ge", "drain"):
            return lambda *a, **k: None
        return lambda *a, **k: _DryInst(self._cnt)


class EvTab:

    def __init__(self):
        self.cnt = {}
        self.ev = {}
        self.dry = True

    def rec(self, sem, name):
        val = self.cnt.get(id(sem), 0)
        if self.dry:
            self.ev[name] = val
        return self.ev[name]

    def get(self, name):
        return self.ev[name]


def build_token_prog(phase):
    NT = TOK // 128
    NK = D // 128
    EPS_LN = 1e-5
    EPS_RMS = 1e-6
    EPS_Q = 1e-5
    nc = bass.Bass("TRN2", target_bir_lowering=False, debug=False)
    Win = nc.dram_tensor("Win", [D, D], F32, kind="ExternalInput").ap()
    gam = nc.dram_tensor("gam", [1, D], F32, kind="ExternalInput").ap()
    ident = nc.dram_tensor("ident", [128, 128], BF16, kind="ExternalInput").ap()
    xin = nc.dram_tensor("xin", [TOK, D], F32, kind="ExternalInput").ap()
    if phase == 1:
        r1o = nc.dram_tensor("r1o", [TOK, D], F32, kind="ExternalOutput").ap()
        xlnT = nc.dram_tensor("xlnT", [D, TOK], BF16, kind="ExternalOutput").ap()
        dtv = nc.dram_tensor("dtv", [RT, TOK], BF16, kind="ExternalOutput").ap()
        bcv = nc.dram_tensor("bcv", [2 * N, TOK], F32, kind="ExternalOutput").ap()
        xpwTin = nc.dram_tensor("xpwT", [D, RT + 2 * N], F32, kind="ExternalInput").ap()
    else:
        identf = nc.dram_tensor("identf", [128, 128], F32, kind="ExternalInput").ap()
        yTin = nc.dram_tensor("yTin", [D, TOK], F32, kind="ExternalInput").ap()
        dpar = nc.dram_tensor("dpar", [1, D], F32, kind="ExternalInput").ap()
        outo = nc.dram_tensor("outo", [TOK, D], F32, kind="ExternalOutput").ap()

    with ExitStack() as es:
        sbt = lambda name, shape, dt: es.enter_context(nc.sbuf_tensor(name, shape, dt))
        t_w = sbt("t_w", [128, 2 * D], F32)
        t_wq = sbt("t_wq", [128, NK * D], BF16)
        t_x = sbt("t_x", [128, 2 * D], F32)
        t_xln = sbt("t_xln", [128, D], F32)
        t_scr = sbt("t_scr", [128, D], F32)
        t_q = sbt("t_q", [128, D], F32)
        t_xq = sbt("t_xq", [128, D], BF16)
        t_xqT = sbt("t_xqT", [128, NK * 128], BF16)
        t_gam = sbt("t_gam", [128, D], F32)
        t_r1 = sbt("t_r1", [128, D], F32)
        t_st = sbt("t_st", [128, 24], F32)
        t_cst = sbt("t_cst", [128, 4], F32)
        t_ones = sbt("t_ones", [128, 128], F32)
        t_id = sbt("t_id", [128, 128], BF16)
        if phase == 1:
            t_xlb = sbt("t_xlb", [128, D], BF16)
            t_xt = sbt("t_xt", [128, NK * 128], BF16)
            t_xpw = sbt("t_xpw", [128, NK * (RT + 2 * N)], BF16)
            t_xpwf = sbt("t_xpwf", [128, RT + 2 * N], F32)
            t_pd = sbt("t_pd", [128, 128], BF16)
            t_pb = sbt("t_pb", [32, 128], F32)
        else:
            t_idf = sbt("t_idf", [128, 128], F32)
            t_y = sbt("t_y", [128, 2 * D], F32)
            t_yt = sbt("t_yt", [128, D], F32)
            t_r2 = sbt("t_r2", [128, D], F32)
            t_dr = sbt("t_dr", [128, D], F32)
        ps_m = es.enter_context(nc.psum_tensor("ps_m", [128, D], F32))
        ps_t1 = es.enter_context(nc.psum_tensor("ps_t1", [128, 128], BF16))
        ps_t2 = es.enter_context(nc.psum_tensor("ps_t2", [128, 128], BF16))
        if phase == 1:
            ps_p1 = es.enter_context(nc.psum_tensor("ps_p1", [128, 128], F32))
            ps_p2 = es.enter_context(nc.psum_tensor("ps_p2", [32, 128], F32))
        else:
            ps_f1 = es.enter_context(nc.psum_tensor("ps_f1", [128, 128], F32))
            ps_f2 = es.enter_context(nc.psum_tensor("ps_f2", [128, 128], F32))

        sems = {}
        for nm in ["s_wa0", "s_wa1", "s_wb0", "s_wb1", "s_x0", "s_x1", "s_g",
                   "s_xp", "s_t", "s_s", "s_v", "s_g2", "s_or", "s_odt", "s_ob",
                   "s_y0", "s_y1"]:
            sems[nm] = es.enter_context(nc.semaphore(nm))
        (s_wa0, s_wa1, s_wb0, s_wb1, s_x0, s_x1, s_g, s_xp, s_t, s_s, s_v,
         s_g2, s_or, s_odt, s_ob, s_y0, s_y1) = (
            sems[k] for k in ["s_wa0", "s_wa1", "s_wb0", "s_wb1", "s_x0", "s_x1",
                              "s_g", "s_xp", "s_t", "s_s", "s_v", "s_g2", "s_or",
                              "s_odt", "s_ob", "s_y0", "s_y1"])

        tab = EvTab()

        def ix(sem):
            return tab.cnt.get(id(sem), 0)

        def em_sync(eng, E):
            eng.dma_start(out=t_gam[0:1, :], in_=gam).then_inc(s_g, 16)
            eng.dma_start(out=t_id[:], in_=ident).then_inc(s_g, 16)
            if phase == 3:
                eng.dma_start(out=t_dr[0:1, :], in_=dpar).then_inc(s_g, 16)
                eng.dma_start(out=t_idf[:], in_=identf).then_inc(s_g, 16)
            if phase == 1:
                for k in range(NK):
                    if k >= 1:
                        eng.wait_ge(s_v, E(("xpc", k - 1)))
                    eng.dma_start(out=t_xpwf[:],
                                  in_=xpwTin[k * 128:(k + 1) * 128, :]).then_inc(s_xp, 16)
            for k in range(NK):
                if k >= 2:
                    eng.wait_ge(s_v, E(("wstat", k - 2)))
                eng.dma_start(out=t_w[:, (k % 2) * D:(k % 2 + 1) * D],
                              in_=Win[k * 128:(k + 1) * 128, :]).then_inc(
                    s_wa0 if k % 2 == 0 else s_wa1, 16)
            for k in range(NK):
                eng.wait_ge(s_v, E(("wstat", NK - 2 + k)) if k < 2 else E(("wq", k - 2)))
                eng.dma_start(out=t_w[:, (k % 2) * D:(k % 2 + 1) * D],
                              in_=Win[k * 128:(k + 1) * 128, :]).then_inc(
                    s_wb0 if k % 2 == 0 else s_wb1, 16)
            for i in range(NT):
                if i >= 2:
                    eng.wait_ge(s_v, E(("ep", i - 2)))
                eng.dma_start(out=t_x[:, (i % 2) * D:(i % 2 + 1) * D],
                              in_=xin[i * 128:(i + 1) * 128, :]).then_inc(
                    s_x0 if i % 2 == 0 else s_x1, 16)

        def em_gp_full(eng, E):
            if phase == 3:
                for i in range(NT + 1):
                    if i < NT:
                        if i >= 2:
                            eng.wait_ge(s_v, E(("r2", i - 2)))
                        for b in range(NK):
                            eng.dma_start(
                                out=t_y[:, (i % 2) * D + b * 128:(i % 2) * D + (b + 1) * 128],
                                in_=yTin[b * 128:(b + 1) * 128, i * 128:(i + 1) * 128]
                            ).then_inc(s_y0 if i % 2 == 0 else s_y1, 16)
                    if i >= 1:
                        eng.wait_ge(s_v, E(("ep", i - 1)))
                        eng.dma_start(out=outo[(i - 1) * 128:i * 128, :],
                                      in_=t_r1[:]).then_inc(s_or, 16)
            else:
                for i in range(NT):
                    eng.wait_ge(s_v, E(("ep", i)))
                    eng.dma_start(out=r1o[i * 128:(i + 1) * 128, :],
                                  in_=t_r1[:]).then_inc(s_or, 16)
                    for b in range(NK):
                        eng.wait_ge(s_v, E(("xtev", i, b)))
                        eng.dma_start(
                            out=xlnT[b * 128:(b + 1) * 128, i * 128:(i + 1) * 128],
                            in_=t_xt[:, b * 128:(b + 1) * 128]).then_inc(s_ob, 16)
                    eng.wait_ge(s_v, E(("xp", i)))
                    eng.dma_start(out=dtv[:, i * 128:(i + 1) * 128],
                                  in_=t_pd[:]).then_inc(s_odt, 16)
                    eng.dma_start(out=bcv[:, i * 128:(i + 1) * 128],
                                  in_=t_pb[:]).then_inc(s_odt, 16)
            eng.wait_ge(s_or, 16 * NT)
            if phase == 1:
                eng.wait_ge(s_ob, 16 * NT * NK)
                eng.wait_ge(s_odt, 32 * NT)

        def em_tensor(eng, E):
            ps_rep = ps_p1 if phase == 1 else ps_f1
            eng.wait_ge(s_g, 16 * (4 if phase == 3 else 2))
            eng.wait_ge(s_v, E("ones"))
            nrep = NK * (2 if phase == 3 else 1)
            for blk in range(nrep):
                src_row = t_gam if blk < NK else t_dr
                cb = (blk % NK) * 128
                if blk > 0:
                    eng.wait_ge(s_v, E(("rev", blk - 1)))
                eng.matmul(out=ps_rep[:], lhsT=t_ones[0:1, :],
                           rhs=src_row[0:1, cb:cb + 128],
                           start=True, stop=True).then_inc(s_t, 1)
                tab.rec(s_t, ("rep", blk))
            eng.wait_ge(s_v, E(("wstat", NK - 1)))
            eng.wait_ge(s_v, E(("rev", nrep - 1)))
            eng.matmul(out=ps_rep[:, 0:1], lhsT=t_ones[:],
                       rhs=t_st[:, 18:19], start=True, stop=True).then_inc(s_t, 1)
            tab.rec(s_t, ("wsum",))
            for k in range(NK):
                eng.wait_ge(s_v, E(("wq3", k)))
                for b in range(NK):
                    pst = ps_t1 if (k * NK + b) % 2 == 0 else ps_t2
                    if k * NK + b >= 2:
                        eng.wait_ge(s_v, E(("wev", (k * NK + b - 2) // NK,
                                            (k * NK + b - 2) % NK)))
                    eng.transpose(out=pst[:], in_=t_xq[:, b * 128:(b + 1) * 128],
                                  identity=t_id[:]).then_inc(s_t, 1)
                    tab.rec(s_t, ("wtr", k, b))
            for i in range(NT):
                if phase == 3:
                    eng.wait_ge(s_y0 if i % 2 == 0 else s_y1, 16 * NK * (i // 2 + 1))
                    for b in range(NK):
                        un = i * NK + b
                        pst = ps_f1 if un % 2 == 0 else ps_f2
                        eng.wait_ge(s_v, E(("yev_prev", i, b)))
                        eng.transpose(out=pst[:],
                                      in_=t_y[:, (i % 2) * D + b * 128:(i % 2) * D + (b + 1) * 128],
                                      identity=t_idf[:]).then_inc(s_t, 1)
                        tab.rec(s_t, ("ytr", i, b))
                eng.wait_ge(s_v, E(("xq", i)))
                for b in range(NK):
                    un = i * NK + b
                    pst = ps_t1 if un % 2 == 0 else ps_t2
                    eng.wait_ge(s_v, E(("xqev_prev", i, b)))
                    eng.transpose(out=pst[:], in_=t_xq[:, b * 128:(b + 1) * 128],
                                  identity=t_id[:]).then_inc(s_t, 1)
                    tab.rec(s_t, ("xqtr", i, b))
                eng.wait_ge(s_v, E(("xqev", i, NK - 1)))
                if i > 0:
                    eng.wait_ge(s_v, E(("ep", i - 1)))
                for ob in range(4):
                    last = None
                    for k in range(NK):
                        last = eng.matmul(out=ps_m[:, ob * 512:(ob + 1) * 512],
                                          lhsT=t_xqT[:, k * 128:(k + 1) * 128],
                                          rhs=t_wq[:, k * D + ob * 512:k * D + (ob + 1) * 512],
                                          start=(k == 0), stop=(k == NK - 1))
                    last.then_inc(s_t, 1)
                tab.rec(s_t, ("mm", i))
                if phase == 1:
                    eng.wait_ge(s_v, E(("xlb", i)))
                    for b in range(NK):
                        un = i * NK + b
                        pst = ps_t1 if un % 2 == 0 else ps_t2
                        eng.wait_ge(s_v, E(("xtev_prev", i, b)))
                        eng.transpose(out=pst[:], in_=t_xlb[:, b * 128:(b + 1) * 128],
                                      identity=t_id[:]).then_inc(s_t, 1)
                        tab.rec(s_t, ("xltr", i, b))
                    if i > 0:
                        eng.wait_ge(s_v, E(("xp", i - 1)))
                    else:
                        eng.wait_ge(s_v, E(("wsev",)))
                    lastd = lastb = None
                    for k in range(NK):
                        lastd = eng.matmul(out=ps_p1[:],
                                           lhsT=t_xpw[:, k * 160:k * 160 + 128],
                                           rhs=t_xt[:, k * 128:(k + 1) * 128],
                                           start=(k == 0), stop=(k == NK - 1))
                        lastb = eng.matmul(out=ps_p2[:],
                                           lhsT=t_xpw[:, k * 160 + 128:k * 160 + 160],
                                           rhs=t_xt[:, k * 128:(k + 1) * 128],
                                           start=(k == 0), stop=(k == NK - 1))
                    lastd.then_inc(s_t, 1)
                    lastb.then_inc(s_t, 1)
                    tab.rec(s_t, ("xpt", i))

        def em_scalar(eng, E):
            c_eps = t_cst[:, 0:1]
            c_rms = t_cst[:, 1:2]
            c_invd = t_cst[:, 2:3]
            c_half = t_cst[:, 3:4]
            for i in range(NT):
                eng.wait_ge(s_v, E(("var1", i)))
                eng.activation(out=t_st[:, 5:6], in_=t_st[:, 3:4], func=AF.Ln,
                               bias=c_eps).then_inc(s_s, 1)
                eng.drain()
                eng.activation(out=t_st[:, 6:7], in_=t_st[:, 5:6], func=AF.Exp,
                               scale=c_half).then_inc(s_s, 1)
                tab.rec(s_s, ("rstd1", i))
                if phase == 3:
                    eng.wait_ge(s_v, E(("var2", i)))
                    eng.drain()
                    eng.activation(out=t_st[:, 12:13], in_=t_st[:, 11:12], func=AF.Ln,
                                   bias=c_eps).then_inc(s_s, 1)
                    eng.drain()
                    eng.activation(out=t_st[:, 13:14], in_=t_st[:, 12:13], func=AF.Exp,
                                   scale=c_half).then_inc(s_s, 1)
                    tab.rec(s_s, ("rstd2", i))
                eng.wait_ge(s_v, E(("ssq2", i)))
                eng.drain()
                eng.activation(out=t_st[:, 8:9], in_=t_st[:, 7:8], func=AF.Ln,
                               scale=c_invd, bias=c_rms).then_inc(s_s, 1)
                eng.drain()
                eng.activation(out=t_st[:, 9:10], in_=t_st[:, 8:9], func=AF.Exp,
                               scale=c_half).then_inc(s_s, 1)
                tab.rec(s_s, ("rms2", i))
                if phase == 1:
                    eng.wait_ge(s_v, E(("var2", i)))
                    eng.drain()
                    eng.activation(out=t_st[:, 12:13], in_=t_st[:, 11:12], func=AF.Ln,
                                   bias=c_eps).then_inc(s_s, 1)
                    eng.drain()
                    eng.activation(out=t_st[:, 13:14], in_=t_st[:, 12:13], func=AF.Exp,
                                   scale=c_half).then_inc(s_s, 1)
                    tab.rec(s_s, ("rstd2", i))

        def ln_stats(eng, src_ap, sum_c, ssq_c, negm_c, var_c, i, first_wait=None):
            if first_wait is not None:
                eng.wait_ge(*first_wait)
            eng.tensor_reduce(out=t_st[:, sum_c:sum_c + 1], in_=src_ap,
                              axis=AX.X, op=OP.add).then_inc(s_v, 1)
            eng.scalar_tensor_tensor(out=t_scr[:], in0=src_ap, scalar=1.0,
                                     in1=src_ap, op0=OP.mult, op1=OP.mult,
                                     accum_out=t_st[:, ssq_c:ssq_c + 1]).then_inc(s_v, 1)
            eng.drain()
            eng.tensor_scalar(out=t_st[:, negm_c:negm_c + 1],
                              in0=t_st[:, sum_c:sum_c + 1],
                              scalar1=float(-1.0 / D), scalar2=None,
                              op0=OP.mult).then_inc(s_v, 1)
            eng.drain()
            eng.tensor_scalar(out=t_st[:, var_c + 1:var_c + 2],
                              in0=t_st[:, negm_c:negm_c + 1],
                              scalar1=t_st[:, negm_c:negm_c + 1], scalar2=None,
                              op0=OP.mult).then_inc(s_v, 1)
            eng.drain()
            eng.scalar_tensor_tensor(out=t_st[:, var_c:var_c + 1],
                                     in0=t_st[:, ssq_c:ssq_c + 1],
                                     scalar=float(1.0 / D),
                                     in1=t_st[:, var_c + 1:var_c + 2],
                                     op0=OP.mult, op1=OP.subtract).then_inc(s_v, 1)

        def em_vector(eng, E):
            eng.memset(t_cst[:, 0:1], float(EPS_LN))
            eng.memset(t_cst[:, 1:2], float(EPS_RMS))
            eng.memset(t_cst[:, 2:3], float(1.0 / D))
            eng.memset(t_cst[:, 3:4], -0.5)
            eng.memset(t_ones[:], 1.0).then_inc(s_v, 1)
            tab.rec(s_v, "ones")
            eng.wait_ge(s_g, 16 * (4 if phase == 3 else 2))
            ps_rep = ps_p1 if phase == 1 else ps_f1
            nrep = NK * (2 if phase == 3 else 1)
            for blk in range(nrep):
                dst_row = t_gam if blk < NK else t_dr
                cb = (blk % NK) * 128
                eng.wait_ge(s_t, E(("rep", blk)))
                eng.tensor_copy(out=dst_row[:, cb:cb + 128],
                                in_=ps_rep[:]).then_inc(s_v, 1)
                tab.rec(s_v, ("rev", blk))
            if phase == 1:
                for k in range(NK):
                    eng.wait_ge(s_xp, 16 * (k + 1))
                    eng.tensor_copy(out=t_xpw[:, k * 160:(k + 1) * 160],
                                    in_=t_xpwf[:]).then_inc(s_v, 1)
                    tab.rec(s_v, ("xpc", k))
            for k in range(NK):
                eng.wait_ge(s_wa0 if k % 2 == 0 else s_wa1, 16 * (k // 2 + 1))
                eng.tensor_reduce(out=t_st[:, 16 + (k % 2):17 + (k % 2)],
                                  in_=t_w[:, (k % 2) * D:(k % 2 + 1) * D],
                                  axis=AX.X, op=OP.add,
                                  apply_absolute_value=True).then_inc(s_v, 1)
                eng.drain()
                if k == 0:
                    eng.tensor_copy(out=t_st[:, 18:19],
                                    in_=t_st[:, 16:17]).then_inc(s_v, 1)
                else:
                    eng.tensor_tensor(out=t_st[:, 19:20], in0=t_st[:, 18:19],
                                      in1=t_st[:, 16 + (k % 2):17 + (k % 2)],
                                      op=OP.add).then_inc(s_v, 1)
                    eng.drain()
                    eng.tensor_copy(out=t_st[:, 18:19],
                                    in_=t_st[:, 19:20]).then_inc(s_v, 1)
                tab.rec(s_v, ("wstat", k))
            eng.drain()
            eng.wait_ge(s_t, E(("wsum",)))
            eng.tensor_copy(out=t_st[:, 18:19], in_=ps_rep[:, 0:1]).then_inc(s_v, 1)
            tab.rec(s_v, ("wsev",))
            eng.drain()
            eng.tensor_scalar(out=t_st[:, 20:21], in0=t_st[:, 18:19],
                              scalar1=float(1.0 / (D * D)), scalar2=float(EPS_Q),
                              op0=OP.mult, op1=OP.add).then_inc(s_v, 1)
            eng.drain()
            eng.reciprocal(out=t_st[:, 21:22], in_=t_st[:, 20:21]).then_inc(s_v, 1)
            eng.drain()
            tab.rec(s_v, ("ws",))
            for k in range(NK):
                eng.wait_ge(s_wb0 if k % 2 == 0 else s_wb1, 16 * (k // 2 + 1))
                eng.tensor_scalar(out=t_q[:], in0=t_w[:, (k % 2) * D:(k % 2 + 1) * D],
                                  scalar1=t_st[:, 21:22], scalar2=MAGIC,
                                  op0=OP.mult, op1=OP.add).then_inc(s_v, 1)
                eng.drain()
                eng.tensor_scalar(out=t_scr[:], in0=t_q[:],
                                  scalar1=-MAGIC, scalar2=1.0,
                                  op0=OP.add, op1=OP.min).then_inc(s_v, 1)
                eng.drain()
                tab.rec(s_v, ("wq", k))
                if k >= 1:
                    eng.wait_ge(s_t, E(("wtr", k - 1, NK - 1)))
                eng.tensor_scalar(out=t_xq[:], in0=t_scr[:],
                                  scalar1=-1.0, scalar2=None,
                                  op0=OP.max).then_inc(s_v, 1)
                tab.rec(s_v, ("wq3", k))
                for b in range(NK):
                    un = k * NK + b
                    pst = ps_t1 if un % 2 == 0 else ps_t2
                    eng.wait_ge(s_t, E(("wtr", k, b)))
                    eng.tensor_copy(out=t_wq[:, b * D + k * 128:b * D + (k + 1) * 128],
                                    in_=pst[:]).then_inc(s_v, 1)
                    tab.rec(s_v, ("wev", k, b))
            for i in range(NT):
                sx = s_x0 if i % 2 == 0 else s_x1
                xs = t_x[:, (i % 2) * D:(i % 2 + 1) * D]
                ln_stats(eng, xs, 0, 1, 2, 3, i,
                         first_wait=(sx, 16 * (i // 2 + 1)))
                tab.rec(s_v, ("var1", i))
                eng.wait_ge(s_s, E(("rstd1", i)))
                eng.drain()
                eng.tensor_scalar(out=t_xln[:], in0=xs,
                                  scalar1=t_st[:, 2:3], scalar2=t_st[:, 6:7],
                                  op0=OP.add, op1=OP.mult).then_inc(s_v, 1)
                tab.rec(s_v, ("xln1", i))
                if phase == 3:
                    for b in range(NK):
                        un = i * NK + b
                        pst = ps_f1 if un % 2 == 0 else ps_f2
                        tab.rec(s_v, ("yev_prev", i, b))
                        eng.wait_ge(s_t, E(("ytr", i, b)))
                        eng.tensor_copy(out=t_yt[:, b * 128:(b + 1) * 128],
                                        in_=pst[:]).then_inc(s_v, 1)
                        tab.rec(s_v, ("yev", i, b))
                    eng.drain()
                    eng.tensor_tensor(out=t_scr[:], in0=t_xln[:], in1=t_dr[:],
                                      op=OP.mult).then_inc(s_v, 1)
                    eng.drain()
                    eng.tensor_tensor(out=t_r2[:], in0=xs, in1=t_yt[:],
                                      op=OP.add).then_inc(s_v, 1)
                    eng.drain()
                    eng.tensor_tensor(out=t_yt[:], in0=t_r2[:], in1=t_scr[:],
                                      op=OP.add).then_inc(s_v, 1)
                    tab.rec(s_v, ("r2", i))
                    eng.drain()
                    ln_stats(eng, t_yt[:], 0, 10, 14, 11, i)
                    tab.rec(s_v, ("var2", i))
                    eng.wait_ge(s_s, E(("rstd2", i)))
                    eng.drain()
                    eng.tensor_scalar(out=t_xln[:], in0=t_yt[:],
                                      scalar1=t_st[:, 14:15], scalar2=t_st[:, 13:14],
                                      op0=OP.add, op1=OP.mult).then_inc(s_v, 1)
                resid = xs if phase == 1 else t_yt[:]
                eng.drain()
                eng.scalar_tensor_tensor(out=t_scr[:], in0=t_xln[:], scalar=1.0,
                                         in1=t_xln[:], op0=OP.mult, op1=OP.mult,
                                         accum_out=t_st[:, 7:8]).then_inc(s_v, 1)
                tab.rec(s_v, ("ssq2", i))
                eng.wait_ge(s_s, E(("rms2", i)))
                eng.drain()
                eng.scalar_tensor_tensor(out=t_scr[:], in0=t_xln[:],
                                         scalar=t_st[:, 9:10], in1=t_gam[:],
                                         op0=OP.mult, op1=OP.mult).then_inc(s_v, 1)
                eng.drain()
                eng.tensor_reduce(out=t_st[:, 15:16], in_=t_scr[:], axis=AX.X,
                                  op=OP.max, apply_absolute_value=True).then_inc(s_v, 1)
                eng.drain()
                eng.tensor_scalar(out=t_st[:, 22:23], in0=t_st[:, 15:16],
                                  scalar1=float(EPS_Q), scalar2=None,
                                  op0=OP.add).then_inc(s_v, 1)
                eng.drain()
                eng.reciprocal(out=t_st[:, 23:24], in_=t_st[:, 22:23]).then_inc(s_v, 1)
                eng.drain()
                eng.tensor_scalar(out=t_st[:, 10:11], in0=t_st[:, 23:24],
                                  scalar1=127.0, scalar2=None,
                                  op0=OP.mult).then_inc(s_v, 1)
                eng.tensor_scalar(out=t_st[:, 11:12], in0=t_st[:, 22:23],
                                  scalar1=float(0.5 / 127.0), scalar2=t_st[:, 20:21],
                                  op0=OP.mult, op1=OP.mult).then_inc(s_v, 1)
                tab.rec(s_v, ("qs", i))
                eng.drain()
                eng.tensor_scalar(out=t_q[:], in0=t_scr[:],
                                  scalar1=t_st[:, 10:11], scalar2=MAGIC,
                                  op0=OP.mult, op1=OP.add).then_inc(s_v, 1)
                eng.drain()
                if i > 0:
                    eng.wait_ge(s_t, E(("xqtr", i - 1, NK - 1)))
                else:
                    eng.wait_ge(s_t, E(("wtr", NK - 1, NK - 1)))
                eng.tensor_scalar(out=t_xq[:], in0=t_q[:],
                                  scalar1=-MAGIC, scalar2=None,
                                  op0=OP.add).then_inc(s_v, 1)
                tab.rec(s_v, ("xq", i))
                for b in range(NK):
                    un = i * NK + b
                    pst = ps_t1 if un % 2 == 0 else ps_t2
                    tab.rec(s_v, ("xqev_prev", i, b))
                    eng.wait_ge(s_t, E(("xqtr", i, b)))
                    eng.tensor_copy(out=t_xqT[:, b * 128:(b + 1) * 128],
                                    in_=pst[:]).then_inc(s_v, 1)
                    tab.rec(s_v, ("xqev", i, b))
                eng.wait_ge(s_t, E(("mm", i)))
                if i > 0:
                    eng.wait_ge(s_or, 16 * i)
                eng.drain()
                eng.scalar_tensor_tensor(out=t_r1[:], in0=ps_m[:],
                                         scalar=t_st[:, 11:12], in1=resid,
                                         op0=OP.mult, op1=OP.add).then_inc(s_v, 1)
                tab.rec(s_v, ("ep", i))
                if phase == 1:
                    eng.drain()
                    ln_stats(eng, t_r1[:], 0, 10, 14, 11, i)
                    tab.rec(s_v, ("var2", i))
                    eng.wait_ge(s_s, E(("rstd2", i)))
                    eng.drain()
                    eng.tensor_scalar(out=t_xln[:], in0=t_r1[:],
                                      scalar1=t_st[:, 14:15], scalar2=t_st[:, 13:14],
                                      op0=OP.add, op1=OP.mult).then_inc(s_v, 1)
                    eng.drain()
                    eng.tensor_copy(out=t_xlb[:], in_=t_xln[:]).then_inc(s_v, 1)
                    tab.rec(s_v, ("xlb", i))
                    for b in range(NK):
                        un = i * NK + b
                        pst = ps_t1 if un % 2 == 0 else ps_t2
                        tab.rec(s_v, ("xtev_prev", i, b))
                        eng.wait_ge(s_t, E(("xltr", i, b)))
                        if b == 0 and i > 0:
                            eng.wait_ge(s_ob, 16 * NK * i)
                        eng.tensor_copy(out=t_xt[:, b * 128:(b + 1) * 128],
                                        in_=pst[:]).then_inc(s_v, 1)
                        tab.rec(s_v, ("xtev", i, b))
                    eng.wait_ge(s_t, E(("xpt", i)))
                    if i > 0:
                        eng.wait_ge(s_odt, 32 * i)
                    eng.tensor_copy(out=t_pd[:], in_=ps_p1[:]).then_inc(s_v, 1)
                    eng.tensor_copy(out=t_pb[:], in_=ps_p2[:]).then_inc(s_v, 1)
                    tab.rec(s_v, ("xp", i))

        dry = _DryEng(tab.cnt)
        E = lambda name: tab.ev.get(name, 0) if tab.dry else tab.ev[name]
        for _ in range(2):
            tab.cnt.clear()
            em_vector(dry, E)
            em_scalar(dry, E)
            em_tensor(dry, E)
            em_sync(dry, E)
            em_gp_full(dry, E)
        tab.dry = False

        block = es.enter_context(nc.Block())
        tab.cnt.clear()

        @block.sync
        def _(eng):
            em_sync(eng, E)

        @block.gpsimd
        def _(eng):
            em_gp_full(eng, E)

        @block.tensor
        def _(eng):
            em_tensor(eng, E)

        @block.scalar
        def _(eng):
            em_scalar(eng, E)

        @block.vector
        def _(eng):
            em_vector(eng, E)

    return nc



def kernel_unfused(L0_attn_x, L0_attn_A_log, L0_attn_D_param, L0_attn_x_proj_w,
           L0_attn_dt_proj_w, L0_attn_dt_proj_b, L0_ffn_gamma, L0_ffn_W):
    f32 = np.float32
    x = np.ascontiguousarray(np.asarray(L0_attn_x, dtype=f32))
    W = np.ascontiguousarray(np.asarray(L0_ffn_W, dtype=f32))
    gamma = np.asarray(L0_ffn_gamma, dtype=f32)
    A_log = np.ascontiguousarray(np.asarray(L0_attn_A_log, dtype=f32))
    D_par = np.asarray(L0_attn_D_param, dtype=f32)
    xpw = np.asarray(L0_attn_x_proj_w, dtype=f32)
    dtw = np.asarray(L0_attn_dt_proj_w, dtype=f32)
    dtb = np.asarray(L0_attn_dt_proj_b, dtype=f32)
    ident_b = np.eye(128, dtype=f32).astype(bf16)
    ident_f = np.eye(128, dtype=f32)
    selmat = np.repeat(np.eye(N, dtype=f32), 128, axis=1).astype(bf16)
    xpwT = np.ascontiguousarray(xpw.T)

    nc1 = build_token_prog(1)
    im1 = []
    for c in range(8):
        b, h = c >> 1, c & 1
        im1.append({
            "Win": W, "gam": gamma[None, :], "ident": ident_b, "xpwT": xpwT,
            "xin": np.ascontiguousarray(x[b, h * TOK:(h + 1) * TOK, :]),
        })
    r1 = run_bass_kernel_spmd(nc1, im1, core_ids=list(range(8)))
    R1 = [r1.results[c]["r1o"] for c in range(8)]
    XLT = [r1.results[c]["xlnT"] for c in range(8)]
    DTV = [r1.results[c]["dtv"] for c in range(8)]
    BCV = [r1.results[c]["bcv"] for c in range(8)]

    nc2 = build_scan_prog()
    im2 = []
    for c in range(8):
        b, r = c >> 1, c & 1
        rs = slice(r * DH, (r + 1) * DH)
        im2.append({
            "xln": np.ascontiguousarray(
                np.concatenate([XLT[2 * b][rs, :], XLT[2 * b + 1][rs, :]], axis=1)),
            "dt": np.ascontiguousarray(
                np.concatenate([DTV[2 * b], DTV[2 * b + 1]], axis=1)),
            "Bm": np.ascontiguousarray(
                np.concatenate([BCV[2 * b][:N, :], BCV[2 * b + 1][:N, :]], axis=1)),
            "Cm": np.ascontiguousarray(
                np.concatenate([BCV[2 * b][N:, :], BCV[2 * b + 1][N:, :]], axis=1)),
            "dtwT": np.ascontiguousarray(dtw[rs, :].T),
            "dtbias": np.ascontiguousarray(dtb[rs, None]),
            "Alog": np.ascontiguousarray(A_log[rs, :]),
            "sel": selmat,
        })
    r2 = run_bass_kernel_spmd(nc2, im2, core_ids=list(range(8)))
    Y = [r2.results[c]["yout"] for c in range(8)]

    nc3 = build_token_prog(3)
    im3 = []
    for c in range(8):
        b, h = c >> 1, c & 1
        ts = slice(h * TOK, (h + 1) * TOK)
        im3.append({
            "Win": W, "gam": gamma[None, :], "ident": ident_b, "identf": ident_f,
            "xin": np.ascontiguousarray(R1[c]),
            "yTin": np.ascontiguousarray(
                np.concatenate([Y[2 * b][:, ts], Y[2 * b + 1][:, ts]], axis=0)),
            "dpar": D_par[None, :],
        })
    r3 = run_bass_kernel_spmd(nc3, im3, core_ids=list(range(8)))
    out = np.empty((B, L, D), dtype=f32)
    for c in range(8):
        b, h = c >> 1, c & 1
        out[b, h * TOK:(h + 1) * TOK, :] = r3.results[c]["outo"]
    return out



def build_fused():
    NT = TOK // 128
    NK = D // 128
    EPS_LN, EPS_RMS, EPS_Q = 1e-5, 1e-6, 1e-5
    nc = bass.Bass("TRN2", target_bir_lowering=False, debug=False)
    xin = nc.dram_tensor("xin", [TOK, D], F32, kind="ExternalInput").ap()
    Win = nc.dram_tensor("Win", [D, D], F32, kind="ExternalInput").ap()
    gam = nc.dram_tensor("gam", [1, D], F32, kind="ExternalInput").ap()
    ident = nc.dram_tensor("ident", [128, 128], BF16, kind="ExternalInput").ap()
    identf = nc.dram_tensor("identf", [128, 128], F32, kind="ExternalInput").ap()
    xpwTin = nc.dram_tensor("xpwT", [D, RT + 2 * N], F32, kind="ExternalInput").ap()
    dtwT = nc.dram_tensor("dtwT", [RT, DH], F32, kind="ExternalInput").ap()
    dtbias = nc.dram_tensor("dtbias", [DH, 1], F32, kind="ExternalInput").ap()
    Alog = nc.dram_tensor("Alog", [DH, N], F32, kind="ExternalInput").ap()
    selin = nc.dram_tensor("sel", [N, N * 128], BF16, kind="ExternalInput").ap()
    dpar = nc.dram_tensor("dpar", [1, D], F32, kind="ExternalInput").ap()
    mask = nc.dram_tensor("mask", [128, 2], F32, kind="ExternalInput").ap()
    outo = nc.dram_tensor("outo", [TOK, D], F32, kind="ExternalOutput").ap()
    r1d = nc.dram_tensor("r1d", [TOK, D], F32, kind="Internal").ap()
    wqd = nc.dram_tensor("wqd", [D, D], BF16, kind="Internal").ap()
    xci_h0 = nc.dram_tensor("xci_h0", [D, TOK // 2], BF16, kind="Internal").ap()
    xci_h1 = nc.dram_tensor("xci_h1", [D, TOK // 2], BF16, kind="Internal").ap()
    xco_h0 = nc.dram_tensor("xco_h0", [2, D, TOK // 2], BF16, kind="Internal").ap()
    xco_h1 = nc.dram_tensor("xco_h1", [2, D, TOK // 2], BF16, kind="Internal").ap()
    dtci = nc.dram_tensor("dtci", [RT, TOK], BF16, kind="Internal").ap()
    dtco = nc.dram_tensor("dtco", [2, RT, TOK], BF16, kind="Internal").ap()
    bcci = nc.dram_tensor("bcci", [2 * N, TOK], F32, kind="Internal").ap()
    bcco = nc.dram_tensor("bcco", [2, 2 * N, TOK], F32, kind="Internal").ap()
    yci_a = nc.dram_tensor("yci_a", [DH // 2, L], BF16, kind="Internal").ap()
    yci_b = nc.dram_tensor("yci_b", [DH // 2, L], BF16, kind="Internal").ap()
    yco_a = nc.dram_tensor("yco_a", [2, DH // 2, L], BF16, kind="Internal").ap()
    yco_b = nc.dram_tensor("yco_b", [2, DH // 2, L], BF16, kind="Internal").ap()
    brep_d = nc.dram_tensor("brep_d", [N * 128, L], BF16, kind="Internal").ap()
    crep_d = nc.dram_tensor("crep_d", [N * 128, L], BF16, kind="Internal").ap()
    RG = [[0, 1], [2, 3], [4, 5], [6, 7]]

    with ExitStack() as ces:
        csb = lambda name, shape, dt: ces.enter_context(nc.sbuf_tensor(name, shape, dt))
        t_cst = csb("t_cst", [128, 6], F32)
        t_st = csb("t_st", [128, 48], F32)
        t_msk = csb("t_msk", [128, 2], F32)
        s_cc = ces.enter_context(nc.semaphore("s_cc"))
        s_wqs = ces.enter_context(nc.semaphore("s_wqs"))
        s_r1s = ces.enter_context(nc.semaphore("s_r1s"))
        s_ob2 = ces.enter_context(nc.semaphore("s_ob2"))
        s_odt2 = ces.enter_context(nc.semaphore("s_odt2"))
        s_ys = ces.enter_context(nc.semaphore("s_ys"))

        block = ces.enter_context(nc.Block())
        tab = EvTab()
        E = lambda name: tab.ev.get(name, 0) if tab.dry else tab.ev[name]
        dry = _DryEng(tab.cnt)

        def token_phase(P, es):
            ph = P["phase"]
            sbt = lambda name, shape, dt: es.enter_context(
                nc.sbuf_tensor(f"p{ph}_{name}", shape, dt))
            pre = f"p{ph}_"
            t_w = sbt("t_w", [128, 2 * D], F32) if ph == 1 else None
            t_wq = sbt("t_wq", [128, NK * D], BF16)
            t_x = sbt("t_x", [128, 2 * D], F32)
            t_xln = sbt("t_xln", [128, D], F32)
            t_scr = sbt("t_scr", [128, D], F32)
            t_q = sbt("t_q", [128, D], F32)
            t_xq = sbt("t_xq", [128, D], BF16)
            t_xqT = sbt("t_xqT", [128, NK * 128], BF16)
            t_gam = sbt("t_gam", [128, D], F32)
            t_r1 = sbt("t_r1", [128, D], F32)
            t_id = sbt("t_id", [128, 128], BF16)
            t_ones = sbt("t_ones", [128, 128], F32)
            if ph == 1:
                t_xlb = sbt("t_xlb", [128, D], BF16)
                t_xt = sbt("t_xt", [128, NK * 128], BF16)
                t_xpw = sbt("t_xpw", [128, NK * (RT + 2 * N)], BF16)
                t_xpwf = sbt("t_xpwf", [128, RT + 2 * N], F32)
                t_pd = sbt("t_pd", [128, 128], BF16)
                t_pb = sbt("t_pb", [32, 128], F32)
            else:
                t_idf = sbt("t_idf", [128, 128], F32)
                t_y = sbt("t_y", [128, 2 * D], BF16)
                t_y2 = sbt("t_y2", [128, 2 * D], BF16)
                t_yt = sbt("t_yt", [128, D], BF16)
                t_ytk = sbt("t_ytk", [128, D], BF16)
                t_r2 = sbt("t_r2", [128, 2 * D], F32)
                t_dr = sbt("t_dr", [128, D], F32)
            ps_m = es.enter_context(nc.psum_tensor(pre + "ps_m", [128, D], F32))
            ps_t1 = es.enter_context(nc.psum_tensor(pre + "ps_t1", [128, 512], BF16))
            ps_t2 = es.enter_context(nc.psum_tensor(pre + "ps_t2", [128, 512], BF16))
            if ph == 1:
                ps_p1 = es.enter_context(nc.psum_tensor(pre + "ps_p1", [128, 128], F32))
                ps_p2 = es.enter_context(nc.psum_tensor(pre + "ps_p2", [32, 128], F32))
                ps_rep = ps_p1
            else:
                ps_f1 = es.enter_context(nc.psum_tensor(pre + "ps_f1", [128, 512], BF16))
                ps_f2 = es.enter_context(nc.psum_tensor(pre + "ps_f2", [128, 512], BF16))
                ps_rep = ps_m
            sm = {}
            for nm in ["s_wa0", "s_wa1", "s_wb0", "s_wb1", "s_x0", "s_x1", "s_g",
                       "s_xp", "s_t", "s_s", "s_v", "s_or", "s_odt", "s_ob",
                       "s_y0", "s_y1", "s_y20", "s_y21"]:
                sm[nm] = es.enter_context(nc.semaphore(pre + nm))
            xsrc = xin if ph == 1 else r1d

            def V(name):
                return (ph, name)

            def em_sync(eng):
                if ph == 3:
                    eng.wait_ge(s_ys, 128)
                eng.dma_start(out=t_gam[0:1, :], in_=gam).then_inc(sm["s_g"], 16)
                eng.dma_start(out=t_id[:], in_=ident).then_inc(sm["s_g"], 16)
                if ph == 3:
                    eng.dma_start(out=t_dr[0:1, :], in_=dpar).then_inc(sm["s_g"], 16)
                    eng.dma_start(out=t_idf[:], in_=identf).then_inc(sm["s_g"], 16)
                if ph == 1:
                    for k in range(NK):
                        if k >= 1:
                            eng.wait_ge(sm["s_v"], E(V(("xpc", k - 1))))
                        eng.dma_start(out=t_xpwf[:],
                                      in_=xpwTin[k * 128:(k + 1) * 128, :]).then_inc(sm["s_xp"], 16)
                    for k in range(NK):
                        if k >= 2:
                            eng.wait_ge(sm["s_v"], E(V(("wstat", k - 2))))
                        eng.dma_start(out=t_w[:, (k % 2) * D:(k % 2 + 1) * D],
                                      in_=Win[k * 128:(k + 1) * 128, :]).then_inc(
                            sm["s_wa0"] if k % 2 == 0 else sm["s_wa1"], 16)
                    for k in range(NK):
                        eng.wait_ge(sm["s_v"], E(V(("wstat", NK - 2 + k))) if k < 2
                                    else E(V(("wq", k - 2))))
                        eng.dma_start(out=t_w[:, (k % 2) * D:(k % 2 + 1) * D],
                                      in_=Win[k * 128:(k + 1) * 128, :]).then_inc(
                            sm["s_wb0"] if k % 2 == 0 else sm["s_wb1"], 16)
                else:
                    eng.wait_ge(s_wqs, 16 * NK)
                    for b in range(NK):
                        eng.dma_start(out=t_wq[:, b * D:(b + 1) * D],
                                      in_=wqd[b * 128:(b + 1) * 128, :]).then_inc(
                            sm["s_g"], 16)
                    eng.wait_ge(s_r1s, 16 * NT)
                for i in range(NT):
                    if i >= 2:
                        eng.wait_ge(sm["s_v"], E(V(("ep", i - 2))))
                    eng.dma_start(out=t_x[:, (i % 2) * D:(i % 2 + 1) * D],
                                  in_=xsrc[i * 128:(i + 1) * 128, :]).then_inc(
                        sm["s_x0"] if i % 2 == 0 else sm["s_x1"], 16)
                    if ph == 3:
                        if i >= 2:
                            eng.wait_ge(sm["s_v"], E(V(("r2", i - 2))))
                        if i == 0:
                            eng.wait_ge(s_cc, 6)
                        for b in range(NK):
                            half, bb = b // (NK // 2), b % (NK // 2)
                            yct = yco_a if bb < NK // 4 else yco_b
                            bb2 = bb % (NK // 4)
                            src0 = yct[half, bb2 * 128:(bb2 + 1) * 128,
                                       0 * TOK + i * 128:0 * TOK + (i + 1) * 128]
                            eng.dma_start(
                                out=t_y[:, (i % 2) * D + b * 128:(i % 2) * D + (b + 1) * 128],
                                in_=src0).then_inc(
                                sm["s_y0"] if i % 2 == 0 else sm["s_y1"], 16)

            def em_gp(eng):
                if ph == 3:
                    eng.wait_ge(s_cc, 6)
                    for i in range(NT):
                        if i >= 2:
                            eng.wait_ge(sm["s_v"], E(V(("r2", i - 2))))
                        for b in range(NK):
                            half, bb = b // (NK // 2), b % (NK // 2)
                            yct = yco_a if bb < NK // 4 else yco_b
                            bb2 = bb % (NK // 4)
                            src1 = yct[half, bb2 * 128:(bb2 + 1) * 128,
                                       1 * TOK + i * 128:1 * TOK + (i + 1) * 128]
                            eng.dma_start(
                                out=t_y2[:, (i % 2) * D + b * 128:(i % 2) * D + (b + 1) * 128],
                                in_=src1).then_inc(
                                sm["s_y20"] if i % 2 == 0 else sm["s_y21"], 16)
                        if i >= 1:
                            eng.wait_ge(sm["s_v"], E(V(("ep", i - 1))))
                            eng.dma_start(out=outo[(i - 1) * 128:i * 128, :],
                                          in_=t_r1[:]).then_inc(sm["s_or"], 16)
                    eng.wait_ge(sm["s_v"], E(V(("ep", NT - 1))))
                    eng.dma_start(out=outo[(NT - 1) * 128:NT * 128, :],
                                  in_=t_r1[:]).then_inc(sm["s_or"], 16)
                    eng.wait_ge(sm["s_or"], 16 * NT)
                else:
                    for b in range(NK):
                        eng.wait_ge(sm["s_v"], E(V(("wev", NK - 1, b // 4))))
                        eng.dma_start(out=wqd[b * 128:(b + 1) * 128, :],
                                      in_=t_wq[:, b * D:(b + 1) * D]).then_inc(s_wqs, 16)
                    for i in range(NT):
                        eng.wait_ge(sm["s_v"], E(V(("ep", i))))
                        eng.dma_start(out=r1d[i * 128:(i + 1) * 128, :],
                                      in_=t_r1[:]).then_inc(s_r1s, 16)
                        for b in range(NK):
                            eng.wait_ge(sm["s_s"], E(V(("xtev", i, b // 4))))
                            xci_t = xci_h0 if i < NT // 2 else xci_h1
                            ii = i % (NT // 2)
                            eng.dma_start(
                                out=xci_t[b * 128:(b + 1) * 128, ii * 128:(ii + 1) * 128],
                                in_=t_xt[:, b * 128:(b + 1) * 128]).then_inc(s_ob2, 16)
                        eng.wait_ge(sm["s_s"], E(V(("xp", i))))
                        eng.dma_start(out=dtci[:, i * 128:(i + 1) * 128],
                                      in_=t_pd[:]).then_inc(s_odt2, 16)
                        eng.dma_start(out=bcci[:, i * 128:(i + 1) * 128],
                                      in_=t_pb[:]).then_inc(s_odt2, 16)
                        if i == NT // 2 - 1:
                            eng.wait_ge(s_ob2, 16 * NK * (NT // 2))
                            eng.collective_compute(
                                "AllGather", OP.bypass, ins=[xci_h0],
                                outs=[xco_h0], replica_groups=RG).then_inc(s_cc, 1)
                    eng.wait_ge(s_odt2, 32 * NT)
                    eng.collective_compute("AllGather", OP.bypass, ins=[bcci],
                                           outs=[bcco], replica_groups=RG).then_inc(s_cc, 1)
                    eng.collective_compute("AllGather", OP.bypass, ins=[dtci],
                                           outs=[dtco], replica_groups=RG).then_inc(s_cc, 1)
                    eng.wait_ge(s_ob2, 16 * NT * NK)
                    eng.collective_compute("AllGather", OP.bypass, ins=[xci_h1],
                                           outs=[xco_h1], replica_groups=RG).then_inc(s_cc, 1)

            def em_tensor(eng):
                if ph == 3:
                    eng.wait_ge(s_ys, 128)
                eng.wait_ge(sm["s_g"], 16 * ((4 + NK) if ph == 3 else 2))
                eng.wait_ge(sm["s_v"], E(V("ones")))
                nrep = NK * (2 if ph == 3 else 1)
                for blk in range(nrep):
                    src_row = t_gam if blk < NK else t_dr
                    cb = (blk % NK) * 128
                    if blk > 0:
                        eng.wait_ge(sm["s_v"], E(V(("rev", blk - 1))))
                    eng.matmul(out=ps_rep[:, 0:128], lhsT=t_ones[0:1, :],
                               rhs=src_row[0:1, cb:cb + 128],
                               start=True, stop=True).then_inc(sm["s_t"], 1)
                    tab.rec(sm["s_t"], V(("rep", blk)))
                if ph == 1:
                    eng.wait_ge(sm["s_v"], E(V(("wstat", NK - 1))))
                    eng.wait_ge(sm["s_v"], E(V(("rev", nrep - 1))))
                    eng.matmul(out=ps_rep[:, 0:1], lhsT=t_ones[:],
                               rhs=t_st[:, 18:19], start=True, stop=True).then_inc(sm["s_t"], 1)
                    tab.rec(sm["s_t"], V(("wsum",)))
                    NG = NK // 4
                    for k in range(NK):
                        eng.wait_ge(sm["s_v"], E(V(("wq3", k))))
                        for g in range(NG):
                            un = k * NG + g
                            pst = ps_t1 if un % 2 == 0 else ps_t2
                            if un >= 2:
                                eng.wait_ge(sm["s_v"], E(V(("wev", (un - 2) // NG,
                                                            (un - 2) % NG))))
                            for q in range(4):
                                b = g * 4 + q
                                eng.transpose(out=pst[:, q * 128:(q + 1) * 128],
                                              in_=t_xq[:, b * 128:(b + 1) * 128],
                                              identity=t_id[:]).then_inc(sm["s_t"], 1)
                            tab.rec(sm["s_t"], V(("wtr", k, g)))

                def emit_xpt(i):
                    eng.wait_ge(sm["s_s"], E(V(("xtev", i, NK // 4 - 1))))
                    if i > 0:
                        eng.wait_ge(sm["s_s"], E(V(("xp", i - 1))))
                    else:
                        eng.wait_ge(sm["s_v"], E(V(("wsev",))))
                    lastd = lastb = None
                    for k in range(NK):
                        lastd = eng.matmul(out=ps_p1[:],
                                           lhsT=t_xpw[:, k * 160:k * 160 + 128],
                                           rhs=t_xt[:, k * 128:(k + 1) * 128],
                                           start=(k == 0), stop=(k == NK - 1))
                        lastb = eng.matmul(out=ps_p2[:],
                                           lhsT=t_xpw[:, k * 160 + 128:k * 160 + 160],
                                           rhs=t_xt[:, k * 128:(k + 1) * 128],
                                           start=(k == 0), stop=(k == NK - 1))
                    lastd.then_inc(sm["s_t"], 1)
                    lastb.then_inc(sm["s_t"], 1)
                    tab.rec(sm["s_t"], V(("xpt", i)))

                for i in range(NT):
                    if ph == 3:
                        eng.wait_ge(sm["s_v"], E(V(("ysel", i))))
                        for g in range(NK // 4):
                            un = i * (NK // 4) + g
                            pst = ps_f1 if un % 2 == 0 else ps_f2
                            if g >= 2:
                                eng.wait_ge(sm["s_s"], E(V(("yev", i, g - 2))))
                            elif i > 0:
                                eng.wait_ge(sm["s_s"], E(V(("yev", i - 1, NK // 4 - 2 + g))))
                            for q in range(4):
                                b = g * 4 + q
                                eng.transpose(out=pst[:, q * 128:(q + 1) * 128],
                                              in_=t_yt[:, b * 128:(b + 1) * 128],
                                              identity=t_id[:]).then_inc(sm["s_t"], 1)
                            tab.rec(sm["s_t"], V(("ytr", i, g)))
                    eng.wait_ge(sm["s_v"], E(V(("xq", i))))
                    for g in range(NK // 4):
                        un = i * (NK // 4) + g
                        pst = ps_t1 if un % 2 == 0 else ps_t2
                        eng.wait_ge(sm["s_s"], E(V(("xqev_prev", i, g))))
                        for q in range(4):
                            b = g * 4 + q
                            eng.transpose(out=pst[:, q * 128:(q + 1) * 128],
                                          in_=t_xq[:, b * 128:(b + 1) * 128],
                                          identity=t_id[:]).then_inc(sm["s_t"], 1)
                        tab.rec(sm["s_t"], V(("xqtr", i, g)))
                    eng.wait_ge(sm["s_s"], E(V(("xqev", i, NK // 4 - 1))))
                    if i > 0:
                        eng.wait_ge(sm["s_v"], E(V(("ep", i - 1))))
                    for ob in range(4):
                        last = None
                        for k in range(NK):
                            last = eng.matmul(out=ps_m[:, ob * 512:(ob + 1) * 512],
                                              lhsT=t_xqT[:, k * 128:(k + 1) * 128],
                                              rhs=t_wq[:, k * D + ob * 512:k * D + (ob + 1) * 512],
                                              start=(k == 0), stop=(k == NK - 1))
                        last.then_inc(sm["s_t"], 1)
                    tab.rec(sm["s_t"], V(("mm", i)))
                    if ph == 1:
                        if i > 0:
                            emit_xpt(i - 1)
                        eng.wait_ge(sm["s_v"], E(V(("xlb", i))))
                        for g in range(NK // 4):
                            un = i * (NK // 4) + g
                            pst = ps_t1 if un % 2 == 0 else ps_t2
                            eng.wait_ge(sm["s_s"], E(V(("xtev_prev", i, g))))
                            for q in range(4):
                                b = g * 4 + q
                                eng.transpose(out=pst[:, q * 128:(q + 1) * 128],
                                              in_=t_xlb[:, b * 128:(b + 1) * 128],
                                              identity=t_id[:]).then_inc(sm["s_t"], 1)
                            tab.rec(sm["s_t"], V(("xltr", i, g)))
                if ph == 1:
                    emit_xpt(NT - 1)

            def em_scalar(eng):
                if ph == 3:
                    eng.wait_ge(s_ys, 128)
                c_eps = t_cst[:, 0:1]
                c_rms = t_cst[:, 1:2]
                c_invd = t_cst[:, 2:3]
                c_half = t_cst[:, 3:4]
                QS = [{"ssq2": 7, "lnw": 8, "rms2": 9, "amax": 15, "ame": 22,
                       "rec": 23, "a": 10, "resid": 11},
                      {"ssq2": 31, "lnw": 32, "rms2": 33, "amax": 34, "ame": 35,
                       "rec": 36, "a": 37, "resid": 38}]

                def rstd1(i):
                    pb = 24 * (i % 2)
                    eng.wait_ge(sm["s_v"], E(V(("var1", i))))
                    eng.drain()
                    eng.activation(out=t_st[:, pb + 5:pb + 6],
                                   in_=t_st[:, pb + 3:pb + 4],
                                   func=AF.Ln, bias=c_eps).then_inc(sm["s_s"], 1)
                    eng.drain()
                    eng.activation(out=t_st[:, pb + 6:pb + 7],
                                   in_=t_st[:, pb + 5:pb + 6],
                                   func=AF.Exp, scale=c_half).then_inc(sm["s_s"], 1)
                    tab.rec(sm["s_s"], V(("rstd1", i)))

                def rstd2(i):
                    eng.wait_ge(sm["s_v"], E(V(("var2", i))))
                    eng.drain()
                    eng.activation(out=t_st[:, 43:44], in_=t_st[:, 41:42], func=AF.Ln,
                                   bias=c_eps).then_inc(sm["s_s"], 1)
                    eng.drain()
                    eng.activation(out=t_st[:, 44:45], in_=t_st[:, 43:44], func=AF.Exp,
                                   scale=c_half).then_inc(sm["s_s"], 1)
                    tab.rec(sm["s_s"], V(("rstd2", i)))

                def s_xqev(i):
                    for g in range(NK // 4):
                        un = i * (NK // 4) + g
                        pst = ps_t1 if un % 2 == 0 else ps_t2
                        tab.rec(sm["s_s"], V(("xqev_prev", i, g)))
                        eng.wait_ge(sm["s_t"], E(V(("xqtr", i, g))))
                        eng.drain()
                        eng.activation(out=t_xqT[:, g * 512:(g + 1) * 512],
                                       in_=pst[:], func=AF.Copy).then_inc(sm["s_s"], 1)
                        tab.rec(sm["s_s"], V(("xqev", i, g)))

                def s_yev(i):
                    for g in range(NK // 4):
                        eng.wait_ge(sm["s_t"], E(V(("ytr", i, g))))
                        pst = ps_f1 if (i * (NK // 4) + g) % 2 == 0 else ps_f2
                        eng.drain()
                        eng.activation(out=t_ytk[:, g * 512:(g + 1) * 512],
                                       in_=pst[:], func=AF.Copy).then_inc(sm["s_s"], 1)
                        tab.rec(sm["s_s"], V(("yev", i, g)))

                def s_xtev(i):
                    for g in range(NK // 4):
                        tab.rec(sm["s_s"], V(("xtev_prev", i, g)))
                        eng.wait_ge(sm["s_t"], E(V(("xltr", i, g))))
                        if g == 0 and i > 0:
                            eng.wait_ge(s_ob2, 16 * NK * i)
                        pst = ps_t1 if (i * (NK // 4) + g) % 2 == 0 else ps_t2
                        eng.drain()
                        eng.activation(out=t_xt[:, g * 512:(g + 1) * 512],
                                       in_=pst[:], func=AF.Copy).then_inc(sm["s_s"], 1)
                        tab.rec(sm["s_s"], V(("xtev", i, g)))

                def s_xp(i):
                    eng.wait_ge(sm["s_t"], E(V(("xpt", i))))
                    if i > 0:
                        eng.wait_ge(s_odt2, 32 * i)
                    eng.drain()
                    eng.activation(out=t_pd[:], in_=ps_p1[:],
                                   func=AF.Copy).then_inc(sm["s_s"], 1)
                    eng.activation(out=t_pb[:], in_=ps_p2[:],
                                   func=AF.Copy).then_inc(sm["s_s"], 1)
                    tab.rec(sm["s_s"], V(("xp", i)))

                def rms2(i):
                    q = QS[i % 2]
                    eng.wait_ge(sm["s_v"], E(V(("ssq2", i))))
                    eng.drain()
                    eng.activation(out=t_st[:, q["lnw"]:q["lnw"] + 1],
                                   in_=t_st[:, q["ssq2"]:q["ssq2"] + 1], func=AF.Ln,
                                   scale=c_invd, bias=c_rms).then_inc(sm["s_s"], 1)
                    eng.drain()
                    eng.activation(out=t_st[:, q["rms2"]:q["rms2"] + 1],
                                   in_=t_st[:, q["lnw"]:q["lnw"] + 1], func=AF.Exp,
                                   scale=c_half).then_inc(sm["s_s"], 1)
                    tab.rec(sm["s_s"], V(("rms2", i)))

                rstd1(0)
                if ph == 1:
                    rms2(0)
                    s_xqev(0)
                    rstd1(1)
                    rms2(1)
                    for i in range(NT):
                        rstd2(i)
                        if i + 2 < NT:
                            rstd1(i + 2)
                            rms2(i + 2)
                        s_xtev(i)
                        if i + 1 < NT:
                            s_xqev(i + 1)
                        s_xp(i)
                else:
                    for i in range(NT):
                        s_yev(i)
                        rstd2(i)
                        rms2(i)
                        s_xqev(i)
                        if i + 1 < NT:
                            rstd1(i + 1)

            def ln_stats(eng, src_ap, sum_c, ssq_c, negm_c, var_c, dummy=None):
                dum = t_scr if dummy is None else dummy
                eng.tensor_reduce(out=t_st[:, sum_c:sum_c + 1], in_=src_ap,
                                  axis=AX.X, op=OP.add).then_inc(sm["s_v"], 1)
                eng.scalar_tensor_tensor(out=dum[:], in0=src_ap, scalar=1.0,
                                         in1=src_ap, op0=OP.mult, op1=OP.mult,
                                         accum_out=t_st[:, ssq_c:ssq_c + 1]).then_inc(sm["s_v"], 1)
                eng.drain()
                eng.tensor_scalar(out=t_st[:, negm_c:negm_c + 1],
                                  in0=t_st[:, sum_c:sum_c + 1],
                                  scalar1=float(-1.0 / D), scalar2=None,
                                  op0=OP.mult).then_inc(sm["s_v"], 1)
                eng.drain()
                eng.tensor_scalar(out=t_st[:, var_c + 1:var_c + 2],
                                  in0=t_st[:, negm_c:negm_c + 1],
                                  scalar1=t_st[:, negm_c:negm_c + 1], scalar2=None,
                                  op0=OP.mult).then_inc(sm["s_v"], 1)
                eng.drain()
                eng.scalar_tensor_tensor(out=t_st[:, var_c:var_c + 1],
                                         in0=t_st[:, ssq_c:ssq_c + 1],
                                         scalar=float(1.0 / D),
                                         in1=t_st[:, var_c + 1:var_c + 2],
                                         op0=OP.mult, op1=OP.subtract).then_inc(sm["s_v"], 1)

            def em_vector(eng):
                if ph == 3:
                    eng.wait_ge(s_ys, 128)
                if ph == 1:
                    eng.memset(t_cst[:, 0:1], float(EPS_LN))
                    eng.memset(t_cst[:, 1:2], float(EPS_RMS))
                    eng.memset(t_cst[:, 2:3], float(1.0 / D))
                    eng.memset(t_cst[:, 3:4], -0.5)
                eng.memset(t_ones[:], 1.0).then_inc(sm["s_v"], 1)
                tab.rec(sm["s_v"], V("ones"))
                eng.wait_ge(sm["s_g"], 16 * ((4 + NK) if ph == 3 else 2))
                nrep = NK * (2 if ph == 3 else 1)
                for blk in range(nrep):
                    dst_row = t_gam if blk < NK else t_dr
                    cb = (blk % NK) * 128
                    eng.wait_ge(sm["s_t"], E(V(("rep", blk))))
                    eng.tensor_copy(out=dst_row[:, cb:cb + 128],
                                    in_=ps_rep[:, 0:128]).then_inc(sm["s_v"], 1)
                    tab.rec(sm["s_v"], V(("rev", blk)))
                if ph == 1:
                    for k in range(NK):
                        eng.wait_ge(sm["s_xp"], 16 * (k + 1))
                        eng.tensor_copy(out=t_xpw[:, k * 160:(k + 1) * 160],
                                        in_=t_xpwf[:]).then_inc(sm["s_v"], 1)
                        tab.rec(sm["s_v"], V(("xpc", k)))
                    for k in range(NK):
                        eng.wait_ge(sm["s_wa0"] if k % 2 == 0 else sm["s_wa1"],
                                    16 * (k // 2 + 1))
                        eng.tensor_reduce(out=t_st[:, 16 + (k % 2):17 + (k % 2)],
                                          in_=t_w[:, (k % 2) * D:(k % 2 + 1) * D],
                                          axis=AX.X, op=OP.add,
                                          apply_absolute_value=True).then_inc(sm["s_v"], 1)
                        eng.drain()
                        if k == 0:
                            eng.tensor_copy(out=t_st[:, 18:19],
                                            in_=t_st[:, 16:17]).then_inc(sm["s_v"], 1)
                        else:
                            eng.tensor_tensor(out=t_st[:, 19:20], in0=t_st[:, 18:19],
                                              in1=t_st[:, 16 + (k % 2):17 + (k % 2)],
                                              op=OP.add).then_inc(sm["s_v"], 1)
                            eng.drain()
                            eng.tensor_copy(out=t_st[:, 18:19],
                                            in_=t_st[:, 19:20]).then_inc(sm["s_v"], 1)
                        tab.rec(sm["s_v"], V(("wstat", k)))
                    eng.drain()
                    eng.wait_ge(sm["s_t"], E(V(("wsum",))))
                    eng.tensor_copy(out=t_st[:, 18:19], in_=ps_rep[:, 0:1]).then_inc(sm["s_v"], 1)
                    tab.rec(sm["s_v"], V(("wsev",)))
                    eng.drain()
                    eng.tensor_scalar(out=t_st[:, 20:21], in0=t_st[:, 18:19],
                                      scalar1=float(1.0 / (D * D)), scalar2=float(EPS_Q),
                                      op0=OP.mult, op1=OP.add).then_inc(sm["s_v"], 1)
                    eng.drain()
                    eng.reciprocal(out=t_st[:, 21:22], in_=t_st[:, 20:21]).then_inc(sm["s_v"], 1)
                    eng.drain()
                    tab.rec(sm["s_v"], V(("ws",)))
                    for k in range(NK):
                        eng.wait_ge(sm["s_wb0"] if k % 2 == 0 else sm["s_wb1"],
                                    16 * (k // 2 + 1))
                        eng.tensor_scalar(out=t_q[:], in0=t_w[:, (k % 2) * D:(k % 2 + 1) * D],
                                          scalar1=t_st[:, 21:22], scalar2=MAGIC,
                                          op0=OP.mult, op1=OP.add).then_inc(sm["s_v"], 1)
                        eng.drain()
                        eng.tensor_scalar(out=t_scr[:], in0=t_q[:],
                                          scalar1=-MAGIC, scalar2=1.0,
                                          op0=OP.add, op1=OP.min).then_inc(sm["s_v"], 1)
                        eng.drain()
                        tab.rec(sm["s_v"], V(("wq", k)))
                        if k >= 1:
                            eng.wait_ge(sm["s_t"], E(V(("wtr", k - 1, NK // 4 - 1))))
                        eng.tensor_scalar(out=t_xq[:], in0=t_scr[:],
                                          scalar1=-1.0, scalar2=None,
                                          op0=OP.max).then_inc(sm["s_v"], 1)
                        tab.rec(sm["s_v"], V(("wq3", k)))
                        for g in range(NK // 4):
                            un = k * (NK // 4) + g
                            pst = ps_t1 if un % 2 == 0 else ps_t2
                            eng.wait_ge(sm["s_t"], E(V(("wtr", k, g))))
                            for q in range(4):
                                b = g * 4 + q
                                eng.tensor_copy(
                                    out=t_wq[:, b * D + k * 128:b * D + (k + 1) * 128],
                                    in_=pst[:, q * 128:(q + 1) * 128]).then_inc(sm["s_v"], 1)
                            tab.rec(sm["s_v"], V(("wev", k, g)))
                QS = [{"ssq2": 7, "lnw": 8, "rms2": 9, "amax": 15, "ame": 22,
                       "rec": 23, "a": 10, "resid": 11},
                      {"ssq2": 31, "lnw": 32, "rms2": 33, "amax": 34, "ame": 35,
                       "rec": 36, "a": 37, "resid": 38}]

                def xcol(c):
                    return t_st[:, c:c + 1]

                def pre(i):
                    pb = 24 * (i % 2)
                    q = QS[i % 2]
                    xs = t_x[:, (i % 2) * D:(i % 2 + 1) * D]
                    if i == 0:
                        eng.wait_ge(sm["s_x0"], 16)
                        ln_stats(eng, xs, pb + 0, pb + 1, pb + 2, pb + 3)
                        tab.rec(sm["s_v"], V(("var1", i)))
                    eng.wait_ge(sm["s_s"], E(V(("rstd1", i))))
                    eng.drain()
                    eng.tensor_scalar(out=t_xln[:], in0=xs,
                                      scalar1=xcol(pb + 2), scalar2=xcol(pb + 6),
                                      op0=OP.add, op1=OP.mult).then_inc(sm["s_v"], 1)
                    tab.rec(sm["s_v"], V(("xln1", i)))
                    if ph == 3:
                        r2s = t_r2[:, (i % 2) * D:(i % 2 + 1) * D]
                        eng.wait_ge(sm["s_y0"] if i % 2 == 0 else sm["s_y1"],
                                    16 * NK * (i // 2 + 1))
                        eng.wait_ge(sm["s_y20"] if i % 2 == 0 else sm["s_y21"],
                                    16 * NK * (i // 2 + 1))
                        eng.drain()
                        eng.tensor_scalar(out=t_yt[:], in0=t_y[:, (i % 2) * D:(i % 2 + 1) * D],
                                          scalar1=t_msk[:, 1:2], scalar2=None,
                                          op0=OP.mult).then_inc(sm["s_v"], 1)
                        eng.drain()
                        eng.scalar_tensor_tensor(out=t_yt[:],
                                                 in0=t_y2[:, (i % 2) * D:(i % 2 + 1) * D],
                                                 scalar=t_msk[:, 0:1], in1=t_yt[:],
                                                 op0=OP.mult, op1=OP.add).then_inc(sm["s_v"], 1)
                        tab.rec(sm["s_v"], V(("ysel", i)))
                        eng.drain()
                        eng.tensor_tensor(out=t_scr[:], in0=t_xln[:], in1=t_dr[:],
                                          op=OP.mult).then_inc(sm["s_v"], 1)
                        eng.drain()
                        eng.wait_ge(sm["s_s"], E(V(("yev", i, NK // 4 - 1))))
                        eng.tensor_tensor(out=t_q[:], in0=xs, in1=t_ytk[:],
                                          op=OP.add).then_inc(sm["s_v"], 1)
                        eng.drain()
                        eng.tensor_tensor(out=r2s, in0=t_q[:], in1=t_scr[:],
                                          op=OP.add).then_inc(sm["s_v"], 1)
                        tab.rec(sm["s_v"], V(("r2", i)))
                        eng.drain()
                        ln_stats(eng, r2s, 19, 39, 40, 41)
                        tab.rec(sm["s_v"], V(("var2", i)))
                        eng.wait_ge(sm["s_s"], E(V(("rstd2", i))))
                        eng.drain()
                        eng.tensor_scalar(out=t_xln[:], in0=r2s,
                                          scalar1=xcol(40), scalar2=xcol(44),
                                          op0=OP.add, op1=OP.mult).then_inc(sm["s_v"], 1)
                    eng.drain()
                    eng.scalar_tensor_tensor(out=t_scr[:], in0=t_xln[:], scalar=1.0,
                                             in1=t_xln[:], op0=OP.mult, op1=OP.mult,
                                             accum_out=xcol(q["ssq2"])).then_inc(sm["s_v"], 1)
                    tab.rec(sm["s_v"], V(("ssq2", i)))
                    eng.wait_ge(sm["s_s"], E(V(("rms2", i))))
                    eng.drain()
                    eng.scalar_tensor_tensor(out=t_scr[:], in0=t_xln[:],
                                             scalar=xcol(q["rms2"]), in1=t_gam[:],
                                             op0=OP.mult, op1=OP.mult).then_inc(sm["s_v"], 1)
                    eng.drain()
                    eng.tensor_reduce(out=xcol(q["amax"]), in_=t_scr[:], axis=AX.X,
                                      op=OP.max, apply_absolute_value=True).then_inc(sm["s_v"], 1)
                    eng.drain()
                    eng.tensor_scalar(out=xcol(q["ame"]), in0=xcol(q["amax"]),
                                      scalar1=float(EPS_Q), scalar2=None,
                                      op0=OP.add).then_inc(sm["s_v"], 1)
                    eng.drain()
                    eng.reciprocal(out=xcol(q["rec"]), in_=xcol(q["ame"])).then_inc(sm["s_v"], 1)
                    eng.drain()
                    eng.tensor_scalar(out=xcol(q["a"]), in0=xcol(q["rec"]),
                                      scalar1=127.0, scalar2=None,
                                      op0=OP.mult).then_inc(sm["s_v"], 1)
                    eng.tensor_scalar(out=xcol(q["resid"]), in0=xcol(q["ame"]),
                                      scalar1=float(0.5 / 127.0), scalar2=xcol(20),
                                      op0=OP.mult, op1=OP.mult).then_inc(sm["s_v"], 1)
                    tab.rec(sm["s_v"], V(("qs", i)))
                    eng.drain()
                    eng.tensor_scalar(out=t_q[:], in0=t_scr[:],
                                      scalar1=xcol(q["a"]), scalar2=MAGIC,
                                      op0=OP.mult, op1=OP.add).then_inc(sm["s_v"], 1)
                    eng.drain()
                    if i > 0:
                        eng.wait_ge(sm["s_t"], E(V(("xqtr", i - 1, NK // 4 - 1))))
                    elif ph == 1:
                        eng.wait_ge(sm["s_t"], E(V(("wtr", NK - 1, NK // 4 - 1))))
                    eng.tensor_scalar(out=t_xq[:], in0=t_q[:],
                                      scalar1=-MAGIC, scalar2=None,
                                      op0=OP.add).then_inc(sm["s_v"], 1)
                    tab.rec(sm["s_v"], V(("xq", i)))
                    if i == 0:
                        hoist_stats(1)

                def hoist_stats(k):
                    pbk = 24 * (k % 2)
                    sxk = sm["s_x0"] if k % 2 == 0 else sm["s_x1"]
                    xsk = t_x[:, (k % 2) * D:(k % 2 + 1) * D]
                    eng.wait_ge(sxk, 16 * (k // 2 + 1))
                    eng.drain()
                    ln_stats(eng, xsk, pbk + 0, pbk + 1, pbk + 2, pbk + 3,
                             dummy=t_xln)
                    tab.rec(sm["s_v"], V(("var1", k)))

                pre(0)
                for i in range(NT):
                    q = QS[i % 2]
                    xs = t_x[:, (i % 2) * D:(i % 2 + 1) * D]
                    resid = xs if ph == 1 else t_r2[:, (i % 2) * D:(i % 2 + 1) * D]
                    if i + 1 < NT:
                        pre(i + 1)
                    eng.wait_ge(sm["s_t"], E(V(("mm", i))))
                    if i > 0:
                        eng.wait_ge(sm["s_or"] if ph == 3 else s_r1s, 16 * i)
                    eng.drain()
                    eng.scalar_tensor_tensor(out=t_r1[:], in0=ps_m[:],
                                             scalar=xcol(q["resid"]), in1=resid,
                                             op0=OP.mult, op1=OP.add).then_inc(sm["s_v"], 1)
                    tab.rec(sm["s_v"], V(("ep", i)))
                    if i + 2 < NT:
                        hoist_stats(i + 2)
                    if ph == 1:
                        eng.drain()
                        ln_stats(eng, t_r1[:], 19, 39, 40, 41)
                        tab.rec(sm["s_v"], V(("var2", i)))
                        eng.wait_ge(sm["s_s"], E(V(("rstd2", i))))
                        eng.drain()
                        eng.tensor_scalar(out=t_xlb[:], in0=t_r1[:],
                                          scalar1=xcol(40), scalar2=xcol(44),
                                          op0=OP.add, op1=OP.mult).then_inc(sm["s_v"], 1)
                        tab.rec(sm["s_v"], V(("xlb", i)))

            return dict(sync=em_sync, gp=em_gp, tensor=em_tensor,
                        scalar=em_scalar, vector=em_vector)

        def scan_phase(es):
            NT_T = TOK // 128
            NK = D // 128
            NKT = NT_T * NK
            sbt = lambda name, shape, dt: es.enter_context(
                nc.sbuf_tensor("p2_" + name, shape, dt))
            t_dtwb = sbt("t_dtwb", [RT, DH], BF16)
            t_dtb = sbt("t_dtb", [128, NDT], F32)
            t_alog = sbt("t_alog", [128, NDT * N], F32)
            t_a = sbt("t_a", [128, NDT * N], F32)
            t_an = sbt("t_an", [128, NDT * N], F32)
            t_bc = sbt("t_bc", [N, TOK], F32)
            t_bcb = sbt("t_bcb", [N, L], BF16)
            t_sel = sbt("t_sel", [N, N * 128], BF16)
            t_dt = sbt("t_dt", [RT, L], BF16)
            t_xln = sbt("t_xln", [128, 2 * L], BF16)
            t_xl2 = sbt("t_xl2", [128, L], BF16)
            t_delta = sbt("t_delta", [128, L], BF16)
            t_dx = sbt("t_dx", [128, L], BF16)
            t_sp1 = sbt("t_sp1", [128, HL], F32)
            t_sp2 = sbt("t_sp2", [128, HL], F32)
            t_sp3 = sbt("t_sp3", [128, HL], F32)
            t_sp4 = sbt("t_sp4", [128, HL], F32)
            t_da = sbt("t_da", [128, L], F32)
            t_u = sbt("t_u", [128, L], BF16)
            t_h = sbt("t_h", [128, L], BF16)
            t_yp = sbt("t_yp", [128, L], BF16)
            t_mg = sbt("t_mg", [128, L], BF16)
            t_p1 = sbt("t_p1", [128, L], BF16)
            t_p2 = sbt("t_p2", [128, L], BF16)
            t_p3 = sbt("t_p3", [128, L], BF16)
            t_p4 = sbt("t_p4", [128, L], BF16)
            t_brep = sbt("t_brep", [128, 2 * L], BF16)
            t_crep = sbt("t_crep", [128, L], BF16)
            ps_a = es.enter_context(nc.psum_tensor("p2_ps_a", [128, 2048], F32))
            ps_b = es.enter_context(nc.psum_tensor("p2_ps_b", [128, 2048], F32))
            sm = {}
            for nm in ["s_in", "s_bc", "s_xl0", "s_xl1", "s_br0", "s_br1",
                       "s_brc", "s_bs0", "s_bs1", "s_t", "s_s", "s_v", "s_y",
                       "s_dt"]:
                sm[nm] = es.enter_context(nc.semaphore("p2_" + nm))
            NPARAM = 1 + NDT * 2 + 1 + 1

            def V(name):
                return (2, name)

            def em_sync(eng):
                eng.wait_ge(s_wqs, 16 * NK)
                eng.wait_ge(s_r1s, 16 * NT_T)
                eng.wait_ge(s_ob2, 16 * NKT)
                eng.wait_ge(s_odt2, 32 * NT_T)
                eng.dma_start(out=t_msk[:], in_=mask).then_inc(sm["s_in"], 16)
                eng.dma_start(out=t_sp1[:, 0:DH], in_=dtwT).then_inc(sm["s_in"], 16)
                for j in range(NDT):
                    eng.dma_start(out=t_dtb[:, j:j + 1],
                                  in_=dtbias[j * 128:(j + 1) * 128, :]).then_inc(sm["s_in"], 16)
                    eng.dma_start(out=t_alog[:, j * N:(j + 1) * N],
                                  in_=Alog[j * 128:(j + 1) * 128, :]).then_inc(sm["s_in"], 16)
                eng.dma_start(out=t_sel[:], in_=selin).then_inc(sm["s_in"], 16)
                eng.wait_ge(s_cc, 3)
                eng.dma_start(out=t_dt[:, 0:TOK], in_=dtco[0]).then_inc(sm["s_dt"], 16)
                eng.dma_start(out=t_dt[:, TOK:L], in_=dtco[1]).then_inc(sm["s_dt"], 16)
                eng.wait_ge(s_cc, 2)
                for r in range(4):
                    which, th = r // 2, r % 2
                    if r >= 1:
                        eng.wait_ge(sm["s_v"], E(V(("bccast", r - 1))))
                    eng.dma_start(out=t_bc[:],
                                  in_=bcco[th, which * N:(which + 1) * N, :]).then_inc(sm["s_bc"], 16)
                eng.wait_ge(s_cc, 1)
                HT = TOK // 2
                for j in range(NDT):
                    if j >= 2:
                        eng.wait_ge(sm["s_v"], E(V(("dx", j - 2))))
                    if j >= 1:
                        eng.wait_ge(sm["s_v"], E(V(("xsel", j - 1))))
                    if j == 0:
                        pass
                    sl = (j % 2) * L
                    sx = sm["s_xl0"] if j % 2 == 0 else sm["s_xl1"]
                    for th in range(2):
                        for hh in range(2):
                            xsrc2 = xco_h0 if hh == 0 else xco_h1
                            if hh == 1 and j == 0 and th == 0:
                                eng.wait_ge(s_cc, 4)
                            co = sl + th * TOK + hh * HT
                            eng.dma_start(out=t_xln[:, co:co + HT],
                                          in_=xsrc2[th, j * 128:(j + 1) * 128, :]).then_inc(sx, 16)
                            eng.dma_start(out=t_xl2[:, th * TOK + hh * HT:th * TOK + hh * HT + HT],
                                          in_=xsrc2[th, DH + j * 128:DH + (j + 1) * 128, :]).then_inc(sx, 16)
                    for n in range(N):
                        sn = n % 2
                        if j == 0 and n == 0:
                            eng.wait_ge(sm["s_bs0"], 16 * (N * 2))
                            eng.wait_ge(sm["s_bs1"], 16 * (N * 2))
                        if (j, n) > (0, 1):
                            pj, pn = (j, n - 2) if n >= 2 else (j - 1, N - 2 + n)
                            eng.wait_ge(sm["s_v"], E(V(("u", pj, pn))))
                        eng.dma_start(out=t_brep[:, sn * L:(sn + 1) * L],
                                      in_=brep_d[n * 128:(n + 1) * 128, :]).then_inc(
                            sm["s_br0"] if sn == 0 else sm["s_br1"], 16)
                        if (j, n) != (0, 0):
                            pj, pn = (j, n - 1) if n >= 1 else (j - 1, N - 1)
                            eng.wait_ge(sm["s_v"], E(V(("ym", pj, pn))))
                        eng.dma_start(out=t_crep[:],
                                      in_=crep_d[n * 128:(n + 1) * 128, :]).then_inc(sm["s_brc"], 16)

            def em_gp(eng):
                eng.wait_ge(s_wqs, 16 * NK)
                eng.wait_ge(s_r1s, 16 * NT_T)
                eng.wait_ge(s_ob2, 16 * NKT)
                eng.wait_ge(s_odt2, 32 * NT_T)
                for which in range(2):
                    for n in range(N):
                        for half in range(2):
                            un = which * (N * 2) + n * 2 + half
                            eng.wait_ge(sm["s_v"], E(V(("evac", un))))
                            srcap = t_u[:, half * HL:(half + 1) * HL]
                            dst = (brep_d if which == 0 else crep_d)[
                                n * 128:(n + 1) * 128, half * HL:(half + 1) * HL]
                            eng.dma_start(out=dst, in_=srcap).then_inc(
                                sm["s_bs0"] if un % 2 == 0 else sm["s_bs1"], 16)
                for j in range(NDT):
                    eng.wait_ge(sm["s_v"], E(V(("ysumb", j))))
                    ydst = yci_a if j < NDT // 2 else yci_b
                    jj = j % (NDT // 2)
                    eng.dma_start(out=ydst[jj * 128:(jj + 1) * 128, :],
                                  in_=t_p1[:]).then_inc(s_ys, 16)
                    if j == NDT // 2 - 1:
                        eng.wait_ge(s_ys, 16 * (NDT // 2))
                        eng.collective_compute(
                            "AllGather", OP.bypass, ins=[yci_a],
                            outs=[yco_a], replica_groups=RG).then_inc(s_cc, 1)
                eng.wait_ge(s_ys, 16 * NDT)
                eng.collective_compute("AllGather", OP.bypass, ins=[yci_b],
                                       outs=[yco_b], replica_groups=RG).then_inc(s_cc, 1)

            def em_tensor(eng):
                eng.wait_ge(s_wqs, 16 * NK)
                eng.wait_ge(s_r1s, 16 * NT_T)
                eng.wait_ge(s_ob2, 16 * NKT)
                eng.wait_ge(s_odt2, 32 * NT_T)
                eng.wait_ge(sm["s_in"], 16 * NPARAM)
                eng.wait_ge(sm["s_dt"], 32)
                eng.wait_ge(sm["s_v"], E(V("setup")))
                for which in range(2):
                    if which == 0:
                        eng.wait_ge(sm["s_v"], E(V("castB")))
                    else:
                        eng.wait_ge(sm["s_v"], E(V("castC")))
                    for n in range(N):
                        for half in range(2):
                            un = which * (N * 2) + n * 2 + half
                            pst = ps_a if un % 2 == 0 else ps_b
                            if un >= 2:
                                eng.wait_ge(sm["s_v"], E(V(("evac", un - 2))))
                            last = None
                            for blk in range(4):
                                last = eng.matmul(
                                    out=pst[:, blk * 512:(blk + 1) * 512],
                                    lhsT=t_sel[:, n * 128:(n + 1) * 128],
                                    rhs=t_bcb[:, half * HL + blk * 512:
                                              half * HL + (blk + 1) * 512],
                                    start=True, stop=True)
                            last.then_inc(sm["s_t"], 1)
                            tab.rec(sm["s_t"], V(("bld", un)))
                for j in range(NDT):
                    for half in range(2):
                        un = j * 2 + half
                        pst = ps_a if un % 2 == 0 else ps_b
                        if un < 2:
                            eng.wait_ge(sm["s_v"], E(V(("evac", N * 4 - 2 + un))))
                        else:
                            eng.wait_ge(sm["s_s"], E(V(("psfree", (un - 2) // 2, (un - 2) % 2))))
                        last = None
                        for blk in range(4):
                            last = eng.matmul(
                                out=pst[:, blk * 512:(blk + 1) * 512],
                                lhsT=t_dtwb[:, j * 128:(j + 1) * 128],
                                rhs=t_dt[:, half * HL + blk * 512:half * HL + (blk + 1) * 512],
                                start=True, stop=True)
                        last.then_inc(sm["s_t"], 1)
                        tab.rec(sm["s_t"], V(("dtproj", j, half)))

            def em_scalar(eng):
                eng.wait_ge(s_wqs, 16 * NK)
                eng.wait_ge(s_r1s, 16 * NT_T)
                eng.wait_ge(s_ob2, 16 * NKT)
                eng.wait_ge(s_odt2, 32 * NT_T)
                eng.wait_ge(sm["s_in"], 16 * NPARAM)
                eng.drain()
                eng.activation(out=t_a[:], in_=t_alog[:], func=AF.Exp).then_inc(sm["s_s"], 1)
                tab.rec(sm["s_s"], V("aexp"))
                for j in range(NDT):
                    for half in range(2):
                        un = j * 2 + half
                        pst = ps_a if un % 2 == 0 else ps_b
                        eng.wait_ge(sm["s_t"], E(V(("dtproj", j, half))))
                        if un >= 1:
                            eng.wait_ge(sm["s_v"], E(V(("delta", (un - 1) // 2, (un - 1) % 2))))
                        bias_ap = t_dtb[:, j:j + 1]
                        eng.drain()
                        eng.activation(out=t_sp1[:], in_=pst[:], func=AF.Abs,
                                       bias=bias_ap).then_inc(sm["s_s"], 1)
                        eng.activation(out=t_sp2[:], in_=pst[:], func=AF.Relu,
                                       bias=bias_ap).then_inc(sm["s_s"], 1)
                        tab.rec(sm["s_s"], V(("psfree", j, half)))
                        eng.drain()
                        eng.activation(out=t_sp3[:], in_=t_sp1[:], func=AF.Exp,
                                       scale=-1.0).then_inc(sm["s_s"], 1)
                        eng.drain()
                        eng.activation(out=t_sp4[:], in_=t_sp3[:], func=AF.Ln,
                                       bias=1.0).then_inc(sm["s_s"], 1)
                        tab.rec(sm["s_s"], V(("sp", j, half)))
                    for n in range(N):
                        if (j, n) != (0, 0):
                            pj, pn = (j, n - 1) if n >= 1 else (j - 1, N - 1)
                            eng.wait_ge(sm["s_v"], E(V(("tts", pj, pn))))
                        if n == 0 and j > 0:
                            eng.wait_ge(sm["s_v"], E(V(("ysumb", j - 1))))
                        eng.wait_ge(sm["s_v"], E(V(("dx", j))))
                        eng.drain()
                        eng.activation(out=t_da[:], in_=t_delta[:], func=AF.Exp,
                                       scale=t_an[:, j * N + n:j * N + n + 1]).then_inc(sm["s_s"], 1)
                        tab.rec(sm["s_s"], V(("da", j, n)))

            def em_vector(eng):
                eng.wait_ge(s_wqs, 16 * NK)
                eng.wait_ge(s_r1s, 16 * NT_T)
                eng.wait_ge(s_ob2, 16 * NKT)
                eng.wait_ge(s_odt2, 32 * NT_T)
                eng.wait_ge(sm["s_in"], 16 * NPARAM)
                eng.drain()
                eng.tensor_copy(out=t_dtwb[:], in_=t_sp1[:, 0:DH]).then_inc(sm["s_v"], 1)
                eng.wait_ge(sm["s_s"], E(V("aexp")))
                eng.tensor_scalar(out=t_an[:], in0=t_a[:], scalar1=-1.0, scalar2=None,
                                  op0=OP.mult).then_inc(sm["s_v"], 1)
                tab.rec(sm["s_v"], V("setup"))
                for r in range(2):
                    eng.wait_ge(sm["s_bc"], 16 * (r + 1))
                    eng.drain()
                    eng.tensor_copy(out=t_bcb[:, (r % 2) * TOK:(r % 2 + 1) * TOK],
                                    in_=t_bc[:]).then_inc(sm["s_v"], 1)
                    tab.rec(sm["s_v"], V(("bccast", r)))
                tab.rec(sm["s_v"], V("castB"))
                for which in range(2):
                    for n in range(N):
                        for half in range(2):
                            un = which * (N * 2) + n * 2 + half
                            pst = ps_a if un % 2 == 0 else ps_b
                            eng.wait_ge(sm["s_t"], E(V(("bld", un))))
                            if un >= 2:
                                eng.wait_ge(sm["s_bs0"] if un % 2 == 0 else sm["s_bs1"],
                                            16 * (un // 2))
                            eng.drain()
                            eng.tensor_copy(out=t_u[:, half * HL:(half + 1) * HL],
                                            in_=pst[:]).then_inc(sm["s_v"], 1)
                            tab.rec(sm["s_v"], V(("evac", un)))
                            if un == N * 2 - 1:
                                for r in range(2, 4):
                                    eng.wait_ge(sm["s_bc"], 16 * (r + 1))
                                    eng.drain()
                                    eng.tensor_copy(out=t_bcb[:, (r % 2) * TOK:(r % 2 + 1) * TOK],
                                                    in_=t_bc[:]).then_inc(sm["s_v"], 1)
                                    tab.rec(sm["s_v"], V(("bccast", r)))
                                tab.rec(sm["s_v"], V("castC"))
                for j in range(NDT):
                    sl = (j % 2) * L
                    sx = sm["s_xl0"] if j % 2 == 0 else sm["s_xl1"]
                    for half in range(2):
                        eng.wait_ge(sm["s_s"], E(V(("sp", j, half))))
                        if j > 0 and half == 0:
                            eng.wait_ge(sm["s_s"], E(V(("da", j - 1, N - 1))))
                        eng.drain()
                        eng.tensor_tensor(out=t_delta[:, half * HL:(half + 1) * HL],
                                          in0=t_sp2[:], in1=t_sp4[:], op=OP.add).then_inc(sm["s_v"], 1)
                        tab.rec(sm["s_v"], V(("delta", j, half)))
                    eng.wait_ge(sx, 16 * 8 * (j // 2 + 1))
                    eng.drain()
                    eng.tensor_scalar(out=t_xln[:, sl:sl + L], in0=t_xln[:, sl:sl + L],
                                      scalar1=t_msk[:, 1:2], scalar2=None,
                                      op0=OP.mult).then_inc(sm["s_v"], 1)
                    eng.drain()
                    eng.scalar_tensor_tensor(out=t_xln[:, sl:sl + L],
                                             in0=t_xl2[:],
                                             scalar=t_msk[:, 0:1],
                                             in1=t_xln[:, sl:sl + L],
                                             op0=OP.mult, op1=OP.add).then_inc(sm["s_v"], 1)
                    tab.rec(sm["s_v"], V(("xsel", j)))
                    eng.drain()
                    eng.tensor_tensor(out=t_dx[:], in0=t_delta[:],
                                      in1=t_xln[:, sl:sl + L],
                                      op=OP.mult).then_inc(sm["s_v"], 1)
                    tab.rec(sm["s_v"], V(("dx", j)))
                    eng.drain()
                    for n in range(N):
                        sn = n % 2
                        idx = j * N + n
                        if j == 0 and n == 0:
                            eng.wait_ge(sm["s_bs0"], 16 * (N * 2))
                            eng.wait_ge(sm["s_bs1"], 16 * (N * 2))
                        eng.wait_ge(sm["s_br0"] if sn == 0 else sm["s_br1"],
                                    16 * (idx // 2 + 1))
                        eng.drain()
                        eng.tensor_tensor(out=t_u[:], in0=t_dx[:],
                                          in1=t_brep[:, sn * L:(sn + 1) * L],
                                          op=OP.mult).then_inc(sm["s_v"], 1)
                        tab.rec(sm["s_v"], V(("u", j, n)))
                        eng.wait_ge(sm["s_s"], E(V(("da", j, n))))
                        eng.drain()
                        eng.tensor_tensor_scan(out=t_h[:], data0=t_da[:], data1=t_u[:],
                                               initial=0.0, op0=OP.mult,
                                               op1=OP.add).then_inc(sm["s_v"], 1)
                        tab.rec(sm["s_v"], V(("tts", j, n)))
                        ym_dst = t_p1 if n % 2 == 0 else t_yp
                        if n == 0 and j > 0:
                            eng.wait_ge(s_ys, 16 * j)
                        eng.wait_ge(sm["s_brc"], 16 * (idx + 1))
                        eng.drain()
                        eng.tensor_tensor(out=ym_dst[:], in0=t_h[:],
                                          in1=t_crep[:],
                                          op=OP.mult).then_inc(sm["s_v"], 1)
                        tab.rec(sm["s_v"], V(("ym", j, n)))
                        if n % 2 == 1:
                            k = n + 1
                            chain = 0
                            kk = k
                            while kk % 2 == 0 and kk > 1:
                                chain += 1
                                kk //= 2
                            bufs = [t_mg, t_u]
                            cur = None
                            lvslots = [t_p2, t_p3, t_p4]
                            for c in range(chain):
                                src0 = t_p1 if c == 0 else cur
                                src1 = t_yp if c == 0 else lvslots[c - 1]
                                if c == chain - 1:
                                    if k == N:
                                        dst = t_da
                                    else:
                                        dst = lvslots[c]
                                else:
                                    dst = bufs[c % 2]
                                eng.drain()
                                eng.tensor_tensor(out=dst[:], in0=src0[:], in1=src1[:],
                                                  op=OP.add).then_inc(sm["s_v"], 1)
                                cur = dst
                    tab.rec(sm["s_v"], V(("ysum", j)))
                    eng.drain()
                    eng.tensor_copy(out=t_p1[:], in_=t_da[:]).then_inc(sm["s_v"], 1)
                    tab.rec(sm["s_v"], V(("ysumb", j)))

            return dict(sync=em_sync, gp=em_gp, tensor=em_tensor,
                        scalar=em_scalar, vector=em_vector)

        p1es = ExitStack()
        p1 = token_phase({"phase": 1}, p1es)
        for _ in range(2):
            tab.cnt.clear()
            p1["vector"](dry); p1["scalar"](dry); p1["tensor"](dry)
            p1["sync"](dry); p1["gp"](dry)
        tab.dry = False

        @block.sync
        def _(eng):
            p1["sync"](eng)

        @block.gpsimd
        def _(eng):
            p1["gp"](eng)

        @block.tensor
        def _(eng):
            p1["tensor"](eng)

        @block.scalar
        def _(eng):
            p1["scalar"](eng)

        @block.vector
        def _(eng):
            p1["vector"](eng)

        p1es.close()

        p2es = ExitStack()
        tab.dry = True
        p2 = scan_phase(p2es)
        for _ in range(2):
            tab.cnt.clear()
            p2["vector"](dry); p2["scalar"](dry); p2["tensor"](dry)
            p2["sync"](dry); p2["gp"](dry)
        tab.dry = False

        @block.sync
        def _(eng):
            p2["sync"](eng)

        @block.gpsimd
        def _(eng):
            p2["gp"](eng)

        @block.tensor
        def _(eng):
            p2["tensor"](eng)

        @block.scalar
        def _(eng):
            p2["scalar"](eng)

        @block.vector
        def _(eng):
            p2["vector"](eng)

        p2es.close()

        p3es = ExitStack()
        tab.dry = True
        p3 = token_phase({"phase": 3}, p3es)
        for _ in range(2):
            tab.cnt.clear()
            p3["vector"](dry); p3["scalar"](dry); p3["tensor"](dry)
            p3["sync"](dry); p3["gp"](dry)
        tab.dry = False

        @block.sync
        def _(eng):
            p3["sync"](eng)

        @block.gpsimd
        def _(eng):
            p3["gp"](eng)

        @block.tensor
        def _(eng):
            p3["tensor"](eng)

        @block.scalar
        def _(eng):
            p3["scalar"](eng)

        @block.vector
        def _(eng):
            p3["vector"](eng)

        p3es.close()
    return nc


def kernel_fused(L0_attn_x, L0_attn_A_log, L0_attn_D_param, L0_attn_x_proj_w,
                 L0_attn_dt_proj_w, L0_attn_dt_proj_b, L0_ffn_gamma, L0_ffn_W):
    f32 = np.float32
    x = np.ascontiguousarray(np.asarray(L0_attn_x, dtype=f32))
    W = np.ascontiguousarray(np.asarray(L0_ffn_W, dtype=f32))
    gamma = np.asarray(L0_ffn_gamma, dtype=f32)
    A_log = np.ascontiguousarray(np.asarray(L0_attn_A_log, dtype=f32))
    D_par = np.asarray(L0_attn_D_param, dtype=f32)
    xpw = np.asarray(L0_attn_x_proj_w, dtype=f32)
    dtw = np.asarray(L0_attn_dt_proj_w, dtype=f32)
    dtb = np.asarray(L0_attn_dt_proj_b, dtype=f32)
    ident_b = np.eye(128, dtype=f32).astype(bf16)
    ident_f = np.eye(128, dtype=f32)
    selmat = np.repeat(np.eye(N, dtype=f32), 128, axis=1).astype(bf16)
    xpwT = np.ascontiguousarray(xpw.T)
    nc = build_fused()
    ims = []
    for c in range(8):
        b, h = c >> 1, c & 1
        rs = slice(h * DH, (h + 1) * DH)
        msk = np.zeros((128, 2), dtype=f32)
        msk[:, 0] = float(h)
        msk[:, 1] = float(1 - h)
        ims.append({
            "xin": np.ascontiguousarray(x[b, h * TOK:(h + 1) * TOK, :]),
            "Win": W, "gam": gamma[None, :], "ident": ident_b, "identf": ident_f,
            "xpwT": xpwT,
            "dtwT": np.ascontiguousarray(dtw[rs, :].T),
            "dtbias": np.ascontiguousarray(dtb[rs, None]),
            "Alog": np.ascontiguousarray(A_log[rs, :]),
            "sel": selmat, "dpar": D_par[None, :], "mask": msk,
        })
    res = run_bass_kernel_spmd(nc, ims, core_ids=list(range(8)))
    out = np.empty((B, L, D), dtype=f32)
    for c in range(8):
        b, h = c >> 1, c & 1
        out[b, h * TOK:(h + 1) * TOK, :] = res.results[c]["outo"]
    return out


kernel = kernel_fused
